# version 06
# baseline (speedup 1.0000x reference)
import numpy as np
import ml_dtypes
from contextlib import ExitStack

import concourse.bass as bass
import concourse.mybir as mybir
from concourse.bass_utils import run_bass_kernel_spmd

F32 = mybir.dt.float32
F32R = mybir.dt.float32r
BF16 = mybir.dt.bfloat16
FP8 = mybir.dt.float8e4
U8 = mybir.dt.uint8
U32 = mybir.dt.uint32
AF = mybir.ActivationFunctionType
ALU = mybir.AluOpType
AX = mybir.AxisListType

NCORES = 8
DM = 2048
TOKQ = 1024
EPS_LN = 1e-5
BF = ml_dtypes.bfloat16



def tiled_lhsT(w_t):
    K, M = w_t.shape
    nk, nm = K // 128, M // 128
    a = np.ascontiguousarray(w_t).reshape(nk, 128, nm, 128).transpose(2, 1, 0, 3)
    return np.ascontiguousarray(a.reshape(nm, 128, nk * 128))


def perm_for_core(p):
    own = np.concatenate([np.arange(128) + 128 * g for g in range(p, 16, 2)])
    par = np.concatenate([np.arange(128) + 128 * g for g in range(1 - p, 16, 2)])
    return np.concatenate([own, par])


def l0_head_order():
    ev = [h for h in range(32) if (h // 4) % 2 == 0]
    od = [h for h in range(32) if (h // 4) % 2 == 1]
    heads = []
    for t in range(16):
        heads.append(ev[t])
        heads.append(od[t])
    return heads


def l0_row_order():
    order = []
    for h in l0_head_order():
        order.extend(range(64 * h, 64 * h + 64))
    return np.array(order)


def rope_tables(positions):
    hd = 64
    inv = 1.0 / (10000.0 ** (np.arange(0, hd, 2, dtype=np.float64) / hd))
    fr = positions[None, :].astype(np.float64) * inv[:, None]
    cos = np.cos(fr).astype(np.float32)
    sin = np.sin(fr).astype(np.float32)
    c64 = np.concatenate([cos, cos], 0)
    s64 = np.concatenate([-sin, sin], 0)
    return (np.ascontiguousarray(np.concatenate([c64, c64], 0)),
            np.ascontiguousarray(np.concatenate([s64, s64], 0)))


def head_indicator_tiles(nrows, nheads):
    per = nrows // nheads
    e = np.zeros((nrows, nheads), dtype=np.float32)
    for h in range(nheads):
        e[h * per:(h + 1) * per, h] = 1.0
    nk = nrows // 128
    t = e.reshape(nk, 128, nheads).transpose(1, 0, 2).reshape(128, nk * nheads)
    return np.ascontiguousarray(t)


def u8_view(w_int32):
    v = w_int32.view(np.uint8).reshape(*w_int32.shape, 4)[..., 0]
    return np.ascontiguousarray(v)





_EMIT_METHODS = {
    "activation", "copy", "add", "mul", "square", "sqrt", "sign", "memzero",
    "memset", "tensor_copy", "tensor_tensor", "tensor_add", "tensor_sub",
    "tensor_mul", "tensor_max", "tensor_scalar", "tensor_single_scalar",
    "tensor_scalar_add", "tensor_scalar_sub", "tensor_scalar_mul",
    "tensor_scalar_max", "tensor_scalar_min", "tensor_relu", "tensor_reduce",
    "reduce_max", "reduce_sum", "scalar_tensor_tensor", "tensor_tensor_scan",
    "select", "transpose", "reciprocal", "iota", "affine_select",
}


class SerialEng:

    def __init__(self, state):
        object.__setattr__(self, "_st", state)
        object.__setattr__(self, "_eng", None)

    def bind(self, eng):
        object.__setattr__(self, "_eng", eng)
        return self

    @property
    def n(self):
        return self._st["n"]

    def __getattr__(self, name):
        st = object.__getattribute__(self, "_st")
        eng = object.__getattribute__(self, "_eng")
        attr = getattr(eng, name)
        if name not in _EMIT_METHODS:
            return attr

        def wrapped(*a, **kw):
            if st["n"] > 0:
                eng.wait_ge(st["sem"], st["n"])
            ins = attr(*a, **kw)
            ins.then_inc(st["sem"], 1)
            st["n"] += 1
            return ins
        return wrapped


class Marks:

    def __init__(self):
        self.d = {}
        self.pass2 = False

    def mark(self, ser_state, name):
        n = ser_state["n"]
        if self.pass2:
            assert self.d.get(name) == n, (name, self.d.get(name), n)
        else:
            assert name not in self.d, name
            self.d[name] = n

    def get(self, name):
        if self.pass2:
            return self.d[name]
        return self.d.get(name, 0)



def build_attn(layer):
    marks = Marks()
    _build_attn(layer, marks)
    marks.pass2 = True
    return _build_attn(layer, marks)


def _build_attn(layer, marks):
    if layer == 0:
        NH, NKV, HD = 32, 8, 64
        rope, bias_qkv, outproj = True, False, True
        scale = 0.125
    else:
        NH, NKV, HD = 16, 4, 128
        rope, bias_qkv, outproj = False, True, False
        scale = 1.0 / float(np.sqrt(128.0))
    DQ = NH * HD
    DKV = NKV * HD
    NC = 2048
    nseg = NC // 512
    VG = HD + 1
    NMQ, NMK = DQ // 128, DKV // 128

    nc = bass.Bass("TRN2", target_bir_lowering=False)
    nc.num_devices = 8

    xdr = nc.dram_tensor("x", [DM, NC], F32, kind="ExternalInput").ap()
    wqd = nc.dram_tensor("wq", [NMQ, 128, 16 * 128], F32, kind="ExternalInput").ap()
    wkd = nc.dram_tensor("wk", [NMK, 128, 16 * 128], F32, kind="ExternalInput").ap()
    wvd = nc.dram_tensor("wv", [16, 128, DKV], F32, kind="ExternalInput").ap()
    wod = bqd = bvgd = tabqd = tabkd = None
    if outproj:
        wod = nc.dram_tensor("wo", [16, 128, 16 * 128], F32, kind="ExternalInput").ap()
    if bias_qkv:
        bqd = nc.dram_tensor("bqk", [128, NMQ + NMK], F32, kind="ExternalInput").ap()
        bvgd = nc.dram_tensor("bvg", [128, NKV * VG], F32, kind="ExternalInput").ap()
    if rope:
        tabqd = nc.dram_tensor("tabq", [128, 2 * TOKQ], F32, kind="ExternalInput").ap()
        tabkd = nc.dram_tensor("tabk", [128, 2 * NC], F32, kind="ExternalInput").ap()
    eqd = nc.dram_tensor("eq", [128, 16 * NH], F32, kind="ExternalInput").ap()
    ekd = nc.dram_tensor("ek", [128, NMK * NKV], F32, kind="ExternalInput").ap()
    mpd = nc.dram_tensor("maskp", [128, 128], F32, kind="ExternalInput").ap()
    mdd = nc.dram_tensor("maskd", [128, 128], F32, kind="ExternalInput").ap()
    idd = nc.dram_tensor("ident", [128, 128], F32, kind="ExternalInput").ap()
    ydr = nc.dram_tensor("y", [DM, TOKQ], F32, kind="ExternalOutput").ap()

    with ExitStack() as top:
        E = top.enter_context
        blk0 = E(nc.Block())

        qsb = [E(nc.sbuf_tensor(f"q{i}", [128, TOKQ], BF16)) for i in range(NMQ)]
        ksb = [E(nc.sbuf_tensor(f"k{i}", [128, NC], BF16)) for i in range(NMK)]
        vsb = [E(nc.sbuf_tensor(f"v{i}", [128, NKV * VG], BF16)) for i in range(16)]
        mk_p = E(nc.sbuf_tensor("mk_p", [128, 128], BF16))
        mk_d = E(nc.sbuf_tensor("mk_d", [128, 128], BF16))
        id_s = E(nc.sbuf_tensor("id_s", [128, 128], F32))
        ones_c = E(nc.sbuf_tensor("ones_c", [128, 1], BF16))
        ones_r = E(nc.sbuf_tensor("ones_r", [1, 128], BF16))
        zrow = E(nc.sbuf_tensor("zrow", [1, 512], BF16))
        bias_sb = E(nc.sbuf_tensor("bias_sb", [128, 1], F32))
        eq_s = E(nc.sbuf_tensor("eq_s", [128, 16 * NH], BF16))
        ek_s = E(nc.sbuf_tensor("ek_s", [128, NMK * NKV], BF16))
        if bias_qkv:
            bq_s = E(nc.sbuf_tensor("bq_s", [128, NMQ + NMK], F32))
            bvg_s = E(nc.sbuf_tensor("bvg_s", [128, NKV * VG], BF16))

        x_ld = E(nc.semaphore("x_ld"))
        c_ld = E(nc.semaphore("c_ld"))
        w_lda = E(nc.semaphore("w_lda"))
        w_ldb = E(nc.semaphore("w_ldb"))
        s_ld = E(nc.semaphore("s_ld"))
        sx_a = E(nc.semaphore("sx_a"))
        sx_b = E(nc.semaphore("sx_b"))
        st_a = E(nc.semaphore("st_a"))
        st_b = E(nc.semaphore("st_b"))
        t_ln = E(nc.semaphore("t_ln"))
        t_qk = E(nc.semaphore("t_qk"))
        t_nb = E(nc.semaphore("t_nb"))
        t_s = E(nc.semaphore("t_s"))
        t_av = E(nc.semaphore("t_av"))
        t_bc = E(nc.semaphore("t_bc"))
        t_p5 = E(nc.semaphore("t_p5"))
        v_self = E(nc.semaphore("v_self"))
        a_self = E(nc.semaphore("a_self"))
        g_self = E(nc.semaphore("g_self"))
        SV = {"sem": v_self, "n": 0}
        SA = {"sem": a_self, "n": 0}
        SG = {"sem": g_self, "n": 0}
        _vw, _aw, _gw = SerialEng(SV), SerialEng(SA), SerialEng(SG)

        class blk:
            tensor = staticmethod(blk0.tensor)
            sync = staticmethod(blk0.sync)
            vector = staticmethod(lambda f: blk0.vector(lambda e: f(_vw.bind(e))))
            scalar = staticmethod(lambda f: blk0.scalar(lambda e: f(_aw.bind(e))))
            gpsimd = staticmethod(lambda f: blk0.gpsimd(lambda e: f(_gw.bind(e))))

        def MKV(name):
            marks.mark(SV, name)

        def MKA(name):
            marks.mark(SA, name)

        def WV(eng, name):
            val = marks.get(name)
            if val > 0:
                eng.wait_ge(v_self, val)

        def WA(eng, name):
            val = marks.get(name)
            if val > 0:
                eng.wait_ge(a_self, val)

        with ExitStack() as ph:
            e = ph.enter_context
            xh = [e(nc.sbuf_tensor(f"xh{i}", [128, NC], BF16)) for i in range(16)]
            stat = e(nc.sbuf_tensor("stat", [1, 3 * 512 + 64], F32))
            ab_bf = e(nc.sbuf_tensor("ab_bf", [1, 2 * NC], BF16))
            abc = e(nc.sbuf_tensor("abc", [128, NC], BF16))
            wbuf = [e(nc.sbuf_tensor(f"wbuf{i}", [128, DM], BF16)) for i in range(2)]
            wv_c = [e(nc.sbuf_tensor(f"wvc{i}", [128, DKV], BF16)) for i in range(16)]
            if rope:
                tq_s = e(nc.sbuf_tensor("tq_s", [128, 2 * TOKQ], BF16))
                tk_s = e(nc.sbuf_tensor("tk_s", [128, 2 * NC], BF16))
            rp = [e(nc.sbuf_tensor(f"rp{i}", [128, NC], BF16)) for i in range(4)]
            x2b = rp

            NW = NMQ + NMK
            def tqk_q(mt): return mt + 1
            def tqk_k(j, half): return NMQ + 2 * j + half + 1
            def tqk_v(tt): return NMQ + 2 * NMK + tt + 1
            def wdone(i):
                return tqk_q(i) if i < NMQ else tqk_k(i - NMQ, 1)

            def _(g):
                for kc in range(16):
                    g.dma_start(out=xh[kc][:],
                                in_=xdr[128 * kc:128 * (kc + 1), :]).then_inc(x_ld, 16)
                g.dma_start(out=eq_s[:], in_=eqd).then_inc(c_ld, 16)
                g.dma_start(out=ek_s[:], in_=ekd).then_inc(c_ld, 16)
                g.dma_start(out=mk_p[:], in_=mpd).then_inc(c_ld, 16)
                g.dma_start(out=mk_d[:], in_=mdd).then_inc(c_ld, 16)
                if rope:
                    g.dma_start(out=tq_s[:], in_=tabqd).then_inc(c_ld, 16)
                    g.dma_start(out=tk_s[:], in_=tabkd).then_inc(c_ld, 16)
                if bias_qkv:
                    g.dma_start(out=bvg_s[:], in_=bvgd).then_inc(c_ld, 16)
                for kc in range(16):
                    g.dma_start(out=wv_c[kc][:], in_=wvd[kc]).then_inc(c_ld, 16)
                for i in range(NW):
                    if i >= 2:
                        g.wait_ge(t_qk, wdone(i - 2))
                    src = wqd[i] if i < NMQ else wkd[i - NMQ]
                    g.dma_start(out=wbuf[i % 2][:],
                                in_=src).then_inc(w_lda if i % 2 == 0 else w_ldb, 16)
            blk.gpsimd(_)
            XTOT = 16 * 16
            CTOT = 16 * (20 + (2 if rope else 0) + (1 if bias_qkv else 0))

            def _(s):
                s.dma_start(out=id_s[:], in_=idd).then_inc(s_ld, 16)
                if bias_qkv:
                    s.dma_start(out=bq_s[:], in_=bqd).then_inc(s_ld, 16)
            blk.sync(_)
            S_CONST = 16 * (2 if bias_qkv else 1)

            def _(v):
                v.memset(ones_c[:], 1.0)
                for tt in range(16):
                    v.memset(vsb[tt][:], 1.0)
                v.memset(ones_r[:], 1.0)
                v.memset(zrow[:], 0.0)
                MKV("ones")
            blk.vector(_)

            def _(sc):
                sc.wait_ge(x_ld, XTOT)
                for kc in range(16):
                    if kc >= 4:
                        sc.wait_ge(t_ln, kc - 3)
                    sc.activation(out=x2b[kc % 4][:], in_=xh[kc][:],
                                  func=AF.Square)
                    MKA(f"sq{kc}")
            blk.scalar(_)

            with ExitStack() as pp:
                ee = pp.enter_context
                ps_sum = [ee(nc.psum_tensor(f"lnps{i}", [1, 512], F32))
                          for i in range(nseg)]
                ps_sq = [ee(nc.psum_tensor(f"lnpq{i}", [1, 512], F32))
                         for i in range(nseg)]

                def _(t):
                    WV(t, "ones")
                    for kc in range(16):
                        WA(t, f"sq{kc}")
                        for sg in range(nseg):
                            t.matmul(out=ps_sum[sg][0:1, :], lhsT=ones_c[:, 0:1],
                                     rhs=xh[kc][:, 512 * sg:512 * (sg + 1)],
                                     start=(kc == 0), stop=(kc == 15))
                        ins = None
                        for sg in range(nseg):
                            ins = t.matmul(out=ps_sq[sg][0:1, :], lhsT=ones_c[:, 0:1],
                                           rhs=x2b[kc % 4][:, 512 * sg:512 * (sg + 1)],
                                           start=(kc == 0), stop=(kc == 15))
                        ins.then_inc(t_ln, 1)
                blk.tensor(_)

                eps_r = stat[:, 1536:1537]
                mu_r = stat[:, 0:512]
                var_r = stat[:, 512:1024]
                sc_r = stat[:, 1024:1536]

                def _(v):
                    v.memset(eps_r, EPS_LN)
                    v.wait_ge(t_ln, 16)
                    for sg in range(nseg):
                        v.tensor_scalar_mul(out=mu_r, in0=ps_sum[sg][0:1, :],
                                            scalar1=1.0 / DM)
                        v.tensor_scalar_mul(out=var_r, in0=ps_sq[sg][0:1, :],
                                            scalar1=1.0 / DM)
                        v.tensor_tensor(out=sc_r, in0=mu_r, in1=mu_r, op=ALU.mult)
                        v.tensor_tensor(out=var_r, in0=var_r, in1=sc_r,
                                        op=ALU.subtract)
                        MKV(f"prep{sg}")
                        WA(v, f"rstd{sg}")
                        v.tensor_tensor(out=sc_r, in0=mu_r, in1=var_r, op=ALU.mult)
                        v.tensor_scalar_mul(out=sc_r, in0=sc_r, scalar1=-1.0)
                        v.tensor_copy(out=ab_bf[:, 512 * sg:512 * (sg + 1)],
                                      in_=var_r)
                        v.tensor_copy(out=ab_bf[:, NC + 512 * sg:NC + 512 * (sg + 1)],
                                      in_=sc_r)
                        MKV(f"ab{sg}")
                blk.vector(_)

                def _(sc):
                    for sg in range(nseg):
                        WV(sc, f"prep{sg}")
                        sc.activation(out=var_r, in_=var_r, func=AF.Ln, bias=eps_r)
                        sc.activation(out=var_r, in_=var_r, func=AF.Exp,
                                      scale=-0.5)
                        MKA(f"rstd{sg}")
                blk.scalar(_)

            with ExitStack() as pp:
                ee = pp.enter_context
                ps_bc = [ee(nc.psum_tensor(f"lnbc{i}", [128, 512], F32))
                         for i in range(2)]

                def _(t):
                    for half in range(2):
                        if half == 1:
                            WV(t, "xh_mul_done")
                        for sg in range(nseg):
                            idx = half * nseg + sg
                            WV(t, f"ab{sg}" if half == 0 else f"ab{nseg - 1}")
                            if idx >= 2:
                                WA(t, f"bce{idx - 2}")
                            t.matmul(out=ps_bc[sg % 2][:, :], lhsT=ones_r[0:1, :],
                                     rhs=ab_bf[:, 512 * idx:512 * (idx + 1)],
                                     start=True, stop=True).then_inc(t_ln, 1)
                blk.tensor(_)

                def _(sc):
                    for half in range(2):
                        for sg in range(nseg):
                            idx = half * nseg + sg
                            sc.wait_ge(t_ln, 16 + idx + 1)
                            if half == 1 and sg == 0:
                                WV(sc, "xh_mul_done")
                            sc.copy(out=abc[:, 512 * sg:512 * (sg + 1)],
                                    in_=ps_bc[sg % 2][:, :])
                            MKA(f"bce{idx}")
                blk.scalar(_)

                def _(v):
                    for half, op in ((0, ALU.mult), (1, ALU.add)):
                        WA(v, f"bce{half * nseg + nseg - 1}")
                        for kc in range(16):
                            v.tensor_tensor(out=xh[kc][:], in0=xh[kc][:],
                                            in1=abc[:, 0:NC], op=op)
                        if half == 0:
                            MKV("xh_mul_done")
                        else:
                            MKV("xh_done")
                blk.vector(_)

            with ExitStack() as pp:
                ee = pp.enter_context
                ps_p = [ee(nc.psum_tensor(f"pp{i}", [128, TOKQ], F32))
                        for i in range(2)]
                ps_v = [ee(nc.psum_tensor(f"pv{i}", [128, DKV], F32))
                        for i in range(2)]

                def _(t):
                    WV(t, "xh_done")
                    t.wait_ge(c_ld, CTOT)
                    u = 0
                    for mt in range(NMQ):
                        t.wait_ge(w_lda if mt % 2 == 0 else w_ldb,
                                  16 * (mt // 2 + 1))
                        if u >= 2:
                            WA(t, f"ev{u - 2}")
                        ins = None
                        for kc in range(16):
                            for sg in range(2):
                                ins = t.matmul(
                                    out=ps_p[u % 2][:, 512 * sg:512 * (sg + 1)],
                                    lhsT=wbuf[mt % 2][:, 128 * kc:128 * (kc + 1)],
                                    rhs=xh[kc][:, 512 * sg:512 * (sg + 1)],
                                    start=(kc == 0), stop=(kc == 15))
                        ins.then_inc(t_qk, 1)
                        u += 1
                    for j in range(NMK):
                        i = NMQ + j
                        t.wait_ge(w_lda if i % 2 == 0 else w_ldb,
                                  16 * (i // 2 + 1))
                        for half in range(2):
                            if u >= 2:
                                WA(t, f"ev{u - 2}")
                            ins = None
                            for kc in range(16):
                                for sg in range(2):
                                    c0 = 1024 * half + 512 * sg
                                    ins = t.matmul(
                                        out=ps_p[u % 2][:, 512 * sg:512 * (sg + 1)],
                                        lhsT=wbuf[i % 2][:, 128 * kc:128 * (kc + 1)],
                                        rhs=xh[kc][:, c0:c0 + 512],
                                        start=(kc == 0), stop=(kc == 15))
                            ins.then_inc(t_qk, 1)
                            u += 1
                    for tt in range(16):
                        if tt >= 2:
                            WA(t, f"evv{tt - 2}")
                        ins = None
                        for kc in range(16):
                            ins = t.matmul(out=ps_v[tt % 2][:, :],
                                           lhsT=xh[kc][:, 128 * tt:128 * (tt + 1)],
                                           rhs=wv_c[kc][:, :],
                                           start=(kc == 0), stop=(kc == 15))
                        ins.then_inc(t_qk, 1)
                blk.tensor(_)

                def _(sc):
                    if bias_qkv:
                        sc.wait_ge(s_ld, S_CONST)
                    u = 0
                    for mt in range(NMQ):
                        sc.wait_ge(t_qk, tqk_q(mt))
                        if rope:
                            WV(sc, f"rope{u - 2}") if u >= 2 else None
                            r0, r1 = rp[2 * (u % 2)], rp[2 * (u % 2) + 1]
                            p = ps_p[u % 2]
                            sc.copy(out=r0[:, 0:TOKQ], in_=p[:, :])
                            sc.copy(out=r1[0:32, 0:TOKQ], in_=p[32:64, :])
                            sc.copy(out=r1[32:64, 0:TOKQ], in_=p[0:32, :])
                            sc.copy(out=r1[64:96, 0:TOKQ], in_=p[96:128, :])
                            sc.copy(out=r1[96:128, 0:TOKQ], in_=p[64:96, :])
                        else:
                            sc.activation(out=qsb[mt][:], in_=ps_p[u % 2][:, :],
                                          func=AF.Identity,
                                          bias=bq_s[:, mt:mt + 1])
                        MKA(f"ev{u}")
                        u += 1
                    for j in range(NMK):
                        for half in range(2):
                            sc.wait_ge(t_qk, tqk_k(j, half))
                            cs = 1024 * half
                            if rope:
                                if half == 0:
                                    WV(sc, f"rope{NMQ + j - 2}")
                                p = ps_p[u % 2]
                                r0 = rp[2 * (j % 2)]
                                r1 = rp[2 * (j % 2) + 1]
                                sc.copy(out=r0[:, cs:cs + 1024], in_=p[:, :])
                                sc.copy(out=r1[0:32, cs:cs + 1024], in_=p[32:64, :])
                                sc.copy(out=r1[32:64, cs:cs + 1024], in_=p[0:32, :])
                                sc.copy(out=r1[64:96, cs:cs + 1024], in_=p[96:128, :])
                                sc.copy(out=r1[96:128, cs:cs + 1024],
                                        in_=p[64:96, :])
                            else:
                                sc.activation(out=ksb[j][:, cs:cs + 1024],
                                              in_=ps_p[u % 2][:, :],
                                              func=AF.Identity,
                                              bias=bq_s[:, NMQ + j:NMQ + j + 1])
                            MKA(f"ev{u}")
                            u += 1
                    WV(sc, "ones")
                    for tt in range(16):
                        sc.wait_ge(t_qk, tqk_v(tt))
                        if bias_qkv:
                            WV(sc, f"vbias{tt - 2}") if tt >= 2 else None
                        src = ps_v[tt % 2][:, :].rearrange("p (g d) -> p g d", g=NKV)
                        if HD == 64:
                            dst = vsb[tt][:, :].rearrange("p (g w) -> p g w",
                                                          g=NKV)[:, :, 0:64]
                            sc.copy(out=dst, in_=src)
                        else:
                            src2 = ps_v[tt % 2][:, :].rearrange(
                                "p (g s d) -> p g s d", g=NKV, s=2)
                            d_lo = vsb[tt][:, :].rearrange("p (g w) -> p g w",
                                                           g=NKV)[:, :, 0:64]
                            d_hi = vsb[tt][:, :].rearrange("p (g w) -> p g w",
                                                           g=NKV)[:, :, 65:129]
                            sc.copy(out=d_lo, in_=src2[:, :, 0, :])
                            sc.copy(out=d_hi, in_=src2[:, :, 1, :])
                        MKA(f"evv{tt}")
                blk.scalar(_)

                def _(v):
                    if rope:
                        v.wait_ge(c_ld, CTOT)
                        for w in range(NMQ + NMK):
                            if w < NMQ:
                                WA(v, f"ev{w}")
                                mt = w
                                r0, r1 = rp[2 * (w % 2)], rp[2 * (w % 2) + 1]
                                v.tensor_tensor(out=qsb[mt][:], in0=r0[:, 0:TOKQ],
                                                in1=tq_s[:, 0:TOKQ], op=ALU.mult)
                                v.tensor_tensor(out=r1[:, 0:TOKQ],
                                                in0=r1[:, 0:TOKQ],
                                                in1=tq_s[:, TOKQ:2 * TOKQ],
                                                op=ALU.mult)
                                v.tensor_tensor(out=qsb[mt][:], in0=qsb[mt][:],
                                                in1=r1[:, 0:TOKQ], op=ALU.add)
                            else:
                                j = w - NMQ
                                WA(v, f"ev{NMQ + 2 * j + 1}")
                                r0 = rp[2 * (j % 2)]
                                r1 = rp[2 * (j % 2) + 1]
                                v.tensor_tensor(out=ksb[j][:], in0=r0[:, :],
                                                in1=tk_s[:, 0:NC], op=ALU.mult)
                                v.tensor_tensor(out=r1[:, :], in0=r1[:, :],
                                                in1=tk_s[:, NC:2 * NC],
                                                op=ALU.mult)
                                v.tensor_tensor(out=ksb[j][:], in0=ksb[j][:],
                                                in1=r1[:, :], op=ALU.add)
                            MKV(f"rope{w}")
                    if bias_qkv:
                        v.wait_ge(c_ld, CTOT)
                        for tt in range(16):
                            WA(v, f"evv{tt}")
                            v.tensor_tensor(out=vsb[tt][:], in0=vsb[tt][:],
                                            in1=bvg_s[:], op=ALU.add)
                            MKV(f"vbias{tt}")
                blk.vector(_)

            with ExitStack() as pp:
                ee = pp.enter_context
                ps_n = ee(nc.psum_tensor("ps_n", [NH, TOKQ], F32))
                ps_nk = ee(nc.psum_tensor("ps_nk", [NKV, NC], F32))
                ps_tr = ee(nc.psum_tensor("ps_tr", [1, 128], F32))
                ps_bb = ee(nc.psum_tensor("ps_bb", [128, 1], F32))
                nb = ee(nc.sbuf_tensor("nb", [NH, 2], F32))
                nbk = ee(nc.sbuf_tensor("nbk", [NKV, 2], F32))
                nbs = ee(nc.sbuf_tensor("nbs", [1, 8], F32))
                nb_bf = ee(nc.sbuf_tensor("nb_bf", [1, 1], BF16))

                def _(sc):
                    if rope:
                        WV(sc, f"rope{NMQ + NMK - 1}")
                    for mt in range(NMQ):
                        if mt >= 2:
                            sc.wait_ge(t_nb, mt - 1)
                        sc.activation(out=rp[mt % 2][:, 0:TOKQ], in_=qsb[mt][:],
                                      func=AF.Square)
                        MKA(f"nsq{mt}")
                    for j in range(NMK):
                        if j >= 2:
                            sc.wait_ge(t_nb, NMQ + j - 1)
                        sc.activation(out=rp[2 + j % 2][:, :], in_=ksb[j][:],
                                      func=AF.Square)
                        MKA(f"nsk{j}")
                blk.scalar(_)

                def _(t):
                    for mt in range(NMQ):
                        WA(t, f"nsq{mt}")
                        ins = None
                        for sg in range(TOKQ // 512):
                            ins = t.matmul(out=ps_n[:, 512 * sg:512 * (sg + 1)],
                                           lhsT=eq_s[:, NH * mt:NH * (mt + 1)],
                                           rhs=rp[mt % 2][:, 512 * sg:512 * (sg + 1)],
                                           start=(mt == 0), stop=(mt == NMQ - 1))
                        ins.then_inc(t_nb, 1)
                    for j in range(NMK):
                        WA(t, f"nsk{j}")
                        ins = None
                        for sg in range(nseg):
                            ins = t.matmul(out=ps_nk[:, 512 * sg:512 * (sg + 1)],
                                           lhsT=ek_s[:, NKV * j:NKV * (j + 1)],
                                           rhs=rp[2 + j % 2][:, 512 * sg:512 * (sg + 1)],
                                           start=(j == 0), stop=(j == NMK - 1))
                        ins.then_inc(t_nb, 1)
                blk.tensor(_)
                T_NBQK = NMQ + NMK

                def _(v):
                    v.wait_ge(t_nb, T_NBQK)
                    v.tensor_reduce(out=nb[:, 0:1], in_=ps_n[:, :], axis=AX.X,
                                    op=ALU.max)
                    v.tensor_reduce(out=nbk[:, 0:1], in_=ps_nk[:, :], axis=AX.X,
                                    op=ALU.max)
                    MKV("nred1")
                blk.vector(_)

                def _(t):
                    WV(t, "nred1")
                    t.wait_ge(s_ld, S_CONST)
                    t.transpose(out=ps_tr[0:1, 0:NH], in_=nb[:, 0:1],
                                identity=id_s[0:NH, 0:NH])
                    t.transpose(out=ps_tr[0:1, 64:64 + NKV], in_=nbk[:, 0:1],
                                identity=id_s[0:NKV, 0:NKV]).then_inc(t_nb, 1)
                blk.tensor(_)

                def _(v):
                    v.wait_ge(t_nb, T_NBQK + 1)
                    v.tensor_reduce(out=nbs[:, 0:1], in_=ps_tr[0:1, 0:NH],
                                    axis=AX.X, op=ALU.max)
                    v.tensor_reduce(out=nbs[:, 1:2], in_=ps_tr[0:1, 64:64 + NKV],
                                    axis=AX.X, op=ALU.max)
                    v.tensor_tensor(out=nbs[:, 2:3], in0=nbs[:, 0:1],
                                    in1=nbs[:, 1:2], op=ALU.mult)
                    MKV("nred2")
                blk.vector(_)

                def _(sc):
                    WV(sc, "nred2")
                    sc.activation(out=nbs[:, 3:4], in_=nbs[:, 2:3], func=AF.Ln)
                    sc.activation(out=nbs[:, 4:5], in_=nbs[:, 3:4], func=AF.Exp,
                                  scale=0.5)
                    MKA("nsqrt")
                blk.scalar(_)

                def _(v):
                    WA(v, "nsqrt")
                    v.tensor_scalar_mul(out=nbs[:, 5:6], in0=nbs[:, 4:5],
                                        scalar1=-scale)
                    v.tensor_copy(out=nb_bf[:, :], in_=nbs[:, 5:6])
                    MKV("nbf")
                blk.vector(_)

                def _(t):
                    WV(t, "nbf")
                    t.matmul(out=ps_bb[:, 0:1], lhsT=ones_r[0:1, :],
                             rhs=nb_bf[0:1, 0:1], start=True,
                             stop=True).then_inc(t_nb, 1)
                blk.tensor(_)

                def _(sc):
                    sc.wait_ge(t_nb, T_NBQK + 2)
                    sc.copy(out=bias_sb[:, :], in_=ps_bb[:, :])
                    MKA("bias_done")
                blk.scalar(_)

        stor_heads = l0_head_order() if layer == 0 else list(range(NH))
        with ExitStack() as ph:
            e = ph.enter_context
            of = [e(nc.sbuf_tensor(f"of{i}", [128, TOKQ], BF16)) for i in range(16)]
            pt = [e(nc.sbuf_tensor(f"pt{i}", [128, TOKQ], BF16)) for i in range(2)]
            rc_f = e(nc.sbuf_tensor("rc_f", [1, TOKQ], F32))
            rc_b = e(nc.sbuf_tensor("rc_b", [1, TOKQ], BF16))
            bb_s = e(nc.sbuf_tensor("bb_s", [HD, TOKQ], BF16))

            def segs_for(i):
                a = 128 * i
                if a < 512:
                    return [(a, 512), (512, 1024)]
                return [(a, 1024)]

            with ExitStack() as pp:
                ee = pp.enter_context
                ns_s = 2 if layer == 0 else 1
                ps_s = [ee(nc.psum_tensor(f"ps_s{i}", [128, TOKQ], F32))
                        for i in range(ns_s)]
                if layer == 0:
                    ps_o = ee(nc.psum_tensor("ps_o", [HD + 1, TOKQ], F32))
                else:
                    ps_o1 = ee(nc.psum_tensor("ps_o1", [65, TOKQ], F32))
                    ps_o2 = ee(nc.psum_tensor("ps_o2", [64, TOKQ], F32))
                ps_b = ee(nc.psum_tensor("ps_b", [HD, TOKQ], F32))

                def _(t):
                    WA(t, "bias_done")
                    for hs in range(NH):
                        h_orig = stor_heads[hs]
                        g = h_orig // (NH // NKV)
                        if HD == 64:
                            qt, qr = hs // 2, 64 * (hs % 2)
                        else:
                            qt, qr = hs, 0
                        kt, kr = (g * HD) // 128, (g * HD) % 128

                        def emit_S(c):
                            i = c - 8 if c >= 8 else c
                            idx = 16 * hs + c
                            if idx >= ns_s:
                                WA(t, f"exp{idx - ns_s}")
                            ins = None
                            for (a, b) in segs_for(i):
                                ins = t.matmul(
                                    out=ps_s[idx % ns_s][:, a:b],
                                    lhsT=ksb[kt][kr:kr + HD, 128 * c:128 * (c + 1)],
                                    rhs=qsb[qt][qr:qr + HD, a:b],
                                    start=True, stop=True)
                            ins.then_inc(t_s, 1)

                        def emit_AV(c):
                            i = c - 8 if c >= 8 else c
                            idx = 16 * hs + c
                            WV(t, f"msk{idx}")
                            if c == 0 and hs >= 1:
                                WV(t, f"epi{hs - 1}")
                            ins = None
                            for (a, b) in segs_for(i):
                                if layer == 0:
                                    ins = t.matmul(
                                        out=ps_o[:, a:b],
                                        lhsT=vsb[c][:, VG * g:VG * g + 65],
                                        rhs=pt[idx % 2][:, a:b],
                                        start=(c == 0), stop=False)
                                else:
                                    t.matmul(
                                        out=ps_o1[:, a:b],
                                        lhsT=vsb[c][:, VG * g:VG * g + 65],
                                        rhs=pt[idx % 2][:, a:b],
                                        start=(c == 0), stop=False)
                                    ins = t.matmul(
                                        out=ps_o2[:, a:b],
                                        lhsT=vsb[c][:, VG * g + 65:VG * g + 129],
                                        rhs=pt[idx % 2][:, a:b],
                                        start=(c == 0), stop=False)
                            if c != 15:
                                ins.then_inc(t_av, 1)

                        def emit_close():
                            ins = None
                            for sg in range(2):
                                cols = slice(512 * sg, 512 * (sg + 1))
                                if layer == 0:
                                    ins = t.matmul(out=ps_o[:, cols],
                                                   lhsT=zrow[0:1, 0:65],
                                                   rhs=zrow[0:1, 0:512],
                                                   start=False, stop=True)
                                else:
                                    t.matmul(out=ps_o1[:, cols],
                                             lhsT=zrow[0:1, 0:65],
                                             rhs=zrow[0:1, 0:512],
                                             start=False, stop=True)
                                    ins = t.matmul(out=ps_o2[:, cols],
                                                   lhsT=zrow[0:1, 0:64],
                                                   rhs=zrow[0:1, 0:512],
                                                   start=False, stop=True)
                            ins.then_inc(t_av, 1)

                        if ns_s == 2:
                            emit_S(0)
                            for c in range(1, 16):
                                emit_S(c)
                                emit_AV(c - 1)
                            emit_AV(15)
                        else:
                            for c in range(16):
                                emit_S(c)
                                emit_AV(c)
                        emit_close()
                        WV(t, f"rc{hs}")
                        if hs >= 1:
                            WA(t, f"bbe{hs - 1}")
                        ins = None
                        for sg in range(2):
                            ins = t.matmul(out=ps_b[:, 512 * sg:512 * (sg + 1)],
                                           lhsT=ones_r[0:1, 0:HD],
                                           rhs=rc_b[0:1, 512 * sg:512 * (sg + 1)],
                                           start=True, stop=True)
                        ins.then_inc(t_bc, 1)
                blk.tensor(_)

                def _(sc):
                    if rope:
                        pass
                    for hs in range(NH):
                        for c in range(16):
                            i = c - 8 if c >= 8 else c
                            idx = 16 * hs + c
                            sc.wait_ge(t_s, idx + 1)
                            if idx >= 2:
                                sc.wait_ge(t_av, idx - 1)
                            sc.activation(out=pt[idx % 2][:, 128 * i:1024],
                                          in_=ps_s[idx % ns_s][:, 128 * i:1024],
                                          func=AF.Exp, bias=bias_sb[:, 0:1],
                                          scale=scale)
                            MKA(f"exp{idx}")
                        sc.wait_ge(t_bc, hs + 1)
                        sc.copy(out=bb_s[:, :], in_=ps_b[:, :])
                        MKA(f"bbe{hs}")
                blk.scalar(_)

                def _(v):
                    v.wait_ge(c_ld, CTOT)
                    for hs in range(NH):
                        for c in range(16):
                            i = c - 8 if c >= 8 else c
                            idx = 16 * hs + c
                            WA(v, f"exp{idx}")
                            mk = mk_d if c < 8 else mk_p
                            v.tensor_tensor(
                                out=pt[idx % 2][:, 128 * i:128 * (i + 1)],
                                in0=pt[idx % 2][:, 128 * i:128 * (i + 1)],
                                in1=mk[:, :], op=ALU.mult)
                            MKV(f"msk{idx}")
                        v.wait_ge(t_av, 16 * (hs + 1))
                        if layer == 0:
                            v.reciprocal(out=rc_f[:, :], in_=ps_o[64:65, :])
                        else:
                            v.reciprocal(out=rc_f[:, :], in_=ps_o1[64:65, :])
                        v.tensor_copy(out=rc_b[:, :], in_=rc_f[:, :])
                        MKV(f"rc{hs}")
                        WA(v, f"bbe{hs}")
                        if layer == 0:
                            qt, qr = hs // 2, 64 * (hs % 2)
                            v.tensor_tensor(out=of[qt][qr:qr + HD, :],
                                            in0=ps_o[0:64, :], in1=bb_s[:, :],
                                            op=ALU.mult)
                        else:
                            v.tensor_tensor(out=of[hs][0:64, :],
                                            in0=ps_o1[0:64, :], in1=bb_s[0:64, :],
                                            op=ALU.mult)
                            v.tensor_tensor(out=of[hs][64:128, :],
                                            in0=ps_o2[0:64, :],
                                            in1=bb_s[64:128, :], op=ALU.mult)
                        MKV(f"epi{hs}")
                blk.vector(_)

            xres = [e(nc.sbuf_tensor(f"xres{i}", [128, TOKQ], F32)) for i in range(2)]
            yb = [e(nc.sbuf_tensor(f"yb{i}", [128, TOKQ], F32)) for i in range(2)]

            def _(s):
                WA(s, "bias_done")
                for mt in range(16):
                    if mt >= 2:
                        WV(s, f"y{mt - 2}")
                    s.dma_start(out=xres[mt % 2][:],
                                in_=xdr[128 * mt:128 * (mt + 1), 0:TOKQ]).then_inc(
                                    sx_a if mt % 2 == 0 else sx_b, 16)
                    if mt >= 1:
                        WV(s, f"y{mt - 1}")
                        s.dma_start(out=ydr[128 * (mt - 1):128 * mt, :],
                                    in_=yb[(mt - 1) % 2][:]).then_inc(
                                        st_a if (mt - 1) % 2 == 0 else st_b, 16)
                WV(s, "y15")
                s.dma_start(out=ydr[128 * 15:128 * 16, :],
                            in_=yb[15 % 2][:]).then_inc(st_b, 16)
                s.wait_ge(st_a, 16 * 8)
                s.wait_ge(st_b, 16 * 8)
            blk.sync(_)

            if outproj:
                with ExitStack() as pp:
                    ee = pp.enter_context
                    wo_b = [ee(nc.sbuf_tensor(f"wo_b{i}", [128, DM], BF16))
                            for i in range(2)]
                    ps_5 = [ee(nc.psum_tensor(f"ps5_{i}", [128, TOKQ], F32))
                            for i in range(2)]

                    def _(g):
                        WA(g, "bias_done")
                        for mt in range(16):
                            if mt >= 2:
                                g.wait_ge(t_p5, mt - 1)
                            g.dma_start(out=wo_b[mt % 2][:],
                                        in_=wod[mt]).then_inc(
                                            w_lda if mt % 2 == 0 else w_ldb, 16)
                    blk.gpsimd(_)
                    W5A = 16 * ((NW + 1) // 2)
                    W5B = 16 * (NW // 2)

                    def _(t):
                        WV(t, f"epi{NH - 1}")
                        for mt in range(16):
                            t.wait_ge(w_lda if mt % 2 == 0 else w_ldb,
                                      (W5A if mt % 2 == 0 else W5B)
                                      + 16 * (mt // 2 + 1))
                            if mt >= 2:
                                WV(t, f"y{mt - 2}")
                            ins = None
                            for kc in range(16):
                                for sg in range(2):
                                    ins = t.matmul(
                                        out=ps_5[mt % 2][:, 512 * sg:512 * (sg + 1)],
                                        lhsT=wo_b[mt % 2][:, 128 * kc:128 * (kc + 1)],
                                        rhs=of[kc][:, 512 * sg:512 * (sg + 1)],
                                        start=(kc == 0), stop=(kc == 15))
                            ins.then_inc(t_p5, 1)
                    blk.tensor(_)

                    def _(v):
                        for mt in range(16):
                            v.wait_ge(t_p5, mt + 1)
                            v.wait_ge(sx_a if mt % 2 == 0 else sx_b,
                                      16 * (mt // 2 + 1))
                            if mt >= 2:
                                v.wait_ge(st_a if mt % 2 == 0 else st_b,
                                          16 * (mt // 2))
                            v.tensor_tensor(out=yb[mt % 2][:],
                                            in0=ps_5[mt % 2][:, :],
                                            in1=xres[mt % 2][:], op=ALU.add)
                            MKV(f"y{mt}")
                    blk.vector(_)
            else:
                def _(v):
                    for mt in range(16):
                        v.wait_ge(sx_a if mt % 2 == 0 else sx_b,
                                  16 * (mt // 2 + 1))
                        if mt >= 2:
                            v.wait_ge(st_a if mt % 2 == 0 else st_b,
                                      16 * (mt // 2))
                        v.tensor_tensor(out=yb[mt % 2][:], in0=xres[mt % 2][:],
                                        in1=of[mt][:], op=ALU.add)
                        MKV(f"y{mt}")
                blk.vector(_)



    return nc



def prep_attn_inputs(layer, x_full, weights):
    if layer == 0:
        NH, NKV, HD = 32, 8, 64
        ro = l0_row_order()
        wq_t = tiled_lhsT(np.ascontiguousarray(weights["Wq"][ro].T))
        wk_t = tiled_lhsT(np.ascontiguousarray(weights["Wk"].T))
        wv_t = np.ascontiguousarray(weights["Wv"].T).reshape(16, 128, 512)
        wo_t = tiled_lhsT(np.ascontiguousarray(weights["Wo"][:, ro].T))
    else:
        NH, NKV, HD = 16, 4, 128
        wq2 = weights["Wq"].transpose(1, 0, 2).reshape(2048, 2048)
        wk2 = weights["Wk"].transpose(1, 0, 2).reshape(2048, 512)
        wv2 = weights["Wv"].transpose(1, 0, 2).reshape(2048, 512)
        wq_t = tiled_lhsT(np.ascontiguousarray(wq2))
        wk_t = tiled_lhsT(np.ascontiguousarray(wk2))
        wv_t = np.ascontiguousarray(wv2).reshape(16, 128, 512)
        bq = np.ascontiguousarray(weights["bq"].reshape(2048))
        bk = np.ascontiguousarray(weights["bk"].reshape(512))
        bv = np.ascontiguousarray(weights["bv"].reshape(512))
        bqk = np.concatenate([bq.reshape(16, 128).T, bk.reshape(4, 128).T], axis=1)
        bqk = np.ascontiguousarray(bqk.astype(np.float32))
        VG = HD + 1
        bvg = np.zeros((NKV, VG), np.float32)
        bvr = bv.reshape(NKV, HD)
        bvg[:, 0:64] = bvr[:, 0:64]
        bvg[:, 65:129] = bvr[:, 64:128]
        bvg_full = np.ascontiguousarray(
            np.broadcast_to(bvg.reshape(1, NKV * VG), (128, NKV * VG)))

    eq = head_indicator_tiles(2048, NH)
    ek = head_indicator_tiles(512, NKV)
    ident = np.eye(128, dtype=np.float32)
    kk, qq = np.meshgrid(np.arange(128), np.arange(128), indexing="ij")
    maskd = (qq >= kk).astype(np.float32)

    in_maps = []
    for c in range(NCORES):
        b, p = c // 2, c % 2
        perm = perm_for_core(p)
        xc = np.ascontiguousarray(x_full[b].T[:, perm])
        m = {
            "x": xc, "wq": wq_t, "wk": wk_t, "wv": wv_t,
            "eq": eq, "ek": ek,
            "maskp": np.full((128, 128), float(p), np.float32),
            "maskd": maskd, "ident": ident,
        }
        if layer == 0:
            m["wo"] = wo_t
            cq, sq_ = rope_tables(perm[:1024])
            ck, sk_ = rope_tables(perm)
            m["tabq"] = np.ascontiguousarray(np.concatenate([cq, sq_], axis=1))
            m["tabk"] = np.ascontiguousarray(np.concatenate([ck, sk_], axis=1))
        else:
            m["bqk"] = bqk
            m["bvg"] = bvg_full
        in_maps.append(m)
    return in_maps


def gather_attn_output(results, x_full):
    out = np.empty_like(x_full)
    for c in range(NCORES):
        b, p = c // 2, c % 2
        perm = perm_for_core(p)
        out[b, perm[:1024], :] = results[c]["y"].T
    return out


_NC_CACHE = {}


def run_attn_layer(layer, x_full, weights):
    key = ("attn", layer)
    if key not in _NC_CACHE:
        _NC_CACHE[key] = build_attn(layer)
    in_maps = prep_attn_inputs(layer, x_full, weights)
    res = run_bass_kernel_spmd(_NC_CACHE[key], in_maps, core_ids=list(range(NCORES)))
    return gather_attn_output(res.results, x_full)



def build_ffn(layer):
    marks = Marks()
    _build_ffn(layer, marks)
    marks.pass2 = True
    return _build_ffn(layer, marks)


def _emit_ln_1024(blk, marks, sems, xh, consts):
    NCC = 1024
    nsg = NCC // 512
    t_ln = sems["t_ln"]
    x_ld, XTOT = sems["x_ld"], sems["XTOT"]
    ones_c, ones_r = consts["ones_c"], consts["ones_r"]
    stat, ab_bf, abc = consts["stat"], consts["ab_bf"], consts["abc"]
    x2b = consts["x2b"]
    ps_sum, ps_sq = consts["lnps"], consts["lnpq"]
    ps_bc = consts["lnbc"]
    MKV, MKA, WV, WA = sems["MKV"], sems["MKA"], sems["WV"], sems["WA"]

    def _(sc):
        sc.wait_ge(x_ld, XTOT)
        for kc in range(16):
            if kc >= 2:
                sc.wait_ge(t_ln, kc - 1)
            sc.activation(out=x2b[kc % 2][:], in_=xh[kc][:], func=AF.Square)
            MKA(f"sq{kc}")
    blk.scalar(_)

    def _(t):
        WV(t, "ones")
        for kc in range(16):
            WA(t, f"sq{kc}")
            for sg in range(nsg):
                t.matmul(out=ps_sum[sg][0:1, :], lhsT=ones_c[:, 0:1],
                         rhs=xh[kc][:, 512 * sg:512 * (sg + 1)],
                         start=(kc == 0), stop=(kc == 15))
            ins = None
            for sg in range(nsg):
                ins = t.matmul(out=ps_sq[sg][0:1, :], lhsT=ones_c[:, 0:1],
                               rhs=x2b[kc % 2][:, 512 * sg:512 * (sg + 1)],
                               start=(kc == 0), stop=(kc == 15))
            ins.then_inc(t_ln, 1)
    blk.tensor(_)

    eps_r = stat[:, 1536:1537]
    mu_r = stat[:, 0:512]
    var_r = stat[:, 512:1024]
    sc_r = stat[:, 1024:1536]

    def _(v):
        v.memset(eps_r, EPS_LN)
        v.wait_ge(t_ln, 16)
        for sg in range(nsg):
            v.tensor_scalar_mul(out=mu_r, in0=ps_sum[sg][0:1, :], scalar1=1.0 / DM)
            v.tensor_scalar_mul(out=var_r, in0=ps_sq[sg][0:1, :], scalar1=1.0 / DM)
            v.tensor_tensor(out=sc_r, in0=mu_r, in1=mu_r, op=ALU.mult)
            v.tensor_tensor(out=var_r, in0=var_r, in1=sc_r, op=ALU.subtract)
            MKV(f"prep{sg}")
            WA(v, f"rstd{sg}")
            v.tensor_tensor(out=sc_r, in0=mu_r, in1=var_r, op=ALU.mult)
            v.tensor_scalar_mul(out=sc_r, in0=sc_r, scalar1=-1.0)
            v.tensor_copy(out=ab_bf[:, 512 * sg:512 * (sg + 1)], in_=var_r)
            v.tensor_copy(out=ab_bf[:, NCC + 512 * sg:NCC + 512 * (sg + 1)],
                          in_=sc_r)
            MKV(f"ab{sg}")
    blk.vector(_)

    def _(sc):
        for sg in range(nsg):
            WV(sc, f"prep{sg}")
            sc.activation(out=var_r, in_=var_r, func=AF.Ln, bias=eps_r)
            sc.activation(out=var_r, in_=var_r, func=AF.Exp, scale=-0.5)
            MKA(f"rstd{sg}")
    blk.scalar(_)

    def _(t):
        for half in range(2):
            if half == 1:
                WV(t, "xh_mul_done")
            for sg in range(nsg):
                idx = half * nsg + sg
                WV(t, f"ab{sg}" if half == 0 else f"ab{nsg - 1}")
                if idx >= 2:
                    WA(t, f"bce{idx - 2}")
                t.matmul(out=ps_bc[sg % 2][:, :], lhsT=ones_r[0:1, :],
                         rhs=ab_bf[:, 512 * idx:512 * (idx + 1)],
                         start=True, stop=True).then_inc(t_ln, 1)
    blk.tensor(_)

    def _(sc):
        for half in range(2):
            for sg in range(nsg):
                idx = half * nsg + sg
                sc.wait_ge(t_ln, 16 + idx + 1)
                if half == 1 and sg == 0:
                    WV(sc, "xh_mul_done")
                sc.copy(out=abc[:, 512 * sg:512 * (sg + 1)],
                        in_=ps_bc[sg % 2][:, :])
                MKA(f"bce{idx}")
    blk.scalar(_)

    def _(v):
        for half, op in ((0, ALU.mult), (1, ALU.add)):
            WA(v, f"bce{half * nsg + nsg - 1}")
            for kc in range(16):
                v.tensor_tensor(out=xh[kc][:], in0=xh[kc][:],
                                in1=abc[:, 0:NCC], op=op)
            MKV("xh_mul_done" if half == 0 else "xh_done")
    blk.vector(_)


def _build_ffn(layer, marks):
    nc = bass.Bass("TRN2", target_bir_lowering=False)
    nc.num_devices = 8

    xdr = nc.dram_tensor("x", [DM, TOKQ], F32, kind="ExternalInput").ap()
    ydr = nc.dram_tensor("y", [DM, TOKQ], F32, kind="ExternalOutput").ap()
    if layer == 0:
        wgd = nc.dram_tensor("wg", [64, 128, 16 * 128], U8, kind="ExternalInput").ap()
        wud = nc.dram_tensor("wu", [64, 128, 16 * 128], U8, kind="ExternalInput").ap()
        wdd = nc.dram_tensor("wd", [64, 128, 2048], U8, kind="ExternalInput").ap()
        sgd = nc.dram_tensor("sg", [128, 64], F32, kind="ExternalInput").ap()
        sud = nc.dram_tensor("su", [128, 64], F32, kind="ExternalInput").ap()
        sdd = nc.dram_tensor("sd", [128, 16], F32, kind="ExternalInput").ap()
    else:
        wfd = nc.dram_tensor("wf", [16, 128, 16 * 128], BF16, kind="ExternalInput").ap()
        bfd = nc.dram_tensor("bf", [128, 16], BF16, kind="ExternalInput").ap()
        idd = nc.dram_tensor("ident", [128, 128], F32, kind="ExternalInput").ap()
        cc_in = nc.dram_tensor("cc_in", [1, 1], F32, kind="Internal").ap()

    with ExitStack() as top:
        E = top.enter_context
        blk0 = E(nc.Block())

        x_f = [E(nc.sbuf_tensor(f"x{i}", [128, TOKQ], F32)) for i in range(16)]
        xh = [E(nc.sbuf_tensor(f"xh{i}", [128, TOKQ], BF16)) for i in range(16)]
        ones_c = E(nc.sbuf_tensor("ones_c", [128, 1], BF16))
        ones_r = E(nc.sbuf_tensor("ones_r", [1, 128], BF16))
        stat = E(nc.sbuf_tensor("stat", [1, 1664], F32))
        ab_bf = E(nc.sbuf_tensor("ab_bf", [1, 2048], BF16))
        abc = E(nc.sbuf_tensor("abc", [128, 1024], BF16))
        x2b = [E(nc.sbuf_tensor(f"x2b{i}", [128, 1024], BF16)) for i in range(2)]

        x_ld = E(nc.semaphore("x_ld"))
        xb_ld = E(nc.semaphore("xb_ld"))
        c_ld = E(nc.semaphore("c_ld"))
        wl = [E(nc.semaphore(f"wl{i}")) for i in range(4)]
        d_lda = E(nc.semaphore("d_lda"))
        d_ldb = E(nc.semaphore("d_ldb"))
        st_a = E(nc.semaphore("st_a"))
        st_b = E(nc.semaphore("st_b"))
        cc_s = E(nc.semaphore("cc_s"))
        cc_c = E(nc.semaphore("cc_c"))
        t_ln = E(nc.semaphore("t_ln"))
        t_mm = E(nc.semaphore("t_mm"))
        t_dn = E(nc.semaphore("t_dn"))
        v_self = E(nc.semaphore("v_self"))
        a_self = E(nc.semaphore("a_self"))
        g_self = E(nc.semaphore("g_self"))
        SV = {"sem": v_self, "n": 0}
        SA = {"sem": a_self, "n": 0}
        SG = {"sem": g_self, "n": 0}
        _vw, _aw, _gw = SerialEng(SV), SerialEng(SA), SerialEng(SG)

        class blk:
            tensor = staticmethod(blk0.tensor)
            sync = staticmethod(blk0.sync)
            vector = staticmethod(lambda f: blk0.vector(lambda e: f(_vw.bind(e))))
            scalar = staticmethod(lambda f: blk0.scalar(lambda e: f(_aw.bind(e))))
            gpsimd = staticmethod(lambda f: blk0.gpsimd(lambda e: f(_gw.bind(e))))

        def MKV(n):
            marks.mark(SV, n)

        def MKA(n):
            marks.mark(SA, n)

        def MKG(n):
            marks.mark(SG, n)

        def WV(eng, n):
            val = marks.get(n)
            if val > 0:
                eng.wait_ge(v_self, val)

        def WA(eng, n):
            val = marks.get(n)
            if val > 0:
                eng.wait_ge(a_self, val)

        def WG(eng, n):
            val = marks.get(n)
            if val > 0:
                eng.wait_ge(g_self, val)

        def _(s):
            for kc in range(16):
                s.dma_start(out=x_f[kc][:],
                            in_=xdr[128 * kc:128 * (kc + 1), :]).then_inc(x_ld, 16)
        blk.sync(_)
        XF_TOT = 16 * 16

        def _(gq):
            for kc in range(16):
                gq.dma_start(out=xh[kc][:],
                             in_=xdr[128 * kc:128 * (kc + 1), :]).then_inc(xb_ld, 16)
        blk.gpsimd(_)
        XTOT = 16 * 16

        def _(v):
            v.memset(ones_c[:], 1.0)
            v.memset(ones_r[:], 1.0)
            MKV("ones")
        blk.vector(_)

        with ExitStack() as pl:
            ee = pl.enter_context
            lnps = [ee(nc.psum_tensor(f"lnps{i}", [1, 512], F32)) for i in range(2)]
            lnpq = [ee(nc.psum_tensor(f"lnpq{i}", [1, 512], F32)) for i in range(2)]
            lnbc = [ee(nc.psum_tensor(f"lnbc{i}", [128, 512], F32)) for i in range(2)]
            sems = dict(x_ld=xb_ld, XTOT=XTOT, t_ln=t_ln,
                        MKV=MKV, MKA=MKA, WV=WV, WA=WA)
            consts = dict(ones_c=ones_c, ones_r=ones_r, stat=stat, ab_bf=ab_bf,
                          abc=abc, x2b=x2b, lnps=lnps, lnpq=lnpq, lnbc=lnbc)
            _emit_ln_1024(blk, marks, sems, xh, consts)

        if layer == 0:
            with ExitStack() as ph:
                e = ph.enter_context
                wb = [e(nc.sbuf_tensor(f"wb{i}", [128, 2048], BF16)) for i in range(4)]
                hb = [e(nc.sbuf_tensor(f"hb{i}", [128, TOKQ], BF16)) for i in range(8)]
                db = [e(nc.sbuf_tensor(f"db{i}", [128, 2048], BF16)) for i in range(8)]
                sg_s = e(nc.sbuf_tensor("sg_s", [128, 64], F32))
                su_s = e(nc.sbuf_tensor("su_s", [128, 64], F32))
                sd_s = e(nc.sbuf_tensor("sd_s", [128, 16], F32))
                gb = e(nc.sbuf_tensor("gb", [128, TOKQ], BF16))
                ub = e(nc.sbuf_tensor("ub", [128, TOKQ], BF16))
                ps_g = e(nc.psum_tensor("ps_g", [128, TOKQ], F32))
                ps_u = e(nc.psum_tensor("ps_u", [128, TOKQ], F32))
                ps_d = [e(nc.psum_tensor(f"ps_d{i}", [128, TOKQ], F32))
                        for i in range(2)]

                def _(s):
                    s.dma_start(out=sg_s[:], in_=sgd).then_inc(c_ld, 16)
                    s.dma_start(out=su_s[:], in_=sud).then_inc(c_ld, 16)
                    s.dma_start(out=sd_s[:], in_=sdd).then_inc(c_ld, 16)
                blk.sync(_)
                CTOT = 48

                def gu_src(i):
                    return (wud if i % 2 else wgd)[i // 2]

                def _(gq):
                    for i in range(4):
                        gq.dma_start(out=wb[i][:], in_=gu_src(i)).then_inc(wl[i], 16)
                    for c in range(8):
                        gq.dma_start(out=db[c][:],
                                     in_=wdd[c]).then_inc(d_lda, 16)
                    for r in range(8):
                        for m in range(8):
                            mt = 8 * r + m
                            for i in (2 * mt, 2 * mt + 1):
                                gq.wait_ge(wl[i % 4], 16 * (i // 4 + 1))
                                gq.tensor_scalar_add(out=wb[i % 4][:],
                                                     in0=wb[i % 4][:],
                                                     scalar1=-128.0)
                                MKG(f"wsub{i}")
                                if i + 4 < 128:
                                    gq.wait_ge(t_mm, max(0, i + 1))
                                    gq.dma_start(out=wb[(i + 4) % 4][:],
                                                 in_=gu_src(i + 4)).then_inc(
                                                     wl[(i + 4) % 4], 16)
                        gq.wait_ge(d_lda, 16 * 8 * (r + 1))
                        for c in range(8):
                            gq.tensor_scalar_add(out=db[c][:], in0=db[c][:],
                                                 scalar1=-128.0)
                        MKG(f"dsub{r}")
                        if r + 1 < 8:
                            gq.wait_ge(t_dn, 16 * (r + 1))
                            for c in range(8):
                                gq.dma_start(out=db[c][:],
                                             in_=wdd[8 * (r + 1) + c]).then_inc(
                                                 d_lda, 16)
                blk.gpsimd(_)

                def _(t):
                    WV(t, "xh_done")
                    for r in range(8):
                        for m in range(8):
                            mt = 8 * r + m
                            for i in (2 * mt, 2 * mt + 1):
                                WG(t, f"wsub{i}")
                                is_u = i % 2
                                if mt >= 1:
                                    WV(t, f"usc{mt - 1}" if is_u else f"gsc{mt - 1}")
                                ps = ps_u if is_u else ps_g
                                ins = None
                                for kc in range(16):
                                    for sgm in range(2):
                                        ins = t.matmul(
                                            out=ps[:, 512 * sgm:512 * (sgm + 1)],
                                            lhsT=wb[i % 4][:, 128 * kc:128 * (kc + 1)],
                                            rhs=xh[kc][:, 512 * sgm:512 * (sgm + 1)],
                                            start=(kc == 0), stop=(kc == 15))
                                ins.then_inc(t_mm, 1)
                        WG(t, f"dsub{r}")
                        WV(t, f"h{8 * r + 7}")
                        for ht in range(16):
                            if 16 * r + ht >= 2:
                                WV(t, f"dev{16 * r + ht - 2}")
                            ins = None
                            for c in range(8):
                                for sgm in range(2):
                                    ins = t.matmul(
                                        out=ps_d[ht % 2][:, 512 * sgm:512 * (sgm + 1)],
                                        lhsT=db[c][:, 128 * ht:128 * (ht + 1)],
                                        rhs=hb[c][:, 512 * sgm:512 * (sgm + 1)],
                                        start=(c == 0), stop=(c == 7))
                            ins.then_inc(t_dn, 1)
                blk.tensor(_)

                def _(sc):
                    for mt in range(64):
                        WV(sc, f"gsc{mt}")
                        sc.activation(out=gb[:], in_=gb[:], func=AF.Silu)
                        MKA(f"silu{mt}")
                blk.scalar(_)

                def _(v):
                    v.wait_ge(c_ld, CTOT)
                    v.wait_ge(x_ld, XF_TOT)
                    for r in range(8):
                        for m in range(8):
                            mt = 8 * r + m
                            WA(v, f"silu{mt - 1}") if mt >= 1 else None
                            v.wait_ge(t_mm, 2 * mt + 1)
                            v.tensor_scalar_mul(out=gb[:], in0=ps_g[:, :],
                                                scalar1=sg_s[:, mt:mt + 1])
                            MKV(f"gsc{mt}")
                            v.wait_ge(t_mm, 2 * mt + 2)
                            v.tensor_scalar_mul(out=ub[:], in0=ps_u[:, :],
                                                scalar1=su_s[:, mt:mt + 1])
                            MKV(f"usc{mt}")
                            WA(v, f"silu{mt}")
                            v.tensor_tensor(out=hb[m][:], in0=gb[:], in1=ub[:],
                                            op=ALU.mult)
                            MKV(f"h{mt}")
                        for ht in range(16):
                            v.wait_ge(t_dn, 16 * r + ht + 1)
                            v.scalar_tensor_tensor(out=x_f[ht][:],
                                                   in0=ps_d[ht % 2][:, :],
                                                   scalar=sd_s[:, ht:ht + 1],
                                                   in1=x_f[ht][:],
                                                   op0=ALU.mult, op1=ALU.add)
                            MKV(f"dev{16 * r + ht}")
                blk.vector(_)

                def _(s):
                    for ht in range(16):
                        WV(s, f"dev{16 * 7 + ht}")
                        s.dma_start(out=ydr[128 * ht:128 * (ht + 1), :],
                                    in_=x_f[ht][:]).then_inc(
                                        st_a if ht % 2 == 0 else st_b, 16)
                    s.wait_ge(st_a, 16 * 8)
                    s.wait_ge(st_b, 16 * 8)
                blk.sync(_)

        else:
            with ExitStack() as ph:
                e = ph.enter_context
                wq = [e(nc.sbuf_tensor(f"wq{i}", [128, 2048], FP8)) for i in range(16)]
                xq = [e(nc.sbuf_tensor(f"xq{i}", [128, TOKQ], FP8)) for i in range(16)]
                wstg = [e(nc.sbuf_tensor(f"wstg{i}", [128, 2048], BF16))
                        for i in range(2)]
                id_s = e(nc.sbuf_tensor("id_s", [128, 128], F32))
                bf_s = e(nc.sbuf_tensor("bf_s", [128, 16], BF16))
                bf_f = e(nc.sbuf_tensor("bf_f", [128, 16], F32))
                red = e(nc.sbuf_tensor("red", [128, 4], F32))
                sca = e(nc.sbuf_tensor("sca", [1, 8], F32))
                sc_bf = e(nc.sbuf_tensor("sc_bf", [1, 4], BF16))
                sc3 = e(nc.sbuf_tensor("sc3", [128, 4], F32))
                ybf = e(nc.sbuf_tensor("ybf", [128, TOKQ], BF16))
                ps_m = [e(nc.psum_tensor(f"ps_m{i}", [128, TOKQ], F32))
                        for i in range(2)]
                ps_t = e(nc.psum_tensor("ps_t", [1, 128], F32))
                ps_t2 = e(nc.psum_tensor("ps_t2", [1, 128], F32))
                ps_c = e(nc.psum_tensor("ps_c", [128, 4], F32))

                def _(s):
                    s.dma_start(out=bf_s[:], in_=bfd).then_inc(c_ld, 16)
                    s.dma_start(out=id_s[:], in_=idd).then_inc(c_ld, 16)
                blk.sync(_)
                CTOT = 32

                def _(gq):
                    for i in range(16):
                        if i >= 2:
                            WV(gq, f"wmax{i - 2}")
                        gq.dma_start(out=wstg[i % 2][:], in_=wfd[i]).then_inc(
                            wl[i % 2], 16)
                blk.gpsimd(_)

                def _(v):
                    for i in range(16):
                        v.wait_ge(wl[i % 2], 16 * (i // 2 + 1))
                        v.tensor_reduce(out=red[:, 0:1], in_=wstg[i % 2][:],
                                        axis=AX.X, op=ALU.max,
                                        apply_absolute_value=True)
                        if i == 0:
                            v.tensor_copy(out=red[:, 1:2], in_=red[:, 0:1])
                        else:
                            v.tensor_tensor(out=red[:, 1:2], in0=red[:, 1:2],
                                            in1=red[:, 0:1], op=ALU.max)
                        MKV(f"wmax{i}")
                    WV(v, "xh_done")
                    for kc in range(16):
                        v.tensor_reduce(out=red[:, 2:3], in_=xh[kc][:],
                                        axis=AX.X, op=ALU.max,
                                        apply_absolute_value=True)
                        if kc == 0:
                            v.tensor_copy(out=red[:, 3:4], in_=red[:, 2:3])
                        else:
                            v.tensor_tensor(out=red[:, 3:4], in0=red[:, 3:4],
                                            in1=red[:, 2:3], op=ALU.max)
                    MKV("xmax_p")
                blk.vector(_)

                def _(t):
                    WV(t, "xmax_p")
                    t.wait_ge(c_ld, CTOT)
                    t.transpose(out=ps_t[0:1, 0:128], in_=red[:, 3:4],
                                identity=id_s[:, :]).then_inc(t_mm, 1)
                    t.transpose(out=ps_t2[0:1, 0:128], in_=red[:, 1:2],
                                identity=id_s[:, :]).then_inc(t_mm, 1)
                blk.tensor(_)

                def _(v):
                    v.wait_ge(t_mm, 2)
                    v.tensor_reduce(out=sca[:, 0:1], in_=ps_t[0:1, 0:128],
                                    axis=AX.X, op=ALU.max)
                    v.tensor_scalar_max(out=sca[:, 0:1], in0=sca[:, 0:1],
                                        scalar1=1e-6)
                    v.tensor_reduce(out=sca[:, 2:3], in_=ps_t2[0:1, 0:128],
                                    axis=AX.X, op=ALU.max)
                    v.tensor_scalar_max(out=sca[:, 2:3], in0=sca[:, 2:3],
                                        scalar1=1e-6)
                    MKV("xs_loc")
                blk.vector(_)

                def _(gq):
                    WV(gq, "xs_loc")
                    gq.dma_start(out=cc_in, in_=sca[:, 0:1]).then_inc(cc_s, 16)
                    gq.wait_ge(cc_s, 16)
                    gq.collective_compute(
                        "AllReduce", ALU.max,
                        replica_groups=[[0, 1, 2, 3, 4, 5, 6, 7]],
                        ins=[cc_in], outs=[cc_in],
                    ).then_inc(cc_c, 1)
                    gq.wait_ge(cc_c, 1)
                    gq.dma_start(out=sca[:, 1:2], in_=cc_in).then_inc(cc_s, 16)
                    for i in range(16):
                        WV(gq, f"wq{i - 2}" if i >= 2 else f"wmax{14 + i}")
                        gq.dma_start(out=wstg[i % 2][:], in_=wfd[i]).then_inc(
                            wl[i % 2], 16)
                blk.gpsimd(_)

                def _(v):
                    v.wait_ge(cc_s, 32)
                    v.wait_ge(cc_c, 1)
                    v.reciprocal(out=sca[:, 3:4], in_=sca[:, 1:2])
                    v.reciprocal(out=sca[:, 4:5], in_=sca[:, 2:3])
                    v.tensor_tensor(out=sca[:, 5:6], in0=sca[:, 1:2],
                                    in1=sca[:, 2:3], op=ALU.mult)
                    v.tensor_copy(out=sc_bf[:, 0:3], in_=sca[:, 3:6])
                    MKV("scbf")
                blk.vector(_)

                def _(t):
                    WV(t, "scbf")
                    t.matmul(out=ps_c[:, 0:3], lhsT=ones_r[0:1, :],
                             rhs=sc_bf[0:1, 0:3], start=True,
                             stop=True).then_inc(t_mm, 1)
                blk.tensor(_)

                def _(sc):
                    sc.wait_ge(t_mm, 3)
                    sc.copy(out=sc3[:, 0:3], in_=ps_c[:, 0:3])
                    MKA("sc3")
                    sc.wait_ge(c_ld, CTOT)
                    sc.copy(out=bf_f[:, :], in_=bf_s[:, :])
                    MKA("bff")
                blk.scalar(_)

                def _(v):
                    WA(v, "sc3")
                    for i in range(16):
                        v.wait_ge(wl[i % 2], 16 * (8 + i // 2 + 1))
                        v.tensor_scalar_mul(out=wq[i][:], in0=wstg[i % 2][:],
                                            scalar1=sc3[:, 1:2])
                        MKV(f"wq{i}")
                    for kc in range(16):
                        v.tensor_scalar_mul(out=xq[kc][:], in0=xh[kc][:],
                                            scalar1=sc3[:, 0:1])
                        MKV(f"xq{kc}")
                blk.vector(_)

                def _(t):
                    WV(t, "xq15")
                    for mt in range(16):
                        WV(t, f"wq{mt}")
                        if mt >= 2:
                            WV(t, f"yev{mt - 2}")
                        ins = None
                        for kc in range(16):
                            for sgm in range(2):
                                ins = t.matmul(
                                    out=ps_m[mt % 2][:, 512 * sgm:512 * (sgm + 1)],
                                    lhsT=wq[mt][:, 128 * kc:128 * (kc + 1)],
                                    rhs=xq[kc][:, 512 * sgm:512 * (sgm + 1)],
                                    start=(kc == 0), stop=(kc == 15))
                        ins.then_inc(t_dn, 1)
                blk.tensor(_)

                def _(v):
                    WA(v, "bff")
                    v.wait_ge(x_ld, XF_TOT)
                    for mt in range(16):
                        v.wait_ge(t_dn, mt + 1)
                        v.tensor_scalar_mul(out=ybf[:], in0=ps_m[mt % 2][:, :],
                                            scalar1=sc3[:, 2:3])
                        v.tensor_scalar_add(out=ybf[:], in0=ybf[:],
                                            scalar1=bf_f[:, mt:mt + 1])
                        v.tensor_tensor(out=x_f[mt][:], in0=x_f[mt][:],
                                        in1=ybf[:], op=ALU.add)
                        MKV(f"yev{mt}")
                blk.vector(_)

                def _(s):
                    for mt in range(16):
                        WV(s, f"yev{mt}")
                        s.dma_start(out=ydr[128 * mt:128 * (mt + 1), :],
                                    in_=x_f[mt][:]).then_inc(
                                        st_a if mt % 2 == 0 else st_b, 16)
                    s.wait_ge(st_a, 16 * 8)
                    s.wait_ge(st_b, 16 * 8)
                blk.sync(_)

    return nc



def prep_ffn_inputs(layer, x_full, weights):
    in_maps = []
    if layer == 0:
        wg_t = tiled_lhsT(np.ascontiguousarray(u8_view(weights["gate_w"]).T))
        wu_t = tiled_lhsT(np.ascontiguousarray(u8_view(weights["up_w"]).T))
        wd_t = np.ascontiguousarray(u8_view(weights["down_w"]).T).reshape(64, 128, 2048)
        sg_ = np.ascontiguousarray(weights["gate_scale"].reshape(64, 128).T)
        su_ = np.ascontiguousarray(weights["up_scale"].reshape(64, 128).T)
        sd_ = np.ascontiguousarray(weights["down_scale"].reshape(16, 128).T)
        base = {"wg": wg_t, "wu": wu_t, "wd": wd_t,
                "sg": np.ascontiguousarray(sg_.astype(np.float32)),
                "su": np.ascontiguousarray(su_.astype(np.float32)),
                "sd": np.ascontiguousarray(sd_.astype(np.float32))}
    else:
        wf = np.ascontiguousarray(weights["weight"])
        wf_t = tiled_lhsT(np.ascontiguousarray(wf.T))
        bias = np.ascontiguousarray(weights["bias"].reshape(16, 128).T)
        base = {"wf": wf_t, "bf": bias,
                "ident": np.eye(128, dtype=np.float32)}
    for c in range(NCORES):
        b, p = c // 2, c % 2
        perm = perm_for_core(p)
        xc = np.ascontiguousarray(x_full[b].T[:, perm[:1024]])
        m = dict(base)
        m["x"] = xc
        in_maps.append(m)
    return in_maps


def gather_ffn_output(results, x_full):
    out = np.empty_like(x_full)
    for c in range(NCORES):
        b, p = c // 2, c % 2
        perm = perm_for_core(p)
        out[b, perm[:1024], :] = results[c]["y"].T
    return out


def run_ffn_layer(layer, x_full, weights):
    key = ("ffn", layer)
    if key not in _NC_CACHE:
        _NC_CACHE[key] = build_ffn(layer)
    in_maps = prep_ffn_inputs(layer, x_full, weights)
    res = run_bass_kernel_spmd(_NC_CACHE[key], in_maps, core_ids=list(range(NCORES)))
    return gather_ffn_output(res.results, x_full)


def kernel(**inputs):
    x = np.ascontiguousarray(inputs["L0_attn_hidden_states"], dtype=np.float32)
    x = run_attn_layer(0, x, {k: inputs[f"L0_attn_{k}"]
                              for k in ("Wq", "Wk", "Wv", "Wo")})
    x = run_ffn_layer(0, x, {k: inputs[f"L0_ffn_{k}"]
                             for k in ("gate_w", "gate_scale", "up_w",
                                       "up_scale", "down_w", "down_scale")})
    x = run_attn_layer(1, x, {k: inputs[f"L1_attn_{k}"]
                              for k in ("Wq", "bq", "Wk", "bk", "Wv", "bv")})
    x = run_ffn_layer(1, x, {"weight": inputs["L1_ffn_weight"],
                             "bias": inputs["L1_ffn_bias"]})
    return x

def build_fused():
    marks = Marks()
    _build_fused(marks)
    marks.pass2 = True
    return _build_fused(marks)


def _build_fused(marks):
    nc = bass.Bass("TRN2", target_bir_lowering=False)
    nc.num_devices = 8

    xdr = nc.dram_tensor("x", [DM, TOKQ], F32, kind="ExternalInput").ap()
    ydr = nc.dram_tensor("y", [DM, TOKQ], F32, kind="ExternalOutput").ap()
    d = {}
    d["wq0"] = nc.dram_tensor("wq0", [16, 128, 2048], F32, kind="ExternalInput").ap()
    d["wk0"] = nc.dram_tensor("wk0", [16, 128, 512], F32, kind="ExternalInput").ap()
    d["wv0"] = nc.dram_tensor("wv0", [16, 128, 512], F32, kind="ExternalInput").ap()
    d["wo0"] = nc.dram_tensor("wo0", [16, 128, 2048], F32, kind="ExternalInput").ap()
    d["tabq0"] = nc.dram_tensor("tabq0", [128, 2048], F32, kind="ExternalInput").ap()
    d["tabk0"] = nc.dram_tensor("tabk0", [128, 4096], F32, kind="ExternalInput").ap()
    d["eq0"] = nc.dram_tensor("eq0", [128, 512], F32, kind="ExternalInput").ap()
    d["ek0"] = nc.dram_tensor("ek0", [128, 32], F32, kind="ExternalInput").ap()
    d["wq1"] = nc.dram_tensor("wq1", [16, 128, 2048], F32, kind="ExternalInput").ap()
    d["wk1"] = nc.dram_tensor("wk1", [16, 128, 512], F32, kind="ExternalInput").ap()
    d["wv1"] = nc.dram_tensor("wv1", [16, 128, 512], F32, kind="ExternalInput").ap()
    d["bq1"] = nc.dram_tensor("bq1", [128, 16], F32, kind="ExternalInput").ap()
    d["bk1"] = nc.dram_tensor("bk1", [128, 512], F32, kind="ExternalInput").ap()
    d["bv1"] = nc.dram_tensor("bv1", [128, 516], F32, kind="ExternalInput").ap()
    d["eq1"] = nc.dram_tensor("eq1", [128, 256], F32, kind="ExternalInput").ap()
    d["ek1"] = nc.dram_tensor("ek1", [128, 16], F32, kind="ExternalInput").ap()
    d["wg"] = nc.dram_tensor("wg", [64, 128, 2048], U8, kind="ExternalInput").ap()
    d["wu"] = nc.dram_tensor("wu", [64, 128, 2048], U8, kind="ExternalInput").ap()
    d["wd"] = nc.dram_tensor("wd", [64, 128, 2048], U8, kind="ExternalInput").ap()
    d["sg"] = nc.dram_tensor("sg", [128, 64], F32, kind="ExternalInput").ap()
    d["su"] = nc.dram_tensor("su", [128, 64], F32, kind="ExternalInput").ap()
    d["sd"] = nc.dram_tensor("sd", [128, 16], F32, kind="ExternalInput").ap()
    d["wf"] = nc.dram_tensor("wf", [16, 128, 2048], BF16, kind="ExternalInput").ap()
    d["bfb"] = nc.dram_tensor("bfb", [128, 16], BF16, kind="ExternalInput").ap()
    d["maske"] = nc.dram_tensor("maske", [128, 128], F32, kind="ExternalInput").ap()
    d["masko"] = nc.dram_tensor("masko", [128, 128], F32, kind="ExternalInput").ap()
    d["ident"] = nc.dram_tensor("ident", [128, 128], F32, kind="ExternalInput").ap()
    d["sidx"] = nc.dram_tensor("sidx", [128, 8], U32, kind="ExternalInput").ap()
    shr_k = [nc.dram_tensor(f"shr_k{l}", [2048, 512], BF16, kind="Internal",
                            addr_space="Shared").ap() for l in range(2)]
    shr_v = [nc.dram_tensor(f"shr_v{l}", [2048, 520], BF16, kind="Internal",
                            addr_space="Shared").ap() for l in range(2)]
    bar = nc.dram_tensor("bar", [1, 1], F32, kind="Internal").ap()
    cc_in = nc.dram_tensor("cc_in", [1, 1], F32, kind="Internal").ap()

    with ExitStack() as top:
        E = top.enter_context
        blk0 = E(nc.Block())

        x_f = [E(nc.sbuf_tensor(f"x{i}", [128, TOKQ], F32)) for i in range(16)]
        xh = [E(nc.sbuf_tensor(f"xh{i}", [128, TOKQ], BF16)) for i in range(16)]
        ones_c = E(nc.sbuf_tensor("ones_c", [128, 1], BF16))
        ones_r = E(nc.sbuf_tensor("ones_r", [1, 128], BF16))
        zrow = E(nc.sbuf_tensor("zrow", [1, 65], BF16))
        stat = E(nc.sbuf_tensor("stat", [1, 1537], F32))
        ab_bf = E(nc.sbuf_tensor("ab_bf", [1, 2048], BF16))
        abc = E(nc.sbuf_tensor("abc", [128, 1024], BF16))
        x2b = [E(nc.sbuf_tensor(f"x2b{i}", [128, 1024], BF16)) for i in range(2)]
        mk_e = E(nc.sbuf_tensor("mk_e", [128, 128], BF16))
        mk_o = E(nc.sbuf_tensor("mk_o", [128, 128], BF16))
        id_s = E(nc.sbuf_tensor("id_s", [128, 128], F32))
        bias_sb = E(nc.sbuf_tensor("bias_sb", [128, 1], F32))
        sidx_s = E(nc.sbuf_tensor("sidx_s", [128, 8], U32))
        zf = E(nc.sbuf_tensor("zf", [1, 1], F32))

        x_ld = E(nc.semaphore("x_ld"))
        xb_ld = E(nc.semaphore("xb_ld"))
        s_ld = E(nc.semaphore("s_ld"))
        m_ld = E(nc.semaphore("m_ld"))
        ca = [E(nc.semaphore(f"ca{l}")) for l in range(2)]
        cv = [E(nc.semaphore(f"cv{l}")) for l in range(2)]
        b_ld = E(nc.semaphore("b_ld"))
        f0c = E(nc.semaphore("f0c"))
        f1c = E(nc.semaphore("f1c"))
        wl = [E(nc.semaphore(f"wl{i}")) for i in range(4)]
        d_lda = E(nc.semaphore("d_lda"))
        sh_a = E(nc.semaphore("sh_a"))
        sh_b = E(nc.semaphore("sh_b"))
        rb_s = E(nc.semaphore("rb_s"))
        st_a = E(nc.semaphore("st_a"))
        st_b = E(nc.semaphore("st_b"))
        cc_s = E(nc.semaphore("cc_s"))
        cc_c = E(nc.semaphore("cc_c"))
        bz = E(nc.semaphore("bz"))
        t_ln = E(nc.semaphore("t_ln"))
        sems_attn = {}
        for pfx in ("a0.", "a1."):
            for nm in ("t_kv", "t_q", "t_nb", "t_s", "t_av", "t_bc", "t_p5"):
                sems_attn[pfx + nm] = E(nc.semaphore(pfx + nm))
        t_mm = E(nc.semaphore("t_mm"))
        t_dn = E(nc.semaphore("t_dn"))
        t_f1 = E(nc.semaphore("t_f1"))
        v_self = E(nc.semaphore("v_self"))
        a_self = E(nc.semaphore("a_self"))
        g_self = E(nc.semaphore("g_self"))
        SV = {"sem": v_self, "n": 0}
        SA = {"sem": a_self, "n": 0}
        SG = {"sem": g_self, "n": 0}
        _vw, _aw, _gw = SerialEng(SV), SerialEng(SA), SerialEng(SG)

        class blk:
            tensor = staticmethod(blk0.tensor)
            sync = staticmethod(blk0.sync)
            vector = staticmethod(lambda f: blk0.vector(lambda e: f(_vw.bind(e))))
            scalar = staticmethod(lambda f: blk0.scalar(lambda e: f(_aw.bind(e))))
            gpsimd = staticmethod(lambda f: blk0.gpsimd(lambda e: f(_gw.bind(e))))

        def MKV(n):
            marks.mark(SV, n)

        def MKA(n):
            marks.mark(SA, n)

        def MKG(n):
            marks.mark(SG, n)

        def WV(eng, n):
            val = marks.get(n)
            if val > 0:
                eng.wait_ge(v_self, val)

        def WA(eng, n):
            val = marks.get(n)
            if val > 0:
                eng.wait_ge(a_self, val)

        def WG(eng, n):
            val = marks.get(n)
            if val > 0:
                eng.wait_ge(g_self, val)

        tot = {"sh": [0, 0], "rb": 0, "wl": [0, 0, 0, 0], "d": 0,
               "cc_c": 0, "cc_s": 0, "tln": 0}
        mkc = [mk_e if c % 2 == 0 else mk_o for c in range(16)]

        def _(s):
            for kc in range(16):
                s.dma_start(out=x_f[kc][:],
                            in_=xdr[128 * kc:128 * (kc + 1), :]).then_inc(x_ld, 16)
            s.dma_start(out=id_s[:], in_=d["ident"]).then_inc(s_ld, 16)
            s.dma_start(out=sidx_s[:], in_=d["sidx"]).then_inc(s_ld, 16)
        blk.sync(_)

        def _(gq):
            for kc in range(16):
                gq.dma_start(out=xh[kc][:],
                             in_=xdr[128 * kc:128 * (kc + 1), :]).then_inc(xb_ld, 16)
            gq.dma_start(out=mk_e[:], in_=d["maske"]).then_inc(m_ld, 16)
            gq.dma_start(out=mk_o[:], in_=d["masko"]).then_inc(m_ld, 16)
        blk.gpsimd(_)

        def _(v):
            v.memset(ones_c[:], 1.0)
            v.memset(ones_r[:], 1.0)
            v.memset(zrow[:], 0.0)
            v.memset(zf[:], 0.0)
            MKV("ones")
        blk.vector(_)

        def _(gq):
            WV(gq, "ones")
            gq.dma_start(out=bar, in_=zf[:]).then_inc(bz, 16)
        blk.gpsimd(_)

        def emit_ln(pfx, gate_scalar):
            nsg = 2
            tb = tot["tln"]
            with ExitStack() as lnp:
                le = lnp.enter_context
                lnps = [le(nc.psum_tensor(f"{pfx}lnps{i}", [1, 512], F32))
                        for i in range(2)]
                lnpq = [le(nc.psum_tensor(f"{pfx}lnpq{i}", [1, 512], F32))
                        for i in range(2)]
                lnbc = [le(nc.psum_tensor(f"{pfx}lnbc{i}", [128, 512], F32))
                        for i in range(2)]

                def _(sc):
                    gate_scalar(sc)
                    for kc in range(16):
                        if kc >= 2:
                            sc.wait_ge(t_ln, tb + kc - 1)
                        sc.activation(out=x2b[kc % 2][:], in_=xh[kc][:],
                                      func=AF.Square)
                        MKA(f"{pfx}sq{kc}")
                blk.scalar(_)

                def _(t):
                    WV(t, "ones")
                    for kc in range(16):
                        WA(t, f"{pfx}sq{kc}")
                        for sg_ in range(nsg):
                            t.matmul(out=lnps[sg_][0:1, :], lhsT=ones_c[:, 0:1],
                                     rhs=xh[kc][:, 512 * sg_:512 * (sg_ + 1)],
                                     start=(kc == 0), stop=(kc == 15))
                        ins = None
                        for sg_ in range(nsg):
                            ins = t.matmul(out=lnpq[sg_][0:1, :],
                                           lhsT=ones_c[:, 0:1],
                                           rhs=x2b[kc % 2][:, 512 * sg_:512 * (sg_ + 1)],
                                           start=(kc == 0), stop=(kc == 15))
                        ins.then_inc(t_ln, 1)
                blk.tensor(_)

                eps_r = stat[:, 1536:1537]
                mu_r = stat[:, 0:512]
                var_r = stat[:, 512:1024]
                sc_r = stat[:, 1024:1536]

                def _(v):
                    if pfx == "a0.":
                        v.memset(eps_r, EPS_LN)
                    v.wait_ge(t_ln, tb + 16)
                    for sg_ in range(nsg):
                        v.tensor_scalar_mul(out=mu_r, in0=lnps[sg_][0:1, :],
                                            scalar1=1.0 / DM)
                        v.tensor_scalar_mul(out=var_r, in0=lnpq[sg_][0:1, :],
                                            scalar1=1.0 / DM)
                        v.tensor_tensor(out=sc_r, in0=mu_r, in1=mu_r,
                                        op=ALU.mult)
                        v.tensor_tensor(out=var_r, in0=var_r, in1=sc_r,
                                        op=ALU.subtract)
                        MKV(f"{pfx}prep{sg_}")
                        WA(v, f"{pfx}rstd{sg_}")
                        v.tensor_tensor(out=sc_r, in0=mu_r, in1=var_r,
                                        op=ALU.mult)
                        v.tensor_scalar_mul(out=sc_r, in0=sc_r, scalar1=-1.0)
                        v.tensor_copy(out=ab_bf[:, 512 * sg_:512 * (sg_ + 1)],
                                      in_=var_r)
                        v.tensor_copy(
                            out=ab_bf[:, 1024 + 512 * sg_:1024 + 512 * (sg_ + 1)],
                            in_=sc_r)
                        MKV(f"{pfx}ab{sg_}")
                blk.vector(_)

                def _(sc):
                    for sg_ in range(nsg):
                        WV(sc, f"{pfx}prep{sg_}")
                        sc.activation(out=var_r, in_=var_r, func=AF.Ln,
                                      bias=eps_r)
                        sc.activation(out=var_r, in_=var_r, func=AF.Exp,
                                      scale=-0.5)
                        MKA(f"{pfx}rstd{sg_}")
                blk.scalar(_)

                def _(t):
                    for half in range(2):
                        if half == 1:
                            WV(t, f"{pfx}xh_mul")
                        for sg_ in range(nsg):
                            idx = half * nsg + sg_
                            WV(t, f"{pfx}ab{sg_}" if half == 0
                               else f"{pfx}ab{nsg - 1}")
                            if idx >= 2:
                                WA(t, f"{pfx}bce{idx - 2}")
                            t.matmul(out=lnbc[sg_ % 2][:, :],
                                     lhsT=ones_r[0:1, :],
                                     rhs=ab_bf[:, 512 * idx:512 * (idx + 1)],
                                     start=True, stop=True).then_inc(t_ln, 1)
                blk.tensor(_)

                def _(sc):
                    for half in range(2):
                        for sg_ in range(nsg):
                            idx = half * nsg + sg_
                            sc.wait_ge(t_ln, tb + 16 + idx + 1)
                            if half == 1 and sg_ == 0:
                                WV(sc, f"{pfx}xh_mul")
                            sc.copy(out=abc[:, 512 * sg_:512 * (sg_ + 1)],
                                    in_=lnbc[sg_ % 2][:, :])
                            MKA(f"{pfx}bce{idx}")
                blk.scalar(_)

                def _(v):
                    for half, op in ((0, ALU.mult), (1, ALU.add)):
                        WA(v, f"{pfx}bce{half * nsg + nsg - 1}")
                        for kc in range(16):
                            v.tensor_tensor(out=xh[kc][:], in0=xh[kc][:],
                                            in1=abc[:, 0:1024], op=op)
                        MKV(f"{pfx}xh_mul" if half == 0 else f"{pfx}xh_done")
                blk.vector(_)
            tot["tln"] += 20

        def emit_attn(layer, pfx, gate_g):
            if layer == 0:
                NH, NKV, HD = 32, 8, 64
                rope, bias_qkv, outproj = True, False, True
                scale = 0.125
                wqd_, wkd_, wvd_, wod_ = d["wq0"], d["wk0"], d["wv0"], d["wo0"]
            else:
                NH, NKV, HD = 16, 4, 128
                rope, bias_qkv, outproj = False, True, False
                scale = 1.0 / float(np.sqrt(128.0))
                wqd_, wkd_, wvd_ = d["wq1"], d["wk1"], d["wv1"]
                wod_ = None
            VG = HD + 1
            shk, shv = shr_k[layer], shr_v[layer]
            stor_heads = l0_head_order() if layer == 0 else list(range(NH))
            t_kv = sems_attn[pfx + "t_kv"]
            t_q = sems_attn[pfx + "t_q"]
            t_nb = sems_attn[pfx + "t_nb"]
            t_s = sems_attn[pfx + "t_s"]
            t_av = sems_attn[pfx + "t_av"]
            t_bc = sems_attn[pfx + "t_bc"]
            t_p5 = sems_attn[pfx + "t_p5"]

            with ExitStack() as ph:
                e = ph.enter_context
                qsb = [e(nc.sbuf_tensor(f"{pfx}q{i}", [128, TOKQ], BF16))
                       for i in range(16)]
                vsb = [e(nc.sbuf_tensor(f"{pfx}v{i}", [128, NKV * VG], BF16))
                       for i in range(16)]
                rp = [e(nc.sbuf_tensor(f"{pfx}rp{i}", [128, 2048], BF16))
                      for i in range(2)]
                wbuf = [e(nc.sbuf_tensor(f"{pfx}wb{i}", [128, DM], BF16))
                        for i in range(2)]
                if rope:
                    tabq = e(nc.sbuf_tensor(f"{pfx}tq", [128, 2048], BF16))
                if bias_qkv:
                    bq_s = e(nc.sbuf_tensor(f"{pfx}bq", [128, 16], F32))
                    bk_r = e(nc.sbuf_tensor(f"{pfx}bkr", [128, 512], BF16))
                    bv_r = e(nc.sbuf_tensor(f"{pfx}bvr", [128, 516], BF16))
                eq_s = e(nc.sbuf_tensor(f"{pfx}eq", [128, 16 * NH], BF16))
                ek_s = e(nc.sbuf_tensor(f"{pfx}ek", [128, 4 * NKV], BF16))

                def _(gq):
                    gate_g(gq)
                    gq.dma_start(out=eq_s[:], in_=d[f"eq{layer}"]).then_inc(ca[layer], 16)
                    gq.dma_start(out=ek_s[:], in_=d[f"ek{layer}"]).then_inc(ca[layer], 16)
                    if rope:
                        gq.dma_start(out=tabq[:], in_=d["tabq0"]).then_inc(ca[layer], 16)
                    if bias_qkv:
                        gq.dma_start(out=bk_r[:], in_=d["bk1"]).then_inc(ca[layer], 16)
                        gq.dma_start(out=bv_r[:], in_=d["bv1"]).then_inc(ca[layer], 16)
                blk.gpsimd(_)
                CT_A = 16 * ((3 if rope else 2) + (2 if bias_qkv else 0))

                if bias_qkv:
                    def _(s):
                        WG(s, f"{pfx}xcp15")
                        s.dma_start(out=bq_s[:], in_=d["bq1"]).then_inc(b_ld, 16)
                    blk.sync(_)

                pw = ph.enter_context(ExitStack())
                we = pw.enter_context
                wkv = [we(nc.sbuf_tensor(f"{pfx}wkv{i}", [128, 512], BF16))
                       for i in range(16)]
                stg_k = [we(nc.sbuf_tensor(f"{pfx}sk{i}", [128, 512], BF16))
                         for i in range(2)]
                stg_v = [we(nc.sbuf_tensor(f"{pfx}sv{i}", [128, 520], BF16))
                         for i in range(2)]
                ps_kv = [we(nc.psum_tensor(f"{pfx}pkv{i}", [128, 512], F32))
                         for i in range(2)]

                def _(gq):
                    for kc in range(16):
                        gq.dma_start(out=wkv[kc][:],
                                     in_=wkd_[kc]).then_inc(ca[layer], 16)
                blk.gpsimd(_)
                CT_K = CT_A + 256

                def _(t):
                    WV(t, f"{pfx}xh_done")
                    t.wait_ge(ca[layer], CT_K)
                    for tt in range(8):
                        if tt >= 2:
                            if bias_qkv:
                                WV(t, f"{pfx}bk{tt - 2}")
                            else:
                                WA(t, f"{pfx}evk{tt - 2}")
                        ins = None
                        for kc in range(16):
                            ins = t.matmul(
                                out=ps_kv[tt % 2][:, :],
                                lhsT=xh[kc][:, 128 * tt:128 * (tt + 1)],
                                rhs=wkv[kc][:, :],
                                start=(kc == 0), stop=(kc == 15))
                        ins.then_inc(t_kv, 1)
                blk.tensor(_)

                SH0 = [tot["sh"][0], tot["sh"][1]]

                def _(sc):
                    for tt in range(8):
                        sc.wait_ge(t_kv, tt + 1)
                        if tt >= 2:
                            sc.wait_ge(sh_a if tt % 2 == 0 else sh_b,
                                       SH0[tt % 2] + 16 * (tt // 2))
                        sc.copy(out=stg_k[tt % 2][:], in_=ps_kv[tt % 2][:, :])
                        MKA(f"{pfx}evk{tt}")
                blk.scalar(_)

                if bias_qkv:
                    def _(v):
                        v.wait_ge(ca[layer], CT_K)
                        for tt in range(8):
                            WA(v, f"{pfx}evk{tt}")
                            v.tensor_tensor(out=stg_k[tt % 2][:],
                                            in0=stg_k[tt % 2][:],
                                            in1=bk_r[:], op=ALU.add)
                            MKV(f"{pfx}bk{tt}")
                    blk.vector(_)

                def _(gq):
                    gq.wait_ge(s_ld, 32)
                    for tt in range(8):
                        if bias_qkv:
                            WV(gq, f"{pfx}bk{tt}")
                        else:
                            WA(gq, f"{pfx}evk{tt}")
                        gq.indirect_dma_start(
                            out=shk,
                            out_offset=bass.IndirectOffsetOnAxis(
                                ap=sidx_s[:, tt:tt + 1], axis=0),
                            in_=stg_k[tt % 2][:],
                            in_offset=None).then_inc(
                                sh_a if tt % 2 == 0 else sh_b, 16)
                blk.gpsimd(_)
                tot["sh"][0] += 16 * 4
                tot["sh"][1] += 16 * 4
                SHK = [tot["sh"][0], tot["sh"][1]]

                def _(gq):
                    gq.wait_ge(t_kv, 8)
                    for kc in range(16):
                        gq.dma_start(out=wkv[kc][:],
                                     in_=wvd_[kc]).then_inc(cv[layer], 16)
                blk.gpsimd(_)
                CT_V = 256

                def _(v):
                    WV(v, "ones")
                    v.memset(stg_v[0][:], 1.0)
                    v.memset(stg_v[1][:], 1.0)
                    MKV(f"{pfx}vst1")
                blk.vector(_)

                def _(t):
                    t.wait_ge(cv[layer], CT_V)
                    for tt in range(8):
                        if tt >= 2:
                            if bias_qkv:
                                WV(t, f"{pfx}bv{tt - 2}")
                            else:
                                WA(t, f"{pfx}evv{tt - 2}")
                        ins = None
                        for kc in range(16):
                            ins = t.matmul(
                                out=ps_kv[tt % 2][:, :],
                                lhsT=xh[kc][:, 128 * tt:128 * (tt + 1)],
                                rhs=wkv[kc][:, :],
                                start=(kc == 0), stop=(kc == 15))
                        ins.then_inc(t_kv, 1)
                blk.tensor(_)

                def _(sc):
                    WV(sc, f"{pfx}vst1")
                    for tt in range(8):
                        sc.wait_ge(t_kv, 8 + tt + 1)
                        if tt >= 2:
                            sc.wait_ge(sh_a if tt % 2 == 0 else sh_b,
                                       SHK[tt % 2] + 16 * (tt // 2))
                        src = ps_kv[tt % 2][:, :].rearrange(
                            "p (g dd) -> p g dd", g=NKV)
                        if HD == 64:
                            dst = stg_v[tt % 2][:, 0:NKV * VG].rearrange(
                                "p (g w) -> p g w", g=NKV)[:, :, 0:64]
                            sc.copy(out=dst, in_=src)
                        else:
                            src2 = ps_kv[tt % 2][:, :].rearrange(
                                "p (g s2 dd) -> p g s2 dd", g=NKV, s2=2)
                            gv = stg_v[tt % 2][:, 0:NKV * VG].rearrange(
                                "p (g w) -> p g w", g=NKV)
                            sc.copy(out=gv[:, :, 0:64], in_=src2[:, :, 0, :])
                            sc.copy(out=gv[:, :, 65:129], in_=src2[:, :, 1, :])
                        MKA(f"{pfx}evv{tt}")
                blk.scalar(_)

                if bias_qkv:
                    def _(v):
                        for tt in range(8):
                            WA(v, f"{pfx}evv{tt}")
                            v.tensor_tensor(out=stg_v[tt % 2][:, 0:NKV * VG],
                                            in0=stg_v[tt % 2][:, 0:NKV * VG],
                                            in1=bv_r[:], op=ALU.add)
                            MKV(f"{pfx}bv{tt}")
                    blk.vector(_)

                def _(gq):
                    for tt in range(8):
                        if bias_qkv:
                            WV(gq, f"{pfx}bv{tt}")
                        else:
                            WA(gq, f"{pfx}evv{tt}")
                        gq.indirect_dma_start(
                            out=shv,
                            out_offset=bass.IndirectOffsetOnAxis(
                                ap=sidx_s[:, tt:tt + 1], axis=0),
                            in_=stg_v[tt % 2][:],
                            in_offset=None).then_inc(
                                sh_a if tt % 2 == 0 else sh_b, 16)
                    gq.wait_ge(sh_a, SHK[0] + 16 * 4)
                    gq.wait_ge(sh_b, SHK[1] + 16 * 4)
                    gq.wait_ge(bz, 16)
                    gq.collective_compute(
                        "AllReduce", ALU.add,
                        replica_groups=[[0, 1], [2, 3], [4, 5], [6, 7]],
                        ins=[bar], outs=[bar]).then_inc(cc_c, 1)
                blk.gpsimd(_)
                tot["sh"][0] += 16 * 4
                tot["sh"][1] += 16 * 4
                tot["cc_c"] += 1
                AR1 = tot["cc_c"]

                pw.close()
                kbig = e(nc.sbuf_tensor(f"{pfx}kb", [128, 8192], BF16))
                if rope:
                    ptk = ph.enter_context(ExitStack())
                    tabk = ptk.enter_context(
                        nc.sbuf_tensor(f"{pfx}tk", [128, 4096], BF16))

                def kb(r0, r1, kt, c0, c1):
                    return kbig[r0:r1, 2048 * kt + c0:2048 * kt + c1]

                if rope:
                    def _(gq):
                        gq.dma_start(out=tabk[:],
                                     in_=d["tabk0"]).then_inc(ca[layer], 16)
                    blk.gpsimd(_)
                CT_K2 = CT_K + (16 if rope else 0)

                RB0 = tot["rb"]

                def _(s):
                    s.wait_ge(cc_c, AR1)
                    for j in range(16):
                        s.dma_start(out=vsb[j][:, 0:NKV * VG],
                                    in_=shv[128 * j:128 * (j + 1),
                                            0:NKV * VG]).then_inc(rb_s, 16)
                    s.wait_ge(rb_s, RB0 + 256)
                    s.wait_ge(x_ld, 256)
                    s.wait_ge(s_ld, 32)
                    dst = kbig[:].rearrange("p (m f) -> p m f", m=4)
                    s.dma_start(out=dst, in_=shk,
                                transpose=True).then_inc(rb_s, 16)
                    s.wait_ge(rb_s, RB0 + 272)
                blk.sync(_)
                tot["rb"] += 272
                RB_T = tot["rb"]

                with ExitStack() as pq:
                    qe = pq.enter_context
                    ps_p = [qe(nc.psum_tensor(f"{pfx}pp{i}", [128, TOKQ], F32))
                            for i in range(2)]
                    WLQ = [tot["wl"][0], tot["wl"][1]]

                    def _(gq):
                        for i in range(16):
                            if i >= 2:
                                gq.wait_ge(t_q, i - 1)
                            gq.dma_start(out=wbuf[i % 2][:],
                                         in_=wqd_[i]).then_inc(wl[i % 2], 16)
                    blk.gpsimd(_)
                    tot["wl"][0] += 16 * 8
                    tot["wl"][1] += 16 * 8

                    def _(t):
                        for mt in range(16):
                            t.wait_ge(wl[mt % 2],
                                      WLQ[mt % 2] + 16 * (mt // 2 + 1))
                            if mt >= 2:
                                if rope:
                                    WV(t, f"{pfx}ropeq{mt - 2}")
                                else:
                                    WA(t, f"{pfx}evq{mt - 2}")
                            ins = None
                            for kc in range(16):
                                for sgm in range(2):
                                    ins = t.matmul(
                                        out=ps_p[mt % 2][:, 512 * sgm:512 * (sgm + 1)],
                                        lhsT=wbuf[mt % 2][:, 128 * kc:128 * (kc + 1)],
                                        rhs=xh[kc][:, 512 * sgm:512 * (sgm + 1)],
                                        start=(kc == 0), stop=(kc == 15))
                            ins.then_inc(t_q, 1)
                    blk.tensor(_)

                    def _(sc):
                        if bias_qkv:
                            sc.wait_ge(b_ld, 16)
                        for mt in range(16):
                            sc.wait_ge(t_q, mt + 1)
                            if rope:
                                if mt >= 2:
                                    WV(sc, f"{pfx}ropeq{mt - 2}")
                                p = ps_p[mt % 2]
                                r1 = rp[mt % 2]
                                sc.copy(out=qsb[mt][:], in_=p[:, :])
                                sc.copy(out=r1[0:32, 0:TOKQ], in_=p[32:64, :])
                                sc.copy(out=r1[32:64, 0:TOKQ], in_=p[0:32, :])
                                sc.copy(out=r1[64:96, 0:TOKQ], in_=p[96:128, :])
                                sc.copy(out=r1[96:128, 0:TOKQ], in_=p[64:96, :])
                            else:
                                sc.activation(out=qsb[mt][:],
                                              in_=ps_p[mt % 2][:, :],
                                              func=AF.Identity,
                                              bias=bq_s[:, mt:mt + 1])
                            MKA(f"{pfx}evq{mt}")
                    blk.scalar(_)

                    if rope:
                        def _(v):
                            v.wait_ge(ca[layer], CT_K)
                            for mt in range(16):
                                WA(v, f"{pfx}evq{mt}")
                                r1 = rp[mt % 2]
                                v.tensor_tensor(out=qsb[mt][:], in0=qsb[mt][:],
                                                in1=tabq[:, 0:1024],
                                                op=ALU.mult)
                                v.tensor_tensor(out=r1[:, 0:TOKQ],
                                                in0=r1[:, 0:TOKQ],
                                                in1=tabq[:, 1024:2048],
                                                op=ALU.mult)
                                v.tensor_tensor(out=qsb[mt][:], in0=qsb[mt][:],
                                                in1=r1[:, 0:TOKQ], op=ALU.add)
                                MKV(f"{pfx}ropeq{mt}")
                        blk.vector(_)

                if rope:
                    def _(sc):
                        sc.wait_ge(rb_s, RB_T)
                        for j in range(4):
                            if j >= 2:
                                WV(sc, f"{pfx}ropek{j - 2}")
                            else:
                                WV(sc, f"{pfx}ropeq{14 + j}")
                            r1 = rp[j % 2]
                            sc.copy(out=r1[0:32, :], in_=kb(32, 64, j, 0, 2048))
                            sc.copy(out=r1[32:64, :], in_=kb(0, 32, j, 0, 2048))
                            sc.copy(out=r1[64:96, :], in_=kb(96, 128, j, 0, 2048))
                            sc.copy(out=r1[96:128, :], in_=kb(64, 96, j, 0, 2048))
                            MKA(f"{pfx}swk{j}")
                    blk.scalar(_)

                    def _(v):
                        v.wait_ge(ca[layer], CT_K2)
                        for j in range(4):
                            WA(v, f"{pfx}swk{j}")
                            r1 = rp[j % 2]
                            kj = kb(0, 128, j, 0, 2048)
                            v.tensor_tensor(out=kj, in0=kj,
                                            in1=tabk[:, 0:2048], op=ALU.mult)
                            v.tensor_tensor(out=r1[:, :], in0=r1[:, :],
                                            in1=tabk[:, 2048:4096],
                                            op=ALU.mult)
                            v.tensor_tensor(out=kj, in0=kj, in1=r1[:, :],
                                            op=ALU.add)
                            MKV(f"{pfx}ropek{j}")
                    blk.vector(_)
                    ptk.close()

                with ExitStack() as pp:
                    ee = pp.enter_context
                    ps_n = ee(nc.psum_tensor(f"{pfx}psn", [NH, TOKQ], F32))
                    ps_nk = ee(nc.psum_tensor(f"{pfx}psk", [NKV, 2048], F32))
                    ps_tr = ee(nc.psum_tensor(f"{pfx}ptr", [1, 128], F32))
                    ps_bb = ee(nc.psum_tensor(f"{pfx}pbb", [128, 1], F32))
                    nb = ee(nc.sbuf_tensor(f"{pfx}nb", [NH, 2], F32))
                    nbk = ee(nc.sbuf_tensor(f"{pfx}nbk", [NKV, 2], F32))
                    nbs = ee(nc.sbuf_tensor(f"{pfx}nbs", [1, 8], F32))
                    nb_bf = ee(nc.sbuf_tensor(f"{pfx}nbbf", [1, 1], BF16))

                    def _(sc):
                        if rope:
                            WV(sc, f"{pfx}ropek3")
                        else:
                            sc.wait_ge(rb_s, RB_T)
                        for mt in range(16):
                            if mt >= 2:
                                sc.wait_ge(t_nb, mt - 1)
                            sc.activation(out=rp[mt % 2][:, 0:TOKQ],
                                          in_=qsb[mt][:], func=AF.Square)
                            MKA(f"{pfx}nsq{mt}")
                        for j in range(4):
                            sc.wait_ge(t_nb, 15 + j)
                            sc.activation(out=rp[j % 2][:, :],
                                          in_=kb(0, 128, j, 0, 2048),
                                          func=AF.Square)
                            MKA(f"{pfx}nsk{j}")
                    blk.scalar(_)

                    def _(t):
                        for mt in range(16):
                            WA(t, f"{pfx}nsq{mt}")
                            ins = None
                            for sgm in range(2):
                                ins = t.matmul(
                                    out=ps_n[:, 512 * sgm:512 * (sgm + 1)],
                                    lhsT=eq_s[:, NH * mt:NH * (mt + 1)],
                                    rhs=rp[mt % 2][:, 512 * sgm:512 * (sgm + 1)],
                                    start=(mt == 0), stop=(mt == 15))
                            ins.then_inc(t_nb, 1)
                        for j in range(4):
                            WA(t, f"{pfx}nsk{j}")
                            ins = None
                            for sgm in range(4):
                                ins = t.matmul(
                                    out=ps_nk[:, 512 * sgm:512 * (sgm + 1)],
                                    lhsT=ek_s[:, NKV * j:NKV * (j + 1)],
                                    rhs=rp[j % 2][:, 512 * sgm:512 * (sgm + 1)],
                                    start=(j == 0), stop=(j == 3))
                            ins.then_inc(t_nb, 1)
                    blk.tensor(_)

                    def _(v):
                        v.wait_ge(t_nb, 20)
                        v.tensor_reduce(out=nb[:, 0:1], in_=ps_n[:, :],
                                        axis=AX.X, op=ALU.max)
                        v.tensor_reduce(out=nbk[:, 0:1], in_=ps_nk[:, :],
                                        axis=AX.X, op=ALU.max)
                        MKV(f"{pfx}nred1")
                    blk.vector(_)

                    def _(t):
                        WV(t, f"{pfx}nred1")
                        t.wait_ge(s_ld, 32)
                        t.transpose(out=ps_tr[0:1, 0:NH], in_=nb[:, 0:1],
                                    identity=id_s[0:NH, 0:NH])
                        t.transpose(out=ps_tr[0:1, 64:64 + NKV], in_=nbk[:, 0:1],
                                    identity=id_s[0:NKV, 0:NKV]).then_inc(t_nb, 1)
                    blk.tensor(_)

                    def _(v):
                        v.wait_ge(t_nb, 21)
                        v.tensor_reduce(out=nbs[:, 0:1], in_=ps_tr[0:1, 0:NH],
                                        axis=AX.X, op=ALU.max)
                        v.tensor_reduce(out=nbs[:, 1:2],
                                        in_=ps_tr[0:1, 64:64 + NKV],
                                        axis=AX.X, op=ALU.max)
                        v.tensor_tensor(out=nbs[:, 2:3], in0=nbs[:, 0:1],
                                        in1=nbs[:, 1:2], op=ALU.mult)
                        MKV(f"{pfx}nred2")
                    blk.vector(_)

                    def _(sc):
                        WV(sc, f"{pfx}nred2")
                        sc.activation(out=nbs[:, 3:4], in_=nbs[:, 2:3],
                                      func=AF.Ln)
                        sc.activation(out=nbs[:, 4:5], in_=nbs[:, 3:4],
                                      func=AF.Exp, scale=0.5)
                        MKA(f"{pfx}nsqrt")
                    blk.scalar(_)

                    def _(v):
                        WA(v, f"{pfx}nsqrt")
                        v.tensor_scalar_mul(out=nbs[:, 5:6], in0=nbs[:, 4:5],
                                            scalar1=-scale)
                        v.tensor_copy(out=nb_bf[:, :], in_=nbs[:, 5:6])
                        MKV(f"{pfx}nbf")
                    blk.vector(_)

                    def _(t):
                        WV(t, f"{pfx}nbf")
                        t.matmul(out=ps_bb[:, 0:1], lhsT=ones_r[0:1, :],
                                 rhs=nb_bf[0:1, 0:1], start=True,
                                 stop=True).then_inc(t_nb, 1)
                    blk.tensor(_)

                    def _(sc):
                        sc.wait_ge(t_nb, 22)
                        sc.copy(out=bias_sb[:, :], in_=ps_bb[:, :])
                        MKA(f"{pfx}bias")
                    blk.scalar(_)

                of = qsb
                with ExitStack() as pp:
                    ee = pp.enter_context
                    pt = x2b
                    rc_b = ee(nc.sbuf_tensor(f"{pfx}rcb", [1, TOKQ], BF16))
                    bb_s = ee(nc.sbuf_tensor(f"{pfx}bbs", [HD, TOKQ], BF16))
                    ns_s = 2 if layer == 0 else 1
                    ps_s = [ee(nc.psum_tensor(f"{pfx}pss{i}", [128, TOKQ], F32))
                            for i in range(ns_s)]
                    if layer == 0:
                        ps_o = ee(nc.psum_tensor(f"{pfx}pso", [HD + 1, TOKQ], F32))
                    else:
                        ps_o1 = ee(nc.psum_tensor(f"{pfx}po1", [65, TOKQ], F32))
                        ps_o2 = ee(nc.psum_tensor(f"{pfx}po2", [64, TOKQ], F32))
                    ps_b = ee(nc.psum_tensor(f"{pfx}psb", [HD, TOKQ], F32))

                    def segs_for(s0):
                        a = 128 * s0
                        if a < 512:
                            return [(a, 512), (512, 1024)]
                        return [(a, 1024)]

                    def _(t):
                        WA(t, f"{pfx}bias")
                        for hs in range(NH):
                            h_orig = stor_heads[hs]
                            g = h_orig // (NH // NKV)
                            if HD == 64:
                                qt, qr = hs // 2, 64 * (hs % 2)
                            else:
                                qt, qr = hs, 0
                            kt, kr = (g * HD) // 128, (g * HD) % 128

                            def emit_S(c):
                                s0 = c // 2
                                idx = 16 * hs + c
                                if idx >= ns_s:
                                    WA(t, f"{pfx}exp{idx - ns_s}")
                                ins = None
                                for (a, b) in segs_for(s0):
                                    ins = t.matmul(
                                        out=ps_s[idx % ns_s][:, a:b],
                                        lhsT=kb(kr, kr + HD, kt,
                                                128 * c, 128 * (c + 1)),
                                        rhs=qsb[qt][qr:qr + HD, a:b],
                                        start=True, stop=True)
                                ins.then_inc(t_s, 1)

                            def emit_AV(c):
                                s0 = c // 2
                                idx = 16 * hs + c
                                WV(t, f"{pfx}msk{idx}")
                                if c == 0 and hs >= 1:
                                    WV(t, f"{pfx}epi{hs - 1}")
                                ins = None
                                for (a, b) in segs_for(s0):
                                    if layer == 0:
                                        ins = t.matmul(
                                            out=ps_o[:, a:b],
                                            lhsT=vsb[c][:, VG * g:VG * g + 65],
                                            rhs=pt[idx % 2][:, a:b],
                                            start=(c == 0), stop=False)
                                    else:
                                        t.matmul(
                                            out=ps_o1[:, a:b],
                                            lhsT=vsb[c][:, VG * g:VG * g + 65],
                                            rhs=pt[idx % 2][:, a:b],
                                            start=(c == 0), stop=False)
                                        ins = t.matmul(
                                            out=ps_o2[:, a:b],
                                            lhsT=vsb[c][:, VG * g + 65:VG * g + 129],
                                            rhs=pt[idx % 2][:, a:b],
                                            start=(c == 0), stop=False)
                                if c != 15:
                                    ins.then_inc(t_av, 1)

                            def emit_close():
                                ins = None
                                for sgm in range(2):
                                    cols = slice(512 * sgm, 512 * (sgm + 1))
                                    if layer == 0:
                                        ins = t.matmul(out=ps_o[:, cols],
                                                       lhsT=zrow[0:1, 0:65],
                                                       rhs=abc[0:1, 0:512],
                                                       start=False, stop=True)
                                    else:
                                        t.matmul(out=ps_o1[:, cols],
                                                 lhsT=zrow[0:1, 0:65],
                                                 rhs=abc[0:1, 0:512],
                                                 start=False, stop=True)
                                        ins = t.matmul(out=ps_o2[:, cols],
                                                       lhsT=zrow[0:1, 0:64],
                                                       rhs=abc[0:1, 0:512],
                                                       start=False, stop=True)
                                ins.then_inc(t_av, 1)

                            if ns_s == 2:
                                emit_S(0)
                                for c in range(1, 16):
                                    emit_S(c)
                                    emit_AV(c - 1)
                                emit_AV(15)
                            else:
                                for c in range(16):
                                    emit_S(c)
                                    emit_AV(c)
                            emit_close()
                            WV(t, f"{pfx}rc{hs}")
                            if hs >= 1:
                                WA(t, f"{pfx}bbe{hs - 1}")
                            ins = None
                            for sgm in range(2):
                                ins = t.matmul(
                                    out=ps_b[:, 512 * sgm:512 * (sgm + 1)],
                                    lhsT=ones_r[0:1, 0:HD],
                                    rhs=rc_b[0:1, 512 * sgm:512 * (sgm + 1)],
                                    start=True, stop=True)
                            ins.then_inc(t_bc, 1)
                    blk.tensor(_)

                    def _(sc):
                        for hs in range(NH):
                            for c in range(16):
                                s0 = c // 2
                                idx = 16 * hs + c
                                sc.wait_ge(t_s, idx + 1)
                                if idx >= 2:
                                    sc.wait_ge(t_av, idx - 1)
                                sc.activation(out=pt[idx % 2][:, 128 * s0:1024],
                                              in_=ps_s[idx % ns_s][:, 128 * s0:1024],
                                              func=AF.Exp, bias=bias_sb[:, 0:1],
                                              scale=scale)
                                MKA(f"{pfx}exp{idx}")
                            sc.wait_ge(t_bc, hs + 1)
                            sc.copy(out=bb_s[:, :], in_=ps_b[:, :])
                            MKA(f"{pfx}bbe{hs}")
                    blk.scalar(_)

                    def _(v):
                        v.wait_ge(m_ld, 32)
                        for hs in range(NH):
                            for c in range(16):
                                s0 = c // 2
                                idx = 16 * hs + c
                                WA(v, f"{pfx}exp{idx}")
                                v.tensor_tensor(
                                    out=pt[idx % 2][:, 128 * s0:128 * (s0 + 1)],
                                    in0=pt[idx % 2][:, 128 * s0:128 * (s0 + 1)],
                                    in1=mkc[c][:, :], op=ALU.mult)
                                MKV(f"{pfx}msk{idx}")
                            v.wait_ge(t_av, 16 * (hs + 1))
                            with nc.allow_low_precision(reason="1/sum -> bf16"):
                                v.reciprocal(out=rc_b[:, :],
                                             in_=(ps_o if layer == 0
                                                  else ps_o1)[64:65, :])
                            MKV(f"{pfx}rc{hs}")
                            WA(v, f"{pfx}bbe{hs}")
                            if layer == 0:
                                qt, qr = hs // 2, 64 * (hs % 2)
                                v.tensor_tensor(out=of[qt][qr:qr + HD, :],
                                                in0=ps_o[0:64, :],
                                                in1=bb_s[:, :], op=ALU.mult)
                            else:
                                v.tensor_tensor(out=of[hs][0:64, :],
                                                in0=ps_o1[0:64, :],
                                                in1=bb_s[0:64, :], op=ALU.mult)
                                v.tensor_tensor(out=of[hs][64:128, :],
                                                in0=ps_o2[0:64, :],
                                                in1=bb_s[64:128, :],
                                                op=ALU.mult)
                            MKV(f"{pfx}epi{hs}")
                    blk.vector(_)

                if outproj:
                    with ExitStack() as pp:
                        ee = pp.enter_context
                        ps_5 = [ee(nc.psum_tensor(f"{pfx}p5{i}", [128, TOKQ], F32))
                                for i in range(2)]
                        WL5 = [tot["wl"][2], tot["wl"][3]]

                        def _(gq):
                            WA(gq, f"{pfx}bias")
                            for mt in range(16):
                                if mt >= 2:
                                    gq.wait_ge(t_p5, mt - 1)
                                gq.dma_start(out=wbuf[mt % 2][:],
                                             in_=wod_[mt]).then_inc(
                                                 wl[2 + mt % 2], 16)
                        blk.gpsimd(_)
                        tot["wl"][2] += 16 * 8
                        tot["wl"][3] += 16 * 8

                        def _(t):
                            WV(t, f"{pfx}epi{NH - 1}")
                            for mt in range(16):
                                t.wait_ge(wl[2 + mt % 2],
                                          WL5[mt % 2] + 16 * (mt // 2 + 1))
                                if mt >= 2:
                                    WV(t, f"{pfx}res{mt - 2}")
                                ins = None
                                for kc in range(16):
                                    for sgm in range(2):
                                        ins = t.matmul(
                                            out=ps_5[mt % 2][:, 512 * sgm:512 * (sgm + 1)],
                                            lhsT=wbuf[mt % 2][:, 128 * kc:128 * (kc + 1)],
                                            rhs=of[kc][:, 512 * sgm:512 * (sgm + 1)],
                                            start=(kc == 0), stop=(kc == 15))
                                ins.then_inc(t_p5, 1)
                        blk.tensor(_)

                        def _(v):
                            v.wait_ge(x_ld, 256)
                            for mt in range(16):
                                v.wait_ge(t_p5, mt + 1)
                                v.tensor_tensor(out=x_f[mt][:], in0=x_f[mt][:],
                                                in1=ps_5[mt % 2][:, :],
                                                op=ALU.add)
                                MKV(f"{pfx}res{mt}")
                        blk.vector(_)
                else:
                    def _(v):
                        WV(v, f"{pfx}epi{NH - 1}")
                        for mt in range(16):
                            v.tensor_tensor(out=x_f[mt][:], in0=x_f[mt][:],
                                            in1=of[mt][:], op=ALU.add)
                            MKV(f"{pfx}res{mt}")
                    blk.vector(_)

        def emit_ffn0(pfx):
            with ExitStack() as ph:
                e = ph.enter_context
                wb = [e(nc.sbuf_tensor(f"{pfx}wb{i}", [128, 2048], BF16))
                      for i in range(4)]
                hb = [e(nc.sbuf_tensor(f"{pfx}hb{i}", [128, TOKQ], BF16))
                      for i in range(8)]
                db = [e(nc.sbuf_tensor(f"{pfx}db{i}", [128, 2048], BF16))
                      for i in range(8)]
                sg_s = e(nc.sbuf_tensor(f"{pfx}sgs", [128, 64], F32))
                su_s = e(nc.sbuf_tensor(f"{pfx}sus", [128, 64], F32))
                sd_s = e(nc.sbuf_tensor(f"{pfx}sds", [128, 16], F32))
                gb = e(nc.sbuf_tensor(f"{pfx}gb", [128, TOKQ], BF16))
                ub = e(nc.sbuf_tensor(f"{pfx}ub", [128, TOKQ], BF16))
                ps_g = e(nc.psum_tensor(f"{pfx}psg", [128, TOKQ], F32))
                ps_u = e(nc.psum_tensor(f"{pfx}psu", [128, TOKQ], F32))
                ps_d = [e(nc.psum_tensor(f"{pfx}psd{i}", [128, TOKQ], F32))
                        for i in range(2)]

                def _(s):
                    WV(s, "a0.res15")
                    s.dma_start(out=sg_s[:], in_=d["sg"]).then_inc(f0c, 16)
                    s.dma_start(out=su_s[:], in_=d["su"]).then_inc(f0c, 16)
                    s.dma_start(out=sd_s[:], in_=d["sd"]).then_inc(f0c, 16)
                blk.sync(_)

                def gu_src(i):
                    return (d["wu"] if i % 2 else d["wg"])[i // 2]

                WLF = [tot["wl"][j] for j in range(4)]
                D0 = tot["d"]

                def _(gq):
                    WV(gq, "a0.res15")
                    for i in range(4):
                        gq.dma_start(out=wb[i][:], in_=gu_src(i)).then_inc(
                            wl[i], 16)
                    for c in range(8):
                        gq.dma_start(out=db[c][:], in_=d["wd"][c]).then_inc(
                            d_lda, 16)
                    for r in range(8):
                        for m in range(8):
                            mt = 8 * r + m
                            for i in (2 * mt, 2 * mt + 1):
                                gq.wait_ge(wl[i % 4],
                                           WLF[i % 4] + 16 * (i // 4 + 1))
                                gq.tensor_scalar_add(out=wb[i % 4][:],
                                                     in0=wb[i % 4][:],
                                                     scalar1=-128.0)
                                MKG(f"{pfx}wsub{i}")
                                if i + 4 < 128:
                                    gq.wait_ge(t_mm, max(0, i + 1))
                                    gq.dma_start(out=wb[(i + 4) % 4][:],
                                                 in_=gu_src(i + 4)).then_inc(
                                                     wl[(i + 4) % 4], 16)
                        gq.wait_ge(d_lda, D0 + 16 * 8 * (r + 1))
                        for c in range(8):
                            gq.tensor_scalar_add(out=db[c][:], in0=db[c][:],
                                                 scalar1=-128.0)
                        MKG(f"{pfx}dsub{r}")
                        if r + 1 < 8:
                            gq.wait_ge(t_dn, 16 * (r + 1))
                            for c in range(8):
                                gq.dma_start(out=db[c][:],
                                             in_=d["wd"][8 * (r + 1) + c]
                                             ).then_inc(d_lda, 16)
                blk.gpsimd(_)
                for i in range(4):
                    tot["wl"][i] += 16 * 32
                tot["d"] += 16 * 64

                def _(t):
                    WV(t, f"{pfx}xh_done")
                    for r in range(8):
                        for m in range(8):
                            mt = 8 * r + m
                            for i in (2 * mt, 2 * mt + 1):
                                WG(t, f"{pfx}wsub{i}")
                                is_u = i % 2
                                if mt >= 1:
                                    WV(t, f"{pfx}usc{mt - 1}" if is_u
                                       else f"{pfx}gsc{mt - 1}")
                                ps = ps_u if is_u else ps_g
                                ins = None
                                for kc in range(16):
                                    for sgm in range(2):
                                        ins = t.matmul(
                                            out=ps[:, 512 * sgm:512 * (sgm + 1)],
                                            lhsT=wb[i % 4][:, 128 * kc:128 * (kc + 1)],
                                            rhs=xh[kc][:, 512 * sgm:512 * (sgm + 1)],
                                            start=(kc == 0), stop=(kc == 15))
                                ins.then_inc(t_mm, 1)
                        WG(t, f"{pfx}dsub{r}")
                        WV(t, f"{pfx}h{8 * r + 7}")
                        for ht in range(16):
                            if 16 * r + ht >= 2:
                                WV(t, f"{pfx}dev{16 * r + ht - 2}")
                            ins = None
                            for c in range(8):
                                for sgm in range(2):
                                    ins = t.matmul(
                                        out=ps_d[ht % 2][:, 512 * sgm:512 * (sgm + 1)],
                                        lhsT=db[c][:, 128 * ht:128 * (ht + 1)],
                                        rhs=hb[c][:, 512 * sgm:512 * (sgm + 1)],
                                        start=(c == 0), stop=(c == 7))
                            ins.then_inc(t_dn, 1)
                blk.tensor(_)

                def _(sc):
                    for mt in range(64):
                        WV(sc, f"{pfx}gsc{mt}")
                        sc.activation(out=gb[:], in_=gb[:], func=AF.Silu)
                        MKA(f"{pfx}silu{mt}")
                blk.scalar(_)

                def _(v):
                    v.wait_ge(f0c, 48)
                    for r in range(8):
                        for m in range(8):
                            mt = 8 * r + m
                            if mt >= 1:
                                WA(v, f"{pfx}silu{mt - 1}")
                            v.wait_ge(t_mm, 2 * mt + 1)
                            v.tensor_scalar_mul(out=gb[:], in0=ps_g[:, :],
                                                scalar1=sg_s[:, mt:mt + 1])
                            MKV(f"{pfx}gsc{mt}")
                            v.wait_ge(t_mm, 2 * mt + 2)
                            v.tensor_scalar_mul(out=ub[:], in0=ps_u[:, :],
                                                scalar1=su_s[:, mt:mt + 1])
                            MKV(f"{pfx}usc{mt}")
                            WA(v, f"{pfx}silu{mt}")
                            v.tensor_tensor(out=hb[m][:], in0=gb[:], in1=ub[:],
                                            op=ALU.mult)
                            MKV(f"{pfx}h{mt}")
                        for ht in range(16):
                            v.wait_ge(t_dn, 16 * r + ht + 1)
                            v.scalar_tensor_tensor(out=x_f[ht][:],
                                                   in0=ps_d[ht % 2][:, :],
                                                   scalar=sd_s[:, ht:ht + 1],
                                                   in1=x_f[ht][:],
                                                   op0=ALU.mult, op1=ALU.add)
                            MKV(f"{pfx}dev{16 * r + ht}")
                blk.vector(_)

        def emit_ffn1(pfx):
            with ExitStack() as ph:
                e = ph.enter_context
                wq = [e(nc.sbuf_tensor(f"{pfx}wq{i}", [128, 2048], FP8))
                      for i in range(16)]
                xq = [e(nc.sbuf_tensor(f"{pfx}xq{i}", [128, TOKQ], FP8))
                      for i in range(16)]
                wstg = [e(nc.sbuf_tensor(f"{pfx}ws{i}", [128, 2048], BF16))
                        for i in range(2)]
                bf_s = e(nc.sbuf_tensor(f"{pfx}bfs", [128, 16], BF16))
                bf_f = e(nc.sbuf_tensor(f"{pfx}bff", [128, 16], F32))
                red = e(nc.sbuf_tensor(f"{pfx}red", [128, 4], F32))
                sca = e(nc.sbuf_tensor(f"{pfx}sca", [1, 8], F32))
                sc_bf = e(nc.sbuf_tensor(f"{pfx}scbf", [1, 4], BF16))
                sc3 = e(nc.sbuf_tensor(f"{pfx}sc3", [128, 4], F32))
                ybf = e(nc.sbuf_tensor(f"{pfx}ybf", [128, TOKQ], BF16))
                ps_m = [e(nc.psum_tensor(f"{pfx}psm{i}", [128, TOKQ], F32))
                        for i in range(2)]
                ps_t = e(nc.psum_tensor(f"{pfx}pst", [1, 128], F32))
                ps_t2 = e(nc.psum_tensor(f"{pfx}pst2", [1, 128], F32))
                ps_c = e(nc.psum_tensor(f"{pfx}psc", [128, 4], F32))

                def _(s):
                    WV(s, "a1.res15")
                    s.dma_start(out=bf_s[:], in_=d["bfb"]).then_inc(f1c, 16)
                blk.sync(_)

                WLF = [tot["wl"][0], tot["wl"][1]]

                def _(gq):
                    WV(gq, "a1.res15")
                    for i in range(16):
                        if i >= 2:
                            WV(gq, f"{pfx}wmax{i - 2}")
                        gq.dma_start(out=wstg[i % 2][:], in_=d["wf"][i]
                                     ).then_inc(wl[i % 2], 16)
                blk.gpsimd(_)
                tot["wl"][0] += 16 * 8
                tot["wl"][1] += 16 * 8

                def _(v):
                    for i in range(16):
                        v.wait_ge(wl[i % 2], WLF[i % 2] + 16 * (i // 2 + 1))
                        v.tensor_reduce(out=red[:, 0:1], in_=wstg[i % 2][:],
                                        axis=AX.X, op=ALU.max,
                                        apply_absolute_value=True)
                        if i == 0:
                            v.tensor_copy(out=red[:, 1:2], in_=red[:, 0:1])
                        else:
                            v.tensor_tensor(out=red[:, 1:2], in0=red[:, 1:2],
                                            in1=red[:, 0:1], op=ALU.max)
                        MKV(f"{pfx}wmax{i}")
                    WV(v, f"{pfx}xh_done")
                    for kc in range(16):
                        v.tensor_reduce(out=red[:, 2:3], in_=xh[kc][:],
                                        axis=AX.X, op=ALU.max,
                                        apply_absolute_value=True)
                        if kc == 0:
                            v.tensor_copy(out=red[:, 3:4], in_=red[:, 2:3])
                        else:
                            v.tensor_tensor(out=red[:, 3:4], in0=red[:, 3:4],
                                            in1=red[:, 2:3], op=ALU.max)
                    MKV(f"{pfx}xmax")
                blk.vector(_)

                def _(t):
                    WV(t, f"{pfx}xmax")
                    t.wait_ge(s_ld, 32)
                    t.transpose(out=ps_t[0:1, 0:128], in_=red[:, 3:4],
                                identity=id_s[:, :]).then_inc(t_f1, 1)
                    t.transpose(out=ps_t2[0:1, 0:128], in_=red[:, 1:2],
                                identity=id_s[:, :]).then_inc(t_f1, 1)
                blk.tensor(_)

                def _(v):
                    v.wait_ge(t_f1, 2)
                    v.tensor_reduce(out=sca[:, 0:1], in_=ps_t[0:1, 0:128],
                                    axis=AX.X, op=ALU.max)
                    v.tensor_scalar_max(out=sca[:, 0:1], in0=sca[:, 0:1],
                                        scalar1=1e-6)
                    v.tensor_reduce(out=sca[:, 2:3], in_=ps_t2[0:1, 0:128],
                                    axis=AX.X, op=ALU.max)
                    v.tensor_scalar_max(out=sca[:, 2:3], in0=sca[:, 2:3],
                                        scalar1=1e-6)
                    MKV(f"{pfx}xsloc")
                blk.vector(_)

                CS0 = tot["cc_s"]
                CC0 = tot["cc_c"]

                def _(gq):
                    WV(gq, f"{pfx}xsloc")
                    gq.dma_start(out=cc_in, in_=sca[:, 0:1]).then_inc(cc_s, 16)
                    gq.wait_ge(cc_s, CS0 + 16)
                    gq.collective_compute(
                        "AllReduce", ALU.max,
                        replica_groups=[[0, 1, 2, 3, 4, 5, 6, 7]],
                        ins=[cc_in], outs=[cc_in]).then_inc(cc_c, 1)
                    gq.wait_ge(cc_c, CC0 + 1)
                    gq.dma_start(out=sca[:, 1:2], in_=cc_in).then_inc(cc_s, 16)
                    for i in range(16):
                        WV(gq, f"{pfx}wq{i - 2}" if i >= 2
                           else f"{pfx}wmax{14 + i}")
                        gq.dma_start(out=wstg[i % 2][:], in_=d["wf"][i]
                                     ).then_inc(wl[i % 2], 16)
                blk.gpsimd(_)
                tot["cc_s"] += 32
                tot["cc_c"] += 1
                CSX = tot["cc_s"]
                CCX = tot["cc_c"]
                WLB = [WLF[0] + 16 * 8, WLF[1] + 16 * 8]
                tot["wl"][0] += 16 * 8
                tot["wl"][1] += 16 * 8

                def _(v):
                    v.wait_ge(cc_s, CSX)
                    v.wait_ge(cc_c, CCX)
                    v.reciprocal(out=sca[:, 3:4], in_=sca[:, 1:2])
                    v.reciprocal(out=sca[:, 4:5], in_=sca[:, 2:3])
                    v.tensor_tensor(out=sca[:, 5:6], in0=sca[:, 1:2],
                                    in1=sca[:, 2:3], op=ALU.mult)
                    v.tensor_copy(out=sc_bf[:, 0:3], in_=sca[:, 3:6])
                    MKV(f"{pfx}scbf")
                blk.vector(_)

                def _(t):
                    WV(t, f"{pfx}scbf")
                    t.matmul(out=ps_c[:, 0:3], lhsT=ones_r[0:1, :],
                             rhs=sc_bf[0:1, 0:3], start=True,
                             stop=True).then_inc(t_f1, 1)
                blk.tensor(_)

                def _(sc):
                    sc.wait_ge(t_f1, 3)
                    sc.copy(out=sc3[:, 0:3], in_=ps_c[:, 0:3])
                    MKA(f"{pfx}sc3")
                    sc.wait_ge(f1c, 16)
                    sc.copy(out=bf_f[:, :], in_=bf_s[:, :])
                    MKA(f"{pfx}bff")
                blk.scalar(_)

                def _(v):
                    WA(v, f"{pfx}sc3")
                    for i in range(16):
                        v.wait_ge(wl[i % 2], WLB[i % 2] + 16 * (i // 2 + 1))
                        v.tensor_scalar_mul(out=wq[i][:], in0=wstg[i % 2][:],
                                            scalar1=sc3[:, 1:2])
                        MKV(f"{pfx}wq{i}")
                    for kc in range(16):
                        v.tensor_scalar_mul(out=xq[kc][:], in0=xh[kc][:],
                                            scalar1=sc3[:, 0:1])
                        MKV(f"{pfx}xq{kc}")
                blk.vector(_)

                def _(t):
                    WV(t, f"{pfx}xq15")
                    for mt in range(16):
                        WV(t, f"{pfx}wq{mt}")
                        if mt >= 2:
                            WV(t, f"{pfx}yev{mt - 2}")
                        ins = None
                        for kc in range(16):
                            for sgm in range(2):
                                ins = t.matmul(
                                    out=ps_m[mt % 2][:, 512 * sgm:512 * (sgm + 1)],
                                    lhsT=wq[mt][:, 128 * kc:128 * (kc + 1)],
                                    rhs=xq[kc][:, 512 * sgm:512 * (sgm + 1)],
                                    start=(kc == 0), stop=(kc == 15))
                        ins.then_inc(t_f1, 1)
                blk.tensor(_)

                def _(v):
                    WA(v, f"{pfx}bff")
                    for mt in range(16):
                        v.wait_ge(t_f1, 3 + mt + 1)
                        v.tensor_scalar_mul(out=ybf[:], in0=ps_m[mt % 2][:, :],
                                            scalar1=sc3[:, 2:3])
                        v.tensor_scalar_add(out=ybf[:], in0=ybf[:],
                                            scalar1=bf_f[:, mt:mt + 1])
                        v.tensor_tensor(out=x_f[mt][:], in0=x_f[mt][:],
                                        in1=ybf[:], op=ALU.add)
                        MKV(f"{pfx}yev{mt}")
                blk.vector(_)

        emit_ln("a0.", lambda sc: sc.wait_ge(xb_ld, 256))
        emit_attn(0, "a0.", lambda gq: None)

        def _(gq):
            for kc in range(16):
                WV(gq, f"a0.res{kc}")
                gq.tensor_copy(out=xh[kc][:], in_=x_f[kc][:])
                MKG(f"f0.xcp{kc}")
        blk.gpsimd(_)

        emit_ln("f0.", lambda sc: WG(sc, "f0.xcp15"))
        emit_ffn0("f0.")

        def _(gq):
            for kc in range(16):
                WV(gq, f"f0.dev{16 * 7 + kc}")
                gq.tensor_copy(out=xh[kc][:], in_=x_f[kc][:])
                MKG(f"a1.xcp{kc}")
        blk.gpsimd(_)

        emit_ln("a1.", lambda sc: WG(sc, "a1.xcp15"))
        emit_attn(1, "a1.", lambda gq: WG(gq, "a1.xcp15"))

        def _(gq):
            for kc in range(16):
                WV(gq, f"a1.res{kc}")
                gq.tensor_copy(out=xh[kc][:], in_=x_f[kc][:])
                MKG(f"f1.xcp{kc}")
        blk.gpsimd(_)

        emit_ln("f1.", lambda sc: WG(sc, "f1.xcp15"))
        emit_ffn1("f1.")

        def _(s):
            for mt in range(16):
                WV(s, f"f1.yev{mt}")
                s.dma_start(out=ydr[128 * mt:128 * (mt + 1), :],
                            in_=x_f[mt][:]).then_inc(
                                st_a if mt % 2 == 0 else st_b, 16)
            s.wait_ge(st_a, 16 * 8)
            s.wait_ge(st_b, 16 * 8)
        blk.sync(_)

    return nc



def prep_fused_inputs(inputs):
    x_full = np.ascontiguousarray(
        np.asarray(inputs["L0_attn_hidden_states"], np.float32))
    ro = l0_row_order()
    base = {}
    base["wq0"] = tiled_lhsT(np.ascontiguousarray(
        np.asarray(inputs["L0_attn_Wq"], np.float32)[ro].T))
    base["wk0"] = np.ascontiguousarray(
        np.asarray(inputs["L0_attn_Wk"], np.float32).T).reshape(16, 128, 512)
    base["wv0"] = np.ascontiguousarray(
        np.asarray(inputs["L0_attn_Wv"], np.float32).T).reshape(16, 128, 512)
    base["wo0"] = tiled_lhsT(np.ascontiguousarray(
        np.asarray(inputs["L0_attn_Wo"], np.float32)[:, ro].T))
    wq2 = np.asarray(inputs["L1_attn_Wq"],
                     np.float32).transpose(1, 0, 2).reshape(2048, 2048)
    wk2 = np.asarray(inputs["L1_attn_Wk"],
                     np.float32).transpose(1, 0, 2).reshape(2048, 512)
    wv2 = np.asarray(inputs["L1_attn_Wv"],
                     np.float32).transpose(1, 0, 2).reshape(2048, 512)
    base["wq1"] = tiled_lhsT(np.ascontiguousarray(wq2))
    base["wk1"] = np.ascontiguousarray(wk2).reshape(16, 128, 512)
    base["wv1"] = np.ascontiguousarray(wv2).reshape(16, 128, 512)
    bq = np.asarray(inputs["L1_attn_bq"], np.float32).reshape(2048)
    bk = np.asarray(inputs["L1_attn_bk"], np.float32).reshape(512)
    bv = np.asarray(inputs["L1_attn_bv"], np.float32).reshape(512)
    base["bq1"] = np.ascontiguousarray(bq.reshape(16, 128).T)
    base["bk1"] = np.ascontiguousarray(
        np.broadcast_to(bk.reshape(1, 512), (128, 512)))
    bvg = np.zeros((4, 129), np.float32)
    bvr = bv.reshape(4, 128)
    bvg[:, 0:64] = bvr[:, 0:64]
    bvg[:, 65:129] = bvr[:, 64:128]
    base["bv1"] = np.ascontiguousarray(
        np.broadcast_to(bvg.reshape(1, 516), (128, 516)))
    base["eq0"] = head_indicator_tiles(2048, 32)
    base["ek0"] = head_indicator_tiles(512, 8)
    base["eq1"] = head_indicator_tiles(2048, 16)
    base["ek1"] = head_indicator_tiles(512, 4)
    base["wg"] = tiled_lhsT(np.ascontiguousarray(
        u8_view(np.asarray(inputs["L0_ffn_gate_w"])).T))
    base["wu"] = tiled_lhsT(np.ascontiguousarray(
        u8_view(np.asarray(inputs["L0_ffn_up_w"])).T))
    base["wd"] = np.ascontiguousarray(
        u8_view(np.asarray(inputs["L0_ffn_down_w"])).T).reshape(64, 128, 2048)
    base["sg"] = np.ascontiguousarray(
        np.asarray(inputs["L0_ffn_gate_scale"], np.float32).reshape(64, 128).T)
    base["su"] = np.ascontiguousarray(
        np.asarray(inputs["L0_ffn_up_scale"], np.float32).reshape(64, 128).T)
    base["sd"] = np.ascontiguousarray(
        np.asarray(inputs["L0_ffn_down_scale"], np.float32).reshape(16, 128).T)
    wf = np.asarray(inputs["L1_ffn_weight"])
    base["wf"] = tiled_lhsT(np.ascontiguousarray(wf.T))
    base["bfb"] = np.ascontiguousarray(
        np.asarray(inputs["L1_ffn_bias"]).reshape(16, 128).T)
    base["ident"] = np.eye(128, dtype=np.float32)
    kk, qq = np.meshgrid(np.arange(128), np.arange(128), indexing="ij")
    tril = (qq >= kk).astype(np.float32)
    ckg, skg = rope_tables(np.arange(2048))
    tabk_g = np.ascontiguousarray(np.concatenate([ckg, skg], axis=1))

    in_maps = []
    for cid in range(NCORES):
        b, p = cid // 2, cid % 2
        perm = perm_for_core(p)
        own = perm[:1024]
        m = dict(base)
        m["x"] = np.ascontiguousarray(x_full[b].T[:, own])
        cq, sq_ = rope_tables(own)
        m["tabq0"] = np.ascontiguousarray(np.concatenate([cq, sq_], axis=1))
        m["tabk0"] = tabk_g
        if p == 0:
            m["maske"] = tril
            m["masko"] = np.zeros((128, 128), np.float32)
        else:
            m["maske"] = np.ones((128, 128), np.float32)
            m["masko"] = tril
        sidx = np.empty((128, 8), np.uint32)
        for tt in range(8):
            sidx[:, tt] = 128 * (p + 2 * tt) + np.arange(128)
        m["sidx"] = np.ascontiguousarray(sidx)
        in_maps.append(m)
    return in_maps, x_full


def gather_fused_output(results, x_full):
    out = np.empty_like(x_full)
    for cid in range(NCORES):
        b, p = cid // 2, cid % 2
        perm = perm_for_core(p)
        out[b, perm[:1024], :] = results[cid]["y"].T
    return out


def kernel(**inputs):
    key = "fused"
    if key not in _NC_CACHE:
        _NC_CACHE[key] = build_fused()
    in_maps, x_full = prep_fused_inputs(inputs)
    res = run_bass_kernel_spmd(_NC_CACHE[key], in_maps,
                               core_ids=list(range(NCORES)))
    return gather_fused_output(res.results, x_full)

def build_fused():
    marks = Marks()
    _build_fused(marks)
    marks.pass2 = True
    return _build_fused(marks)


def _build_fused(marks):
    nc = bass.Bass("TRN2", target_bir_lowering=False)
    nc.num_devices = 8

    xdr = nc.dram_tensor("x", [DM, TOKQ], F32, kind="ExternalInput").ap()
    ydr = nc.dram_tensor("y", [DM, TOKQ], F32, kind="ExternalOutput").ap()
    d = {}
    d["wq0"] = nc.dram_tensor("wq0", [16, 128, 2048], F32, kind="ExternalInput").ap()
    d["wk0"] = nc.dram_tensor("wk0", [16, 128, 512], F32, kind="ExternalInput").ap()
    d["wv0"] = nc.dram_tensor("wv0", [16, 128, 512], F32, kind="ExternalInput").ap()
    d["wo0"] = nc.dram_tensor("wo0", [16, 128, 2048], F32, kind="ExternalInput").ap()
    d["tabq0"] = nc.dram_tensor("tabq0", [128, 2048], F32, kind="ExternalInput").ap()
    d["tabk0"] = nc.dram_tensor("tabk0", [128, 4096], F32, kind="ExternalInput").ap()
    d["eq0"] = nc.dram_tensor("eq0", [128, 512], F32, kind="ExternalInput").ap()
    d["ek0"] = nc.dram_tensor("ek0", [128, 32], F32, kind="ExternalInput").ap()
    d["wq1"] = nc.dram_tensor("wq1", [16, 128, 2048], F32, kind="ExternalInput").ap()
    d["wk1"] = nc.dram_tensor("wk1", [16, 128, 512], F32, kind="ExternalInput").ap()
    d["wv1"] = nc.dram_tensor("wv1", [16, 128, 512], F32, kind="ExternalInput").ap()
    d["bq1"] = nc.dram_tensor("bq1", [128, 16], F32, kind="ExternalInput").ap()
    d["bk1"] = nc.dram_tensor("bk1", [128, 512], F32, kind="ExternalInput").ap()
    d["bv1"] = nc.dram_tensor("bv1", [128, 516], F32, kind="ExternalInput").ap()
    d["eq1"] = nc.dram_tensor("eq1", [128, 256], F32, kind="ExternalInput").ap()
    d["ek1"] = nc.dram_tensor("ek1", [128, 16], F32, kind="ExternalInput").ap()
    d["wg"] = nc.dram_tensor("wg", [64, 128, 2048], U8, kind="ExternalInput").ap()
    d["wu"] = nc.dram_tensor("wu", [64, 128, 2048], U8, kind="ExternalInput").ap()
    d["wd"] = nc.dram_tensor("wd", [64, 128, 2048], U8, kind="ExternalInput").ap()
    d["sg"] = nc.dram_tensor("sg", [128, 64], F32, kind="ExternalInput").ap()
    d["su"] = nc.dram_tensor("su", [128, 64], F32, kind="ExternalInput").ap()
    d["sd"] = nc.dram_tensor("sd", [128, 16], F32, kind="ExternalInput").ap()
    d["wf"] = nc.dram_tensor("wf", [16, 128, 2048], BF16, kind="ExternalInput").ap()
    d["bfb"] = nc.dram_tensor("bfb", [128, 16], BF16, kind="ExternalInput").ap()
    d["maske"] = nc.dram_tensor("maske", [128, 128], F32, kind="ExternalInput").ap()
    d["masko"] = nc.dram_tensor("masko", [128, 128], F32, kind="ExternalInput").ap()
    d["ident"] = nc.dram_tensor("ident", [128, 128], F32, kind="ExternalInput").ap()
    d["sidx"] = nc.dram_tensor("sidx", [128, 8], U32, kind="ExternalInput").ap()
    shr_k = [nc.dram_tensor(f"shr_k{l}", [2048, 512], BF16, kind="Internal",
                            addr_space="Shared").ap() for l in range(2)]
    shr_v = [nc.dram_tensor(f"shr_v{l}", [2048, 520], BF16, kind="Internal",
                            addr_space="Shared").ap() for l in range(2)]
    bar = nc.dram_tensor("bar", [1, 1], F32, kind="Internal").ap()
    cc_in = nc.dram_tensor("cc_in", [1, 1], F32, kind="Internal").ap()

    with ExitStack() as top:
        E = top.enter_context
        blk0 = E(nc.Block())

        x_f = [E(nc.sbuf_tensor(f"x{i}", [128, TOKQ], F32)) for i in range(16)]
        xh = [E(nc.sbuf_tensor(f"xh{i}", [128, TOKQ], BF16)) for i in range(16)]
        ones_c = E(nc.sbuf_tensor("ones_c", [128, 1], BF16))
        ones_r = E(nc.sbuf_tensor("ones_r", [1, 128], BF16))
        zrow = E(nc.sbuf_tensor("zrow", [1, 65], BF16))
        stat = E(nc.sbuf_tensor("stat", [1, 1537], F32))
        ab_bf = E(nc.sbuf_tensor("ab_bf", [1, 2048], BF16))
        abc = E(nc.sbuf_tensor("abc", [128, 1024], BF16))
        x2b = [E(nc.sbuf_tensor(f"x2b{i}", [128, 1024], BF16)) for i in range(2)]
        mk_e = E(nc.sbuf_tensor("mk_e", [128, 128], BF16))
        mk_o = E(nc.sbuf_tensor("mk_o", [128, 128], BF16))
        id_s = E(nc.sbuf_tensor("id_s", [128, 128], F32))
        bias_sb = E(nc.sbuf_tensor("bias_sb", [128, 1], F32))
        sidx_s = E(nc.sbuf_tensor("sidx_s", [128, 8], U32))
        zf = E(nc.sbuf_tensor("zf", [1, 1], F32))

        x_ld = E(nc.semaphore("x_ld"))
        xb_ld = E(nc.semaphore("xb_ld"))
        s_ld = E(nc.semaphore("s_ld"))
        m_ld = E(nc.semaphore("m_ld"))
        ca = [E(nc.semaphore(f"ca{l}")) for l in range(2)]
        cv = [E(nc.semaphore(f"cv{l}")) for l in range(2)]
        b_ld = E(nc.semaphore("b_ld"))
        f0c = E(nc.semaphore("f0c"))
        f1c = E(nc.semaphore("f1c"))
        wl = [E(nc.semaphore(f"wl{i}")) for i in range(4)]
        d_lda = E(nc.semaphore("d_lda"))
        sh_a = E(nc.semaphore("sh_a"))
        sh_b = E(nc.semaphore("sh_b"))
        rb_s = E(nc.semaphore("rb_s"))
        st_a = E(nc.semaphore("st_a"))
        st_b = E(nc.semaphore("st_b"))
        cc_s = E(nc.semaphore("cc_s"))
        cc_c = E(nc.semaphore("cc_c"))
        bz = E(nc.semaphore("bz"))
        t_ln = E(nc.semaphore("t_ln"))
        sems_attn = {}
        for pfx in ("a0.", "a1."):
            for nm in ("t_kv", "t_q", "t_nb", "t_s", "t_av", "t_bc", "t_p5"):
                sems_attn[pfx + nm] = E(nc.semaphore(pfx + nm))
        t_mm = E(nc.semaphore("t_mm"))
        t_dn = E(nc.semaphore("t_dn"))
        t_f1 = E(nc.semaphore("t_f1"))
        v_self = E(nc.semaphore("v_self"))
        a_self = E(nc.semaphore("a_self"))
        g_self = E(nc.semaphore("g_self"))
        SV = {"sem": v_self, "n": 0}
        SA = {"sem": a_self, "n": 0}
        SG = {"sem": g_self, "n": 0}
        _vw, _aw, _gw = SerialEng(SV), SerialEng(SA), SerialEng(SG)

        class blk:
            tensor = staticmethod(blk0.tensor)
            sync = staticmethod(blk0.sync)
            vector = staticmethod(lambda f: blk0.vector(lambda e: f(_vw.bind(e))))
            scalar = staticmethod(lambda f: blk0.scalar(lambda e: f(_aw.bind(e))))
            gpsimd = staticmethod(lambda f: blk0.gpsimd(lambda e: f(_gw.bind(e))))

        def MKV(n):
            marks.mark(SV, n)

        def MKA(n):
            marks.mark(SA, n)

        def MKG(n):
            marks.mark(SG, n)

        def WV(eng, n):
            val = marks.get(n)
            if val > 0:
                eng.wait_ge(v_self, val)

        def WA(eng, n):
            val = marks.get(n)
            if val > 0:
                eng.wait_ge(a_self, val)

        def WG(eng, n):
            val = marks.get(n)
            if val > 0:
                eng.wait_ge(g_self, val)

        tot = {"sh": [0, 0], "rb": 0, "wl": [0, 0, 0, 0], "d": 0,
               "cc_c": 0, "cc_s": 0, "tln": 0}
        mkc = [mk_e if c % 2 == 0 else mk_o for c in range(16)]

        def _(s):
            for kc in range(16):
                s.dma_start(out=x_f[kc][:],
                            in_=xdr[128 * kc:128 * (kc + 1), :]).then_inc(x_ld, 16)
            s.dma_start(out=id_s[:], in_=d["ident"]).then_inc(s_ld, 16)
            s.dma_start(out=sidx_s[:], in_=d["sidx"]).then_inc(s_ld, 16)
        blk.sync(_)

        def _(gq):
            for kc in range(16):
                gq.dma_start(out=xh[kc][:],
                             in_=xdr[128 * kc:128 * (kc + 1), :]).then_inc(xb_ld, 16)
            gq.dma_start(out=mk_e[:], in_=d["maske"]).then_inc(m_ld, 16)
            gq.dma_start(out=mk_o[:], in_=d["masko"]).then_inc(m_ld, 16)
        blk.gpsimd(_)

        def _(v):
            v.memset(ones_c[:], 1.0)
            v.memset(ones_r[:], 1.0)
            v.memset(zrow[:], 0.0)
            v.memset(zf[:], 0.0)
            MKV("ones")
        blk.vector(_)

        def _(gq):
            WV(gq, "ones")
            gq.dma_start(out=bar, in_=zf[:]).then_inc(bz, 16)
        blk.gpsimd(_)

        def emit_ln(pfx, gate_scalar):
            nsg = 2
            tb = tot["tln"]
            with ExitStack() as lnp:
                le = lnp.enter_context
                lnps = [le(nc.psum_tensor(f"{pfx}lnps{i}", [1, 512], F32))
                        for i in range(2)]
                lnpq = [le(nc.psum_tensor(f"{pfx}lnpq{i}", [1, 512], F32))
                        for i in range(2)]
                lnbc = [le(nc.psum_tensor(f"{pfx}lnbc{i}", [128, 512], F32))
                        for i in range(2)]

                def _(sc):
                    gate_scalar(sc)
                    for kc in range(16):
                        if kc >= 2:
                            sc.wait_ge(t_ln, tb + kc - 1)
                        sc.activation(out=x2b[kc % 2][:], in_=xh[kc][:],
                                      func=AF.Square)
                        MKA(f"{pfx}sq{kc}")
                blk.scalar(_)

                def _(t):
                    WV(t, "ones")
                    for kc in range(16):
                        WA(t, f"{pfx}sq{kc}")
                        for sg_ in range(nsg):
                            t.matmul(out=lnps[sg_][0:1, :], lhsT=ones_c[:, 0:1],
                                     rhs=xh[kc][:, 512 * sg_:512 * (sg_ + 1)],
                                     start=(kc == 0), stop=(kc == 15))
                        ins = None
                        for sg_ in range(nsg):
                            ins = t.matmul(out=lnpq[sg_][0:1, :],
                                           lhsT=ones_c[:, 0:1],
                                           rhs=x2b[kc % 2][:, 512 * sg_:512 * (sg_ + 1)],
                                           start=(kc == 0), stop=(kc == 15))
                        ins.then_inc(t_ln, 1)
                blk.tensor(_)

                eps_r = stat[:, 1536:1537]
                mu_r = stat[:, 0:512]
                var_r = stat[:, 512:1024]
                sc_r = stat[:, 1024:1536]

                def _(v):
                    if pfx == "a0.":
                        v.memset(eps_r, EPS_LN)
                    v.wait_ge(t_ln, tb + 16)
                    for sg_ in range(nsg):
                        v.tensor_scalar_mul(out=mu_r, in0=lnps[sg_][0:1, :],
                                            scalar1=1.0 / DM)
                        v.tensor_scalar_mul(out=var_r, in0=lnpq[sg_][0:1, :],
                                            scalar1=1.0 / DM)
                        v.tensor_tensor(out=sc_r, in0=mu_r, in1=mu_r,
                                        op=ALU.mult)
                        v.tensor_tensor(out=var_r, in0=var_r, in1=sc_r,
                                        op=ALU.subtract)
                        MKV(f"{pfx}prep{sg_}")
                        WA(v, f"{pfx}rstd{sg_}")
                        v.tensor_tensor(out=sc_r, in0=mu_r, in1=var_r,
                                        op=ALU.mult)
                        v.tensor_scalar_mul(out=sc_r, in0=sc_r, scalar1=-1.0)
                        v.tensor_copy(out=ab_bf[:, 512 * sg_:512 * (sg_ + 1)],
                                      in_=var_r)
                        v.tensor_copy(
                            out=ab_bf[:, 1024 + 512 * sg_:1024 + 512 * (sg_ + 1)],
                            in_=sc_r)
                        MKV(f"{pfx}ab{sg_}")
                blk.vector(_)

                def _(sc):
                    for sg_ in range(nsg):
                        WV(sc, f"{pfx}prep{sg_}")
                        sc.activation(out=var_r, in_=var_r, func=AF.Ln,
                                      bias=eps_r)
                        sc.activation(out=var_r, in_=var_r, func=AF.Exp,
                                      scale=-0.5)
                        MKA(f"{pfx}rstd{sg_}")
                blk.scalar(_)

                def _(t):
                    for half in range(2):
                        if half == 1:
                            WV(t, f"{pfx}xh_mul")
                        for sg_ in range(nsg):
                            idx = half * nsg + sg_
                            WV(t, f"{pfx}ab{sg_}" if half == 0
                               else f"{pfx}ab{nsg - 1}")
                            if idx >= 2:
                                WA(t, f"{pfx}bce{idx - 2}")
                            t.matmul(out=lnbc[sg_ % 2][:, :],
                                     lhsT=ones_r[0:1, :],
                                     rhs=ab_bf[:, 512 * idx:512 * (idx + 1)],
                                     start=True, stop=True).then_inc(t_ln, 1)
                blk.tensor(_)

                def _(sc):
                    for half in range(2):
                        for sg_ in range(nsg):
                            idx = half * nsg + sg_
                            sc.wait_ge(t_ln, tb + 16 + idx + 1)
                            if half == 1 and sg_ == 0:
                                WV(sc, f"{pfx}xh_mul")
                            sc.copy(out=abc[:, 512 * sg_:512 * (sg_ + 1)],
                                    in_=lnbc[sg_ % 2][:, :])
                            MKA(f"{pfx}bce{idx}")
                blk.scalar(_)

                def _(v):
                    for half, op in ((0, ALU.mult), (1, ALU.add)):
                        WA(v, f"{pfx}bce{half * nsg + nsg - 1}")
                        for kc in range(16):
                            v.tensor_tensor(out=xh[kc][:], in0=xh[kc][:],
                                            in1=abc[:, 0:1024], op=op)
                        MKV(f"{pfx}xh_mul" if half == 0 else f"{pfx}xh_done")
                blk.vector(_)
            tot["tln"] += 20

        def emit_attn(layer, pfx, gate_g):
            if layer == 0:
                NH, NKV, HD = 32, 8, 64
                rope, bias_qkv, outproj = True, False, True
                scale = 0.125
                wqd_, wkd_, wvd_, wod_ = d["wq0"], d["wk0"], d["wv0"], d["wo0"]
            else:
                NH, NKV, HD = 16, 4, 128
                rope, bias_qkv, outproj = False, True, False
                scale = 1.0 / float(np.sqrt(128.0))
                wqd_, wkd_, wvd_ = d["wq1"], d["wk1"], d["wv1"]
                wod_ = None
            VG = HD + 1
            shk, shv = shr_k[layer], shr_v[layer]
            stor_heads = l0_head_order() if layer == 0 else list(range(NH))
            t_kv = sems_attn[pfx + "t_kv"]
            t_q = sems_attn[pfx + "t_q"]
            t_nb = sems_attn[pfx + "t_nb"]
            t_s = sems_attn[pfx + "t_s"]
            t_av = sems_attn[pfx + "t_av"]
            t_bc = sems_attn[pfx + "t_bc"]
            t_p5 = sems_attn[pfx + "t_p5"]

            with ExitStack() as ph:
                e = ph.enter_context
                qsb = [e(nc.sbuf_tensor(f"{pfx}q{i}", [128, TOKQ], BF16))
                       for i in range(16)]
                vsb = [e(nc.sbuf_tensor(f"{pfx}v{i}", [128, NKV * VG], BF16))
                       for i in range(16)]
                rp = [e(nc.sbuf_tensor(f"{pfx}rp{i}", [128, 2048], BF16))
                      for i in range(2)]
                wbuf = [e(nc.sbuf_tensor(f"{pfx}wb{i}", [128, DM], BF16))
                        for i in range(2)]
                if rope:
                    tabq = e(nc.sbuf_tensor(f"{pfx}tq", [128, 2048], BF16))
                if bias_qkv:
                    bq_s = e(nc.sbuf_tensor(f"{pfx}bq", [128, 16], F32))
                    bk_r = e(nc.sbuf_tensor(f"{pfx}bkr", [128, 512], BF16))
                    bv_r = e(nc.sbuf_tensor(f"{pfx}bvr", [128, 516], BF16))
                eq_s = e(nc.sbuf_tensor(f"{pfx}eq", [128, 16 * NH], BF16))
                ek_s = e(nc.sbuf_tensor(f"{pfx}ek", [128, 4 * NKV], BF16))

                def _(gq):
                    gate_g(gq)
                    gq.dma_start(out=eq_s[:], in_=d[f"eq{layer}"]).then_inc(ca[layer], 16)
                    gq.dma_start(out=ek_s[:], in_=d[f"ek{layer}"]).then_inc(ca[layer], 16)
                    if rope:
                        gq.dma_start(out=tabq[:], in_=d["tabq0"]).then_inc(ca[layer], 16)
                    if bias_qkv:
                        gq.dma_start(out=bk_r[:], in_=d["bk1"]).then_inc(ca[layer], 16)
                        gq.dma_start(out=bv_r[:], in_=d["bv1"]).then_inc(ca[layer], 16)
                blk.gpsimd(_)
                CT_A = 16 * ((3 if rope else 2) + (2 if bias_qkv else 0))

                if bias_qkv:
                    def _(s):
                        WG(s, f"{pfx}xcp15")
                        s.dma_start(out=bq_s[:], in_=d["bq1"]).then_inc(b_ld, 16)
                    blk.sync(_)

                pw = ph.enter_context(ExitStack())
                we = pw.enter_context
                wkv = [we(nc.sbuf_tensor(f"{pfx}wkv{i}", [128, 512], BF16))
                       for i in range(16)]
                stg_k = [we(nc.sbuf_tensor(f"{pfx}sk{i}", [128, 512], BF16))
                         for i in range(2)]
                stg_v = [we(nc.sbuf_tensor(f"{pfx}sv{i}", [128, 520], BF16))
                         for i in range(2)]
                ps_kv = [we(nc.psum_tensor(f"{pfx}pkv{i}", [128, 512], F32))
                         for i in range(2)]

                def _(gq):
                    for kc in range(16):
                        gq.dma_start(out=wkv[kc][:],
                                     in_=wkd_[kc]).then_inc(ca[layer], 16)
                blk.gpsimd(_)
                CT_K = CT_A + 256

                def _(t):
                    WV(t, f"{pfx}xh_done")
                    t.wait_ge(ca[layer], CT_K)
                    for tt in range(8):
                        if tt >= 2:
                            if bias_qkv:
                                WV(t, f"{pfx}bk{tt - 2}")
                            else:
                                WA(t, f"{pfx}evk{tt - 2}")
                        ins = None
                        for kc in range(16):
                            ins = t.matmul(
                                out=ps_kv[tt % 2][:, :],
                                lhsT=xh[kc][:, 128 * tt:128 * (tt + 1)],
                                rhs=wkv[kc][:, :],
                                start=(kc == 0), stop=(kc == 15))
                        ins.then_inc(t_kv, 1)
                blk.tensor(_)

                SH0 = [tot["sh"][0], tot["sh"][1]]

                def _(sc):
                    for tt in range(8):
                        sc.wait_ge(t_kv, tt + 1)
                        if tt >= 2:
                            sc.wait_ge(sh_a if tt % 2 == 0 else sh_b,
                                       SH0[tt % 2] + 16 * (tt // 2))
                        sc.copy(out=stg_k[tt % 2][:], in_=ps_kv[tt % 2][:, :])
                        MKA(f"{pfx}evk{tt}")
                blk.scalar(_)

                if bias_qkv:
                    def _(v):
                        v.wait_ge(ca[layer], CT_K)
                        for tt in range(8):
                            WA(v, f"{pfx}evk{tt}")
                            v.tensor_tensor(out=stg_k[tt % 2][:],
                                            in0=stg_k[tt % 2][:],
                                            in1=bk_r[:], op=ALU.add)
                            MKV(f"{pfx}bk{tt}")
                    blk.vector(_)

                def _(gq):
                    gq.wait_ge(s_ld, 32)
                    for tt in range(8):
                        if bias_qkv:
                            WV(gq, f"{pfx}bk{tt}")
                        else:
                            WA(gq, f"{pfx}evk{tt}")
                        gq.indirect_dma_start(
                            out=shk,
                            out_offset=bass.IndirectOffsetOnAxis(
                                ap=sidx_s[:, tt:tt + 1], axis=0),
                            in_=stg_k[tt % 2][:],
                            in_offset=None).then_inc(
                                sh_a if tt % 2 == 0 else sh_b, 16)
                blk.gpsimd(_)
                tot["sh"][0] += 16 * 4
                tot["sh"][1] += 16 * 4
                SHK = [tot["sh"][0], tot["sh"][1]]

                def _(gq):
                    gq.wait_ge(t_kv, 8)
                    for kc in range(16):
                        gq.dma_start(out=wkv[kc][:],
                                     in_=wvd_[kc]).then_inc(cv[layer], 16)
                blk.gpsimd(_)
                CT_V = 256

                def _(v):
                    WV(v, "ones")
                    v.memset(stg_v[0][:], 1.0)
                    v.memset(stg_v[1][:], 1.0)
                    MKV(f"{pfx}vst1")
                blk.vector(_)

                def _(t):
                    t.wait_ge(cv[layer], CT_V)
                    for tt in range(8):
                        if tt >= 2:
                            if bias_qkv:
                                WV(t, f"{pfx}bv{tt - 2}")
                            else:
                                WA(t, f"{pfx}evv{tt - 2}")
                        ins = None
                        for kc in range(16):
                            ins = t.matmul(
                                out=ps_kv[tt % 2][:, :],
                                lhsT=xh[kc][:, 128 * tt:128 * (tt + 1)],
                                rhs=wkv[kc][:, :],
                                start=(kc == 0), stop=(kc == 15))
                        ins.then_inc(t_kv, 1)
                blk.tensor(_)

                def _(sc):
                    WV(sc, f"{pfx}vst1")
                    for tt in range(8):
                        sc.wait_ge(t_kv, 8 + tt + 1)
                        if tt >= 2:
                            sc.wait_ge(sh_a if tt % 2 == 0 else sh_b,
                                       SHK[tt % 2] + 16 * (tt // 2))
                        src = ps_kv[tt % 2][:, :].rearrange(
                            "p (g dd) -> p g dd", g=NKV)
                        if HD == 64:
                            dst = stg_v[tt % 2][:, 0:NKV * VG].rearrange(
                                "p (g w) -> p g w", g=NKV)[:, :, 0:64]
                            sc.copy(out=dst, in_=src)
                        else:
                            src2 = ps_kv[tt % 2][:, :].rearrange(
                                "p (g s2 dd) -> p g s2 dd", g=NKV, s2=2)
                            gv = stg_v[tt % 2][:, 0:NKV * VG].rearrange(
                                "p (g w) -> p g w", g=NKV)
                            sc.copy(out=gv[:, :, 0:64], in_=src2[:, :, 0, :])
                            sc.copy(out=gv[:, :, 65:129], in_=src2[:, :, 1, :])
                        MKA(f"{pfx}evv{tt}")
                blk.scalar(_)

                if bias_qkv:
                    def _(v):
                        for tt in range(8):
                            WA(v, f"{pfx}evv{tt}")
                            v.tensor_tensor(out=stg_v[tt % 2][:, 0:NKV * VG],
                                            in0=stg_v[tt % 2][:, 0:NKV * VG],
                                            in1=bv_r[:], op=ALU.add)
                            MKV(f"{pfx}bv{tt}")
                    blk.vector(_)

                def _(gq):
                    for tt in range(8):
                        if bias_qkv:
                            WV(gq, f"{pfx}bv{tt}")
                        else:
                            WA(gq, f"{pfx}evv{tt}")
                        gq.indirect_dma_start(
                            out=shv,
                            out_offset=bass.IndirectOffsetOnAxis(
                                ap=sidx_s[:, tt:tt + 1], axis=0),
                            in_=stg_v[tt % 2][:],
                            in_offset=None).then_inc(
                                sh_a if tt % 2 == 0 else sh_b, 16)
                    gq.wait_ge(sh_a, SHK[0] + 16 * 4)
                    gq.wait_ge(sh_b, SHK[1] + 16 * 4)
                    gq.wait_ge(bz, 16)
                    gq.collective_compute(
                        "AllReduce", ALU.add,
                        replica_groups=[[0, 1], [2, 3], [4, 5], [6, 7]],
                        ins=[bar], outs=[bar]).then_inc(cc_c, 1)
                blk.gpsimd(_)
                tot["sh"][0] += 16 * 4
                tot["sh"][1] += 16 * 4
                tot["cc_c"] += 1
                AR1 = tot["cc_c"]

                pw.close()
                kbig = e(nc.sbuf_tensor(f"{pfx}kb", [128, 8192], BF16))
                if rope:
                    ptk = ph.enter_context(ExitStack())
                    tabk = ptk.enter_context(
                        nc.sbuf_tensor(f"{pfx}tk", [128, 4096], BF16))

                def kb(r0, r1, kt, c0, c1):
                    return kbig[r0:r1, 2048 * kt + c0:2048 * kt + c1]

                if rope:
                    def _(gq):
                        gq.dma_start(out=tabk[:],
                                     in_=d["tabk0"]).then_inc(ca[layer], 16)
                    blk.gpsimd(_)
                CT_K2 = CT_K + (16 if rope else 0)

                RB0 = tot["rb"]

                def _(s):
                    s.wait_ge(cc_c, AR1)
                    for j in range(16):
                        s.dma_start(out=vsb[j][:, 0:NKV * VG],
                                    in_=shv[128 * j:128 * (j + 1),
                                            0:NKV * VG]).then_inc(rb_s, 16)
                    s.wait_ge(rb_s, RB0 + 256)
                    s.wait_ge(x_ld, 256)
                    s.wait_ge(s_ld, 32)
                    dst = kbig[:].rearrange("p (m f) -> p m f", m=4)
                    s.dma_start(out=dst, in_=shk,
                                transpose=True).then_inc(rb_s, 16)
                    s.wait_ge(rb_s, RB0 + 272)
                blk.sync(_)
                tot["rb"] += 272
                RB_T = tot["rb"]

                with ExitStack() as pq:
                    qe = pq.enter_context
                    ps_p = [qe(nc.psum_tensor(f"{pfx}pp{i}", [128, TOKQ], F32))
                            for i in range(2)]
                    WLQ = [tot["wl"][0], tot["wl"][1]]

                    def _(gq):
                        for i in range(16):
                            if i >= 2:
                                gq.wait_ge(t_q, i - 1)
                            gq.dma_start(out=wbuf[i % 2][:],
                                         in_=wqd_[i]).then_inc(wl[i % 2], 16)
                    blk.gpsimd(_)
                    tot["wl"][0] += 16 * 8
                    tot["wl"][1] += 16 * 8

                    def _(t):
                        for mt in range(16):
                            t.wait_ge(wl[mt % 2],
                                      WLQ[mt % 2] + 16 * (mt // 2 + 1))
                            if mt >= 2:
                                if rope:
                                    WV(t, f"{pfx}ropeq{mt - 2}")
                                else:
                                    WA(t, f"{pfx}evq{mt - 2}")
                            ins = None
                            for kc in range(16):
                                for sgm in range(2):
                                    ins = t.matmul(
                                        out=ps_p[mt % 2][:, 512 * sgm:512 * (sgm + 1)],
                                        lhsT=wbuf[mt % 2][:, 128 * kc:128 * (kc + 1)],
                                        rhs=xh[kc][:, 512 * sgm:512 * (sgm + 1)],
                                        start=(kc == 0), stop=(kc == 15))
                            ins.then_inc(t_q, 1)
                    blk.tensor(_)

                    def _(sc):
                        if bias_qkv:
                            sc.wait_ge(b_ld, 16)
                        for mt in range(16):
                            sc.wait_ge(t_q, mt + 1)
                            if rope:
                                if mt >= 2:
                                    WV(sc, f"{pfx}ropeq{mt - 2}")
                                p = ps_p[mt % 2]
                                r1 = rp[mt % 2]
                                sc.copy(out=qsb[mt][:], in_=p[:, :])
                                sc.copy(out=r1[0:32, 0:TOKQ], in_=p[32:64, :])
                                sc.copy(out=r1[32:64, 0:TOKQ], in_=p[0:32, :])
                                sc.copy(out=r1[64:96, 0:TOKQ], in_=p[96:128, :])
                                sc.copy(out=r1[96:128, 0:TOKQ], in_=p[64:96, :])
                            else:
                                sc.activation(out=qsb[mt][:],
                                              in_=ps_p[mt % 2][:, :],
                                              func=AF.Identity,
                                              bias=bq_s[:, mt:mt + 1])
                            MKA(f"{pfx}evq{mt}")
                    blk.scalar(_)

                    if rope:
                        def _(v):
                            v.wait_ge(ca[layer], CT_K)
                            for mt in range(16):
                                WA(v, f"{pfx}evq{mt}")
                                r1 = rp[mt % 2]
                                v.tensor_tensor(out=qsb[mt][:], in0=qsb[mt][:],
                                                in1=tabq[:, 0:1024],
                                                op=ALU.mult)
                                v.tensor_tensor(out=r1[:, 0:TOKQ],
                                                in0=r1[:, 0:TOKQ],
                                                in1=tabq[:, 1024:2048],
                                                op=ALU.mult)
                                v.tensor_tensor(out=qsb[mt][:], in0=qsb[mt][:],
                                                in1=r1[:, 0:TOKQ], op=ALU.add)
                                MKV(f"{pfx}ropeq{mt}")
                        blk.vector(_)

                if rope:
                    def _(sc):
                        sc.wait_ge(rb_s, RB_T)
                        for j in range(4):
                            if j >= 2:
                                WV(sc, f"{pfx}ropek{j - 2}")
                            else:
                                WV(sc, f"{pfx}ropeq{14 + j}")
                            r1 = rp[j % 2]
                            sc.copy(out=r1[0:32, :], in_=kb(32, 64, j, 0, 2048))
                            sc.copy(out=r1[32:64, :], in_=kb(0, 32, j, 0, 2048))
                            sc.copy(out=r1[64:96, :], in_=kb(96, 128, j, 0, 2048))
                            sc.copy(out=r1[96:128, :], in_=kb(64, 96, j, 0, 2048))
                            MKA(f"{pfx}swk{j}")
                    blk.scalar(_)

                    def _(v):
                        v.wait_ge(ca[layer], CT_K2)
                        for j in range(4):
                            WA(v, f"{pfx}swk{j}")
                            r1 = rp[j % 2]
                            kj = kb(0, 128, j, 0, 2048)
                            v.tensor_tensor(out=kj, in0=kj,
                                            in1=tabk[:, 0:2048], op=ALU.mult)
                            v.tensor_tensor(out=r1[:, :], in0=r1[:, :],
                                            in1=tabk[:, 2048:4096],
                                            op=ALU.mult)
                            v.tensor_tensor(out=kj, in0=kj, in1=r1[:, :],
                                            op=ALU.add)
                            MKV(f"{pfx}ropek{j}")
                    blk.vector(_)
                    ptk.close()

                with ExitStack() as pp:
                    ee = pp.enter_context
                    ps_n = ee(nc.psum_tensor(f"{pfx}psn", [NH, TOKQ], F32))
                    ps_nk = ee(nc.psum_tensor(f"{pfx}psk", [NKV, 2048], F32))
                    ps_tr = ee(nc.psum_tensor(f"{pfx}ptr", [1, 128], F32))
                    ps_bb = ee(nc.psum_tensor(f"{pfx}pbb", [128, 1], F32))
                    nb = ee(nc.sbuf_tensor(f"{pfx}nb", [NH, 2], F32))
                    nbk = ee(nc.sbuf_tensor(f"{pfx}nbk", [NKV, 2], F32))
                    nbs = ee(nc.sbuf_tensor(f"{pfx}nbs", [1, 8], F32))
                    nb_bf = ee(nc.sbuf_tensor(f"{pfx}nbbf", [1, 1], BF16))

                    def _(sc):
                        if rope:
                            WV(sc, f"{pfx}ropek3")
                        else:
                            sc.wait_ge(rb_s, RB_T)
                        for mt in range(16):
                            if mt >= 2:
                                sc.wait_ge(t_nb, mt - 1)
                            sc.activation(out=rp[mt % 2][:, 0:TOKQ],
                                          in_=qsb[mt][:], func=AF.Square)
                            MKA(f"{pfx}nsq{mt}")
                        for j in range(4):
                            sc.wait_ge(t_nb, 15 + j)
                            sc.activation(out=rp[j % 2][:, :],
                                          in_=kb(0, 128, j, 0, 2048),
                                          func=AF.Square)
                            MKA(f"{pfx}nsk{j}")
                    blk.scalar(_)

                    def _(t):
                        for mt in range(16):
                            WA(t, f"{pfx}nsq{mt}")
                            ins = None
                            for sgm in range(2):
                                ins = t.matmul(
                                    out=ps_n[:, 512 * sgm:512 * (sgm + 1)],
                                    lhsT=eq_s[:, NH * mt:NH * (mt + 1)],
                                    rhs=rp[mt % 2][:, 512 * sgm:512 * (sgm + 1)],
                                    start=(mt == 0), stop=(mt == 15))
                            ins.then_inc(t_nb, 1)
                        for j in range(4):
                            WA(t, f"{pfx}nsk{j}")
                            ins = None
                            for sgm in range(4):
                                ins = t.matmul(
                                    out=ps_nk[:, 512 * sgm:512 * (sgm + 1)],
                                    lhsT=ek_s[:, NKV * j:NKV * (j + 1)],
                                    rhs=rp[j % 2][:, 512 * sgm:512 * (sgm + 1)],
                                    start=(j == 0), stop=(j == 3))
                            ins.then_inc(t_nb, 1)
                    blk.tensor(_)

                    def _(v):
                        v.wait_ge(t_nb, 20)
                        v.tensor_reduce(out=nb[:, 0:1], in_=ps_n[:, :],
                                        axis=AX.X, op=ALU.max)
                        v.tensor_reduce(out=nbk[:, 0:1], in_=ps_nk[:, :],
                                        axis=AX.X, op=ALU.max)
                        MKV(f"{pfx}nred1")
                    blk.vector(_)

                    def _(t):
                        WV(t, f"{pfx}nred1")
                        t.wait_ge(s_ld, 32)
                        t.transpose(out=ps_tr[0:1, 0:NH], in_=nb[:, 0:1],
                                    identity=id_s[0:NH, 0:NH])
                        t.transpose(out=ps_tr[0:1, 64:64 + NKV], in_=nbk[:, 0:1],
                                    identity=id_s[0:NKV, 0:NKV]).then_inc(t_nb, 1)
                    blk.tensor(_)

                    def _(v):
                        v.wait_ge(t_nb, 21)
                        v.tensor_reduce(out=nbs[:, 0:1], in_=ps_tr[0:1, 0:NH],
                                        axis=AX.X, op=ALU.max)
                        v.tensor_reduce(out=nbs[:, 1:2],
                                        in_=ps_tr[0:1, 64:64 + NKV],
                                        axis=AX.X, op=ALU.max)
                        v.tensor_tensor(out=nbs[:, 2:3], in0=nbs[:, 0:1],
                                        in1=nbs[:, 1:2], op=ALU.mult)
                        MKV(f"{pfx}nred2")
                    blk.vector(_)

                    def _(sc):
                        WV(sc, f"{pfx}nred2")
                        sc.activation(out=nbs[:, 3:4], in_=nbs[:, 2:3],
                                      func=AF.Ln)
                        sc.activation(out=nbs[:, 4:5], in_=nbs[:, 3:4],
                                      func=AF.Exp, scale=0.5)
                        MKA(f"{pfx}nsqrt")
                    blk.scalar(_)

                    def _(v):
                        WA(v, f"{pfx}nsqrt")
                        v.tensor_scalar_mul(out=nbs[:, 5:6], in0=nbs[:, 4:5],
                                            scalar1=-scale)
                        v.tensor_copy(out=nb_bf[:, :], in_=nbs[:, 5:6])
                        MKV(f"{pfx}nbf")
                    blk.vector(_)

                    def _(t):
                        WV(t, f"{pfx}nbf")
                        t.matmul(out=ps_bb[:, 0:1], lhsT=ones_r[0:1, :],
                                 rhs=nb_bf[0:1, 0:1], start=True,
                                 stop=True).then_inc(t_nb, 1)
                    blk.tensor(_)

                    def _(sc):
                        sc.wait_ge(t_nb, 22)
                        sc.copy(out=bias_sb[:, :], in_=ps_bb[:, :])
                        MKA(f"{pfx}bias")
                    blk.scalar(_)

                of = qsb
                with ExitStack() as pp:
                    ee = pp.enter_context
                    pt = x2b
                    rc_b = ee(nc.sbuf_tensor(f"{pfx}rcb", [1, TOKQ], BF16))
                    bb_s = ee(nc.sbuf_tensor(f"{pfx}bbs", [HD, TOKQ], BF16))
                    ns_s = 2 if layer == 0 else 1
                    ps_s = [ee(nc.psum_tensor(f"{pfx}pss{i}", [128, TOKQ], F32))
                            for i in range(ns_s)]
                    if layer == 0:
                        ps_o = ee(nc.psum_tensor(f"{pfx}pso", [HD + 1, TOKQ], F32))
                    else:
                        ps_o1 = ee(nc.psum_tensor(f"{pfx}po1", [65, TOKQ], F32))
                        ps_o2 = ee(nc.psum_tensor(f"{pfx}po2", [64, TOKQ], F32))
                    ps_b = ee(nc.psum_tensor(f"{pfx}psb", [HD, TOKQ], F32))

                    def segs_for(s0):
                        a = 128 * s0
                        if a < 512:
                            return [(a, 512), (512, 1024)]
                        return [(a, 1024)]

                    def _(t):
                        WA(t, f"{pfx}bias")
                        for hs in range(NH):
                            h_orig = stor_heads[hs]
                            g = h_orig // (NH // NKV)
                            if HD == 64:
                                qt, qr = hs // 2, 64 * (hs % 2)
                            else:
                                qt, qr = hs, 0
                            kt, kr = (g * HD) // 128, (g * HD) % 128

                            def emit_S(c):
                                s0 = c // 2
                                idx = 16 * hs + c
                                if idx >= ns_s:
                                    WA(t, f"{pfx}exp{idx - ns_s}")
                                ins = None
                                for (a, b) in segs_for(s0):
                                    ins = t.matmul(
                                        out=ps_s[idx % ns_s][:, a:b],
                                        lhsT=kb(kr, kr + HD, kt,
                                                128 * c, 128 * (c + 1)),
                                        rhs=qsb[qt][qr:qr + HD, a:b],
                                        start=True, stop=True)
                                ins.then_inc(t_s, 1)

                            def emit_AV(c):
                                s0 = c // 2
                                idx = 16 * hs + c
                                WV(t, f"{pfx}msk{idx}")
                                if c == 0 and hs >= 1:
                                    WV(t, f"{pfx}epi{hs - 1}")
                                ins = None
                                for (a, b) in segs_for(s0):
                                    if layer == 0:
                                        ins = t.matmul(
                                            out=ps_o[:, a:b],
                                            lhsT=vsb[c][:, VG * g:VG * g + 65],
                                            rhs=pt[idx % 2][:, a:b],
                                            start=(c == 0), stop=False)
                                    else:
                                        t.matmul(
                                            out=ps_o1[:, a:b],
                                            lhsT=vsb[c][:, VG * g:VG * g + 65],
                                            rhs=pt[idx % 2][:, a:b],
                                            start=(c == 0), stop=False)
                                        ins = t.matmul(
                                            out=ps_o2[:, a:b],
                                            lhsT=vsb[c][:, VG * g + 65:VG * g + 129],
                                            rhs=pt[idx % 2][:, a:b],
                                            start=(c == 0), stop=False)
                                if c != 15:
                                    ins.then_inc(t_av, 1)

                            def emit_close():
                                ins = None
                                for sgm in range(2):
                                    cols = slice(512 * sgm, 512 * (sgm + 1))
                                    if layer == 0:
                                        ins = t.matmul(out=ps_o[:, cols],
                                                       lhsT=zrow[0:1, 0:65],
                                                       rhs=abc[0:1, 0:512],
                                                       start=False, stop=True)
                                    else:
                                        t.matmul(out=ps_o1[:, cols],
                                                 lhsT=zrow[0:1, 0:65],
                                                 rhs=abc[0:1, 0:512],
                                                 start=False, stop=True)
                                        ins = t.matmul(out=ps_o2[:, cols],
                                                       lhsT=zrow[0:1, 0:64],
                                                       rhs=abc[0:1, 0:512],
                                                       start=False, stop=True)
                                ins.then_inc(t_av, 1)

                            if ns_s == 2:
                                emit_S(0)
                                for c in range(1, 16):
                                    emit_S(c)
                                    emit_AV(c - 1)
                                emit_AV(15)
                            else:
                                for c in range(16):
                                    emit_S(c)
                                    emit_AV(c)
                            emit_close()
                            WV(t, f"{pfx}rc{hs}")
                            if hs >= 1:
                                WA(t, f"{pfx}bbe{hs - 1}")
                            ins = None
                            for sgm in range(2):
                                ins = t.matmul(
                                    out=ps_b[:, 512 * sgm:512 * (sgm + 1)],
                                    lhsT=ones_r[0:1, 0:HD],
                                    rhs=rc_b[0:1, 512 * sgm:512 * (sgm + 1)],
                                    start=True, stop=True)
                            ins.then_inc(t_bc, 1)
                    blk.tensor(_)

                    def _(sc):
                        for hs in range(NH):
                            for c in range(16):
                                s0 = c // 2
                                idx = 16 * hs + c
                                sc.wait_ge(t_s, idx + 1)
                                if idx >= 2:
                                    sc.wait_ge(t_av, idx - 1)
                                sc.activation(out=pt[idx % 2][:, 128 * s0:1024],
                                              in_=ps_s[idx % ns_s][:, 128 * s0:1024],
                                              func=AF.Exp, bias=bias_sb[:, 0:1],
                                              scale=scale)
                                MKA(f"{pfx}exp{idx}")
                            sc.wait_ge(t_bc, hs + 1)
                            sc.copy(out=bb_s[:, :], in_=ps_b[:, :])
                            MKA(f"{pfx}bbe{hs}")
                    blk.scalar(_)

                    def _(v):
                        v.wait_ge(m_ld, 32)
                        for hs in range(NH):
                            for c in range(16):
                                s0 = c // 2
                                idx = 16 * hs + c
                                WA(v, f"{pfx}exp{idx}")
                                v.tensor_tensor(
                                    out=pt[idx % 2][:, 128 * s0:128 * (s0 + 1)],
                                    in0=pt[idx % 2][:, 128 * s0:128 * (s0 + 1)],
                                    in1=mkc[c][:, :], op=ALU.mult)
                                MKV(f"{pfx}msk{idx}")
                            v.wait_ge(t_av, 16 * (hs + 1))
                            with nc.allow_low_precision(reason="1/sum -> bf16"):
                                v.reciprocal(out=rc_b[:, :],
                                             in_=(ps_o if layer == 0
                                                  else ps_o1)[64:65, :])
                            MKV(f"{pfx}rc{hs}")
                            WA(v, f"{pfx}bbe{hs}")
                            if layer == 0:
                                qt, qr = hs // 2, 64 * (hs % 2)
                                v.tensor_tensor(out=of[qt][qr:qr + HD, :],
                                                in0=ps_o[0:64, :],
                                                in1=bb_s[:, :], op=ALU.mult)
                            else:
                                v.tensor_tensor(out=of[hs][0:64, :],
                                                in0=ps_o1[0:64, :],
                                                in1=bb_s[0:64, :], op=ALU.mult)
                                v.tensor_tensor(out=of[hs][64:128, :],
                                                in0=ps_o2[0:64, :],
                                                in1=bb_s[64:128, :],
                                                op=ALU.mult)
                            MKV(f"{pfx}epi{hs}")
                    blk.vector(_)

                if outproj:
                    with ExitStack() as pp:
                        ee = pp.enter_context
                        ps_5 = [ee(nc.psum_tensor(f"{pfx}p5{i}", [128, TOKQ], F32))
                                for i in range(2)]
                        WL5 = [tot["wl"][2], tot["wl"][3]]

                        def _(gq):
                            WA(gq, f"{pfx}bias")
                            for mt in range(16):
                                if mt >= 2:
                                    gq.wait_ge(t_p5, mt - 1)
                                gq.dma_start(out=wbuf[mt % 2][:],
                                             in_=wod_[mt]).then_inc(
                                                 wl[2 + mt % 2], 16)
                        blk.gpsimd(_)
                        tot["wl"][2] += 16 * 8
                        tot["wl"][3] += 16 * 8

                        def _(t):
                            WV(t, f"{pfx}epi{NH - 1}")
                            for mt in range(16):
                                t.wait_ge(wl[2 + mt % 2],
                                          WL5[mt % 2] + 16 * (mt // 2 + 1))
                                if mt >= 2:
                                    WV(t, f"{pfx}res{mt - 2}")
                                ins = None
                                for kc in range(16):
                                    for sgm in range(2):
                                        ins = t.matmul(
                                            out=ps_5[mt % 2][:, 512 * sgm:512 * (sgm + 1)],
                                            lhsT=wbuf[mt % 2][:, 128 * kc:128 * (kc + 1)],
                                            rhs=of[kc][:, 512 * sgm:512 * (sgm + 1)],
                                            start=(kc == 0), stop=(kc == 15))
                                ins.then_inc(t_p5, 1)
                        blk.tensor(_)

                        def _(v):
                            v.wait_ge(x_ld, 256)
                            for mt in range(16):
                                v.wait_ge(t_p5, mt + 1)
                                v.tensor_tensor(out=x_f[mt][:], in0=x_f[mt][:],
                                                in1=ps_5[mt % 2][:, :],
                                                op=ALU.add)
                                MKV(f"{pfx}res{mt}")
                        blk.vector(_)
                else:
                    def _(v):
                        WV(v, f"{pfx}epi{NH - 1}")
                        for mt in range(16):
                            v.tensor_tensor(out=x_f[mt][:], in0=x_f[mt][:],
                                            in1=of[mt][:], op=ALU.add)
                            MKV(f"{pfx}res{mt}")
                    blk.vector(_)

        def emit_ffn0(pfx):
            with ExitStack() as ph:
                e = ph.enter_context
                wb = [e(nc.sbuf_tensor(f"{pfx}wb{i}", [128, 2048], BF16))
                      for i in range(4)]
                hb = [e(nc.sbuf_tensor(f"{pfx}hb{i}", [128, TOKQ], BF16))
                      for i in range(8)]
                db = [e(nc.sbuf_tensor(f"{pfx}db{i}", [128, 2048], BF16))
                      for i in range(8)]
                sg_s = e(nc.sbuf_tensor(f"{pfx}sgs", [128, 64], F32))
                su_s = e(nc.sbuf_tensor(f"{pfx}sus", [128, 64], F32))
                sd_s = e(nc.sbuf_tensor(f"{pfx}sds", [128, 16], F32))
                gb = e(nc.sbuf_tensor(f"{pfx}gb", [128, TOKQ], BF16))
                ub = e(nc.sbuf_tensor(f"{pfx}ub", [128, TOKQ], BF16))
                ps_g = e(nc.psum_tensor(f"{pfx}psg", [128, TOKQ], F32))
                ps_u = e(nc.psum_tensor(f"{pfx}psu", [128, TOKQ], F32))
                ps_d = [e(nc.psum_tensor(f"{pfx}psd{i}", [128, TOKQ], F32))
                        for i in range(2)]

                def _(s):
                    WV(s, "a0.res15")
                    s.dma_start(out=sg_s[:], in_=d["sg"]).then_inc(f0c, 16)
                    s.dma_start(out=su_s[:], in_=d["su"]).then_inc(f0c, 16)
                    s.dma_start(out=sd_s[:], in_=d["sd"]).then_inc(f0c, 16)
                blk.sync(_)

                def gu_src(i):
                    return (d["wu"] if i % 2 else d["wg"])[i // 2]

                WLF = [tot["wl"][j] for j in range(4)]
                D0 = tot["d"]

                def _(gq):
                    WV(gq, "a0.res15")
                    for i in range(4):
                        gq.dma_start(out=wb[i][:], in_=gu_src(i)).then_inc(
                            wl[i], 16)
                    for c in range(8):
                        gq.dma_start(out=db[c][:], in_=d["wd"][c]).then_inc(
                            d_lda, 16)
                    for r in range(8):
                        for m in range(8):
                            mt = 8 * r + m
                            for i in (2 * mt, 2 * mt + 1):
                                if i + 4 < 128:
                                    gq.wait_ge(t_mm, max(0, i + 1))
                                    gq.dma_start(out=wb[(i + 4) % 4][:],
                                                 in_=gu_src(i + 4)).then_inc(
                                                     wl[(i + 4) % 4], 16)
                        if r + 1 < 8:
                            gq.wait_ge(t_dn, 16 * (r + 1))
                            for c in range(8):
                                gq.dma_start(out=db[c][:],
                                             in_=d["wd"][8 * (r + 1) + c]
                                             ).then_inc(d_lda, 16)
                blk.gpsimd(_)
                for i in range(4):
                    tot["wl"][i] += 16 * 32
                tot["d"] += 16 * 64

                def _(t):
                    WV(t, f"{pfx}xh_done")
                    for r in range(8):
                        for m in range(8):
                            mt = 8 * r + m
                            for i in (2 * mt, 2 * mt + 1):
                                WV(t, f"{pfx}wsub{i}")
                                is_u = i % 2
                                if mt >= 1:
                                    WV(t, f"{pfx}usc{mt - 1}" if is_u
                                       else f"{pfx}gsc{mt - 1}")
                                ps = ps_u if is_u else ps_g
                                ins = None
                                for kc in range(16):
                                    for sgm in range(2):
                                        ins = t.matmul(
                                            out=ps[:, 512 * sgm:512 * (sgm + 1)],
                                            lhsT=wb[i % 4][:, 128 * kc:128 * (kc + 1)],
                                            rhs=xh[kc][:, 512 * sgm:512 * (sgm + 1)],
                                            start=(kc == 0), stop=(kc == 15))
                                ins.then_inc(t_mm, 1)
                        WV(t, f"{pfx}dsub{r}")
                        WV(t, f"{pfx}h{8 * r + 7}")
                        for ht in range(16):
                            if 16 * r + ht >= 2:
                                WV(t, f"{pfx}dev{16 * r + ht - 2}")
                            ins = None
                            for c in range(8):
                                for sgm in range(2):
                                    ins = t.matmul(
                                        out=ps_d[ht % 2][:, 512 * sgm:512 * (sgm + 1)],
                                        lhsT=db[c][:, 128 * ht:128 * (ht + 1)],
                                        rhs=hb[c][:, 512 * sgm:512 * (sgm + 1)],
                                        start=(c == 0), stop=(c == 7))
                            ins.then_inc(t_dn, 1)
                blk.tensor(_)

                def _(sc):
                    for mt in range(64):
                        WV(sc, f"{pfx}gsc{mt}")
                        sc.activation(out=gb[:], in_=gb[:], func=AF.Silu)
                        MKA(f"{pfx}silu{mt}")
                blk.scalar(_)

                def _(v):
                    v.wait_ge(f0c, 48)

                    def wsub(i):
                        v.wait_ge(wl[i % 4], WLF[i % 4] + 16 * (i // 4 + 1))
                        v.tensor_scalar_add(out=wb[i % 4][:], in0=wb[i % 4][:],
                                            scalar1=-128.0)
                        MKV(f"{pfx}wsub{i}")

                    for i in range(4):
                        wsub(i)
                    for r in range(8):
                        for m in range(8):
                            mt = 8 * r + m
                            for i in (2 * mt + 4, 2 * mt + 5):
                                if i < 128:
                                    wsub(i)
                            if mt >= 1:
                                WA(v, f"{pfx}silu{mt - 1}")
                            v.wait_ge(t_mm, 2 * mt + 1)
                            v.tensor_scalar_mul(out=gb[:], in0=ps_g[:, :],
                                                scalar1=sg_s[:, mt:mt + 1])
                            MKV(f"{pfx}gsc{mt}")
                            v.wait_ge(t_mm, 2 * mt + 2)
                            v.tensor_scalar_mul(out=ub[:], in0=ps_u[:, :],
                                                scalar1=su_s[:, mt:mt + 1])
                            MKV(f"{pfx}usc{mt}")
                            WA(v, f"{pfx}silu{mt}")
                            v.tensor_tensor(out=hb[m][:], in0=gb[:], in1=ub[:],
                                            op=ALU.mult)
                            MKV(f"{pfx}h{mt}")
                        v.wait_ge(d_lda, D0 + 16 * 8 * (r + 1))
                        for c in range(8):
                            v.tensor_scalar_add(out=db[c][:], in0=db[c][:],
                                                scalar1=-128.0)
                        MKV(f"{pfx}dsub{r}")
                        for ht in range(16):
                            v.wait_ge(t_dn, 16 * r + ht + 1)
                            v.scalar_tensor_tensor(out=x_f[ht][:],
                                                   in0=ps_d[ht % 2][:, :],
                                                   scalar=sd_s[:, ht:ht + 1],
                                                   in1=x_f[ht][:],
                                                   op0=ALU.mult, op1=ALU.add)
                            MKV(f"{pfx}dev{16 * r + ht}")
                blk.vector(_)

        def emit_ffn1(pfx):
            with ExitStack() as ph:
                e = ph.enter_context
                wq = [e(nc.sbuf_tensor(f"{pfx}wq{i}", [128, 2048], FP8))
                      for i in range(16)]
                xq = [e(nc.sbuf_tensor(f"{pfx}xq{i}", [128, TOKQ], FP8))
                      for i in range(16)]
                wstg = [e(nc.sbuf_tensor(f"{pfx}ws{i}", [128, 2048], BF16))
                        for i in range(2)]
                bf_s = e(nc.sbuf_tensor(f"{pfx}bfs", [128, 16], BF16))
                bf_f = e(nc.sbuf_tensor(f"{pfx}bff", [128, 16], F32))
                red = e(nc.sbuf_tensor(f"{pfx}red", [128, 4], F32))
                sca = e(nc.sbuf_tensor(f"{pfx}sca", [1, 8], F32))
                sc_bf = e(nc.sbuf_tensor(f"{pfx}scbf", [1, 4], BF16))
                sc3 = e(nc.sbuf_tensor(f"{pfx}sc3", [128, 4], F32))
                ybf = e(nc.sbuf_tensor(f"{pfx}ybf", [128, TOKQ], BF16))
                ps_m = [e(nc.psum_tensor(f"{pfx}psm{i}", [128, TOKQ], F32))
                        for i in range(2)]
                ps_t = e(nc.psum_tensor(f"{pfx}pst", [1, 128], F32))
                ps_t2 = e(nc.psum_tensor(f"{pfx}pst2", [1, 128], F32))
                ps_c = e(nc.psum_tensor(f"{pfx}psc", [128, 4], F32))

                def _(s):
                    WV(s, "a1.res15")
                    s.dma_start(out=bf_s[:], in_=d["bfb"]).then_inc(f1c, 16)
                blk.sync(_)

                WLF = [tot["wl"][0], tot["wl"][1]]

                def _(gq):
                    WV(gq, "a1.res15")
                    for i in range(16):
                        if i >= 2:
                            WV(gq, f"{pfx}wmax{i - 2}")
                        gq.dma_start(out=wstg[i % 2][:], in_=d["wf"][i]
                                     ).then_inc(wl[i % 2], 16)
                blk.gpsimd(_)
                tot["wl"][0] += 16 * 8
                tot["wl"][1] += 16 * 8

                def _(v):
                    for i in range(16):
                        v.wait_ge(wl[i % 2], WLF[i % 2] + 16 * (i // 2 + 1))
                        v.tensor_reduce(out=red[:, 0:1], in_=wstg[i % 2][:],
                                        axis=AX.X, op=ALU.max,
                                        apply_absolute_value=True)
                        if i == 0:
                            v.tensor_copy(out=red[:, 1:2], in_=red[:, 0:1])
                        else:
                            v.tensor_tensor(out=red[:, 1:2], in0=red[:, 1:2],
                                            in1=red[:, 0:1], op=ALU.max)
                        MKV(f"{pfx}wmax{i}")
                    WV(v, f"{pfx}xh_done")
                    for kc in range(16):
                        v.tensor_reduce(out=red[:, 2:3], in_=xh[kc][:],
                                        axis=AX.X, op=ALU.max,
                                        apply_absolute_value=True)
                        if kc == 0:
                            v.tensor_copy(out=red[:, 3:4], in_=red[:, 2:3])
                        else:
                            v.tensor_tensor(out=red[:, 3:4], in0=red[:, 3:4],
                                            in1=red[:, 2:3], op=ALU.max)
                    MKV(f"{pfx}xmax")
                blk.vector(_)

                def _(t):
                    WV(t, f"{pfx}xmax")
                    t.wait_ge(s_ld, 32)
                    t.transpose(out=ps_t[0:1, 0:128], in_=red[:, 3:4],
                                identity=id_s[:, :]).then_inc(t_f1, 1)
                    t.transpose(out=ps_t2[0:1, 0:128], in_=red[:, 1:2],
                                identity=id_s[:, :]).then_inc(t_f1, 1)
                blk.tensor(_)

                def _(v):
                    v.wait_ge(t_f1, 2)
                    v.tensor_reduce(out=sca[:, 0:1], in_=ps_t[0:1, 0:128],
                                    axis=AX.X, op=ALU.max)
                    v.tensor_scalar_max(out=sca[:, 0:1], in0=sca[:, 0:1],
                                        scalar1=1e-6)
                    v.tensor_reduce(out=sca[:, 2:3], in_=ps_t2[0:1, 0:128],
                                    axis=AX.X, op=ALU.max)
                    v.tensor_scalar_max(out=sca[:, 2:3], in0=sca[:, 2:3],
                                        scalar1=1e-6)
                    MKV(f"{pfx}xsloc")
                blk.vector(_)

                CS0 = tot["cc_s"]
                CC0 = tot["cc_c"]

                def _(gq):
                    WV(gq, f"{pfx}xsloc")
                    gq.dma_start(out=cc_in, in_=sca[:, 0:1]).then_inc(cc_s, 16)
                    gq.wait_ge(cc_s, CS0 + 16)
                    gq.collective_compute(
                        "AllReduce", ALU.max,
                        replica_groups=[[0, 1, 2, 3, 4, 5, 6, 7]],
                        ins=[cc_in], outs=[cc_in]).then_inc(cc_c, 1)
                    gq.wait_ge(cc_c, CC0 + 1)
                    gq.dma_start(out=sca[:, 1:2], in_=cc_in).then_inc(cc_s, 16)
                    for i in range(16):
                        WV(gq, f"{pfx}wq{i - 2}" if i >= 2
                           else f"{pfx}wmax{14 + i}")
                        gq.dma_start(out=wstg[i % 2][:], in_=d["wf"][i]
                                     ).then_inc(wl[i % 2], 16)
                blk.gpsimd(_)
                tot["cc_s"] += 32
                tot["cc_c"] += 1
                CSX = tot["cc_s"]
                CCX = tot["cc_c"]
                WLB = [WLF[0] + 16 * 8, WLF[1] + 16 * 8]
                tot["wl"][0] += 16 * 8
                tot["wl"][1] += 16 * 8

                def _(v):
                    v.wait_ge(cc_s, CSX)
                    v.wait_ge(cc_c, CCX)
                    v.reciprocal(out=sca[:, 3:4], in_=sca[:, 1:2])
                    v.reciprocal(out=sca[:, 4:5], in_=sca[:, 2:3])
                    v.tensor_tensor(out=sca[:, 5:6], in0=sca[:, 1:2],
                                    in1=sca[:, 2:3], op=ALU.mult)
                    v.tensor_copy(out=sc_bf[:, 0:3], in_=sca[:, 3:6])
                    MKV(f"{pfx}scbf")
                blk.vector(_)

                def _(t):
                    WV(t, f"{pfx}scbf")
                    t.matmul(out=ps_c[:, 0:3], lhsT=ones_r[0:1, :],
                             rhs=sc_bf[0:1, 0:3], start=True,
                             stop=True).then_inc(t_f1, 1)
                blk.tensor(_)

                def _(sc):
                    sc.wait_ge(t_f1, 3)
                    sc.copy(out=sc3[:, 0:3], in_=ps_c[:, 0:3])
                    MKA(f"{pfx}sc3")
                    sc.wait_ge(f1c, 16)
                    sc.copy(out=bf_f[:, :], in_=bf_s[:, :])
                    MKA(f"{pfx}bff")
                blk.scalar(_)

                def _(v):
                    WA(v, f"{pfx}sc3")
                    for i in range(16):
                        v.wait_ge(wl[i % 2], WLB[i % 2] + 16 * (i // 2 + 1))
                        v.tensor_scalar_mul(out=wq[i][:], in0=wstg[i % 2][:],
                                            scalar1=sc3[:, 1:2])
                        MKV(f"{pfx}wq{i}")
                    for kc in range(16):
                        v.tensor_scalar_mul(out=xq[kc][:], in0=xh[kc][:],
                                            scalar1=sc3[:, 0:1])
                        MKV(f"{pfx}xq{kc}")
                blk.vector(_)

                def _(t):
                    WV(t, f"{pfx}xq15")
                    for mt in range(16):
                        WV(t, f"{pfx}wq{mt}")
                        if mt >= 2:
                            WV(t, f"{pfx}yev{mt - 2}")
                        ins = None
                        for kc in range(16):
                            for sgm in range(2):
                                ins = t.matmul(
                                    out=ps_m[mt % 2][:, 512 * sgm:512 * (sgm + 1)],
                                    lhsT=wq[mt][:, 128 * kc:128 * (kc + 1)],
                                    rhs=xq[kc][:, 512 * sgm:512 * (sgm + 1)],
                                    start=(kc == 0), stop=(kc == 15))
                        ins.then_inc(t_f1, 1)
                blk.tensor(_)

                def _(v):
                    WA(v, f"{pfx}bff")
                    for mt in range(16):
                        v.wait_ge(t_f1, 3 + mt + 1)
                        v.tensor_scalar_mul(out=ybf[:], in0=ps_m[mt % 2][:, :],
                                            scalar1=sc3[:, 2:3])
                        v.tensor_scalar_add(out=ybf[:], in0=ybf[:],
                                            scalar1=bf_f[:, mt:mt + 1])
                        v.tensor_tensor(out=x_f[mt][:], in0=x_f[mt][:],
                                        in1=ybf[:], op=ALU.add)
                        MKV(f"{pfx}yev{mt}")
                blk.vector(_)

        emit_ln("a0.", lambda sc: sc.wait_ge(xb_ld, 256))
        emit_attn(0, "a0.", lambda gq: None)

        def _(gq):
            for kc in range(16):
                WV(gq, f"a0.res{kc}")
                gq.tensor_copy(out=xh[kc][:], in_=x_f[kc][:])
                MKG(f"f0.xcp{kc}")
        blk.gpsimd(_)

        emit_ln("f0.", lambda sc: WG(sc, "f0.xcp15"))
        emit_ffn0("f0.")

        def _(gq):
            for kc in range(16):
                WV(gq, f"f0.dev{16 * 7 + kc}")
                gq.tensor_copy(out=xh[kc][:], in_=x_f[kc][:])
                MKG(f"a1.xcp{kc}")
        blk.gpsimd(_)

        emit_ln("a1.", lambda sc: WG(sc, "a1.xcp15"))
        emit_attn(1, "a1.", lambda gq: WG(gq, "a1.xcp15"))

        def _(gq):
            for kc in range(16):
                WV(gq, f"a1.res{kc}")
                gq.tensor_copy(out=xh[kc][:], in_=x_f[kc][:])
                MKG(f"f1.xcp{kc}")
        blk.gpsimd(_)

        emit_ln("f1.", lambda sc: WG(sc, "f1.xcp15"))
        emit_ffn1("f1.")

        def _(s):
            for mt in range(16):
                WV(s, f"f1.yev{mt}")
                s.dma_start(out=ydr[128 * mt:128 * (mt + 1), :],
                            in_=x_f[mt][:]).then_inc(
                                st_a if mt % 2 == 0 else st_b, 16)
            s.wait_ge(st_a, 16 * 8)
            s.wait_ge(st_b, 16 * 8)
        blk.sync(_)

    return nc



def prep_fused_inputs(inputs):
    x_full = np.ascontiguousarray(
        np.asarray(inputs["L0_attn_hidden_states"], np.float32))
    ro = l0_row_order()
    base = {}
    base["wq0"] = tiled_lhsT(np.ascontiguousarray(
        np.asarray(inputs["L0_attn_Wq"], np.float32)[ro].T))
    base["wk0"] = np.ascontiguousarray(
        np.asarray(inputs["L0_attn_Wk"], np.float32).T).reshape(16, 128, 512)
    base["wv0"] = np.ascontiguousarray(
        np.asarray(inputs["L0_attn_Wv"], np.float32).T).reshape(16, 128, 512)
    base["wo0"] = tiled_lhsT(np.ascontiguousarray(
        np.asarray(inputs["L0_attn_Wo"], np.float32)[:, ro].T))
    wq2 = np.asarray(inputs["L1_attn_Wq"],
                     np.float32).transpose(1, 0, 2).reshape(2048, 2048)
    wk2 = np.asarray(inputs["L1_attn_Wk"],
                     np.float32).transpose(1, 0, 2).reshape(2048, 512)
    wv2 = np.asarray(inputs["L1_attn_Wv"],
                     np.float32).transpose(1, 0, 2).reshape(2048, 512)
    base["wq1"] = tiled_lhsT(np.ascontiguousarray(wq2))
    base["wk1"] = np.ascontiguousarray(wk2).reshape(16, 128, 512)
    base["wv1"] = np.ascontiguousarray(wv2).reshape(16, 128, 512)
    bq = np.asarray(inputs["L1_attn_bq"], np.float32).reshape(2048)
    bk = np.asarray(inputs["L1_attn_bk"], np.float32).reshape(512)
    bv = np.asarray(inputs["L1_attn_bv"], np.float32).reshape(512)
    base["bq1"] = np.ascontiguousarray(bq.reshape(16, 128).T)
    base["bk1"] = np.ascontiguousarray(
        np.broadcast_to(bk.reshape(1, 512), (128, 512)))
    bvg = np.zeros((4, 129), np.float32)
    bvr = bv.reshape(4, 128)
    bvg[:, 0:64] = bvr[:, 0:64]
    bvg[:, 65:129] = bvr[:, 64:128]
    base["bv1"] = np.ascontiguousarray(
        np.broadcast_to(bvg.reshape(1, 516), (128, 516)))
    base["eq0"] = head_indicator_tiles(2048, 32)
    base["ek0"] = head_indicator_tiles(512, 8)
    base["eq1"] = head_indicator_tiles(2048, 16)
    base["ek1"] = head_indicator_tiles(512, 4)
    base["wg"] = tiled_lhsT(np.ascontiguousarray(
        u8_view(np.asarray(inputs["L0_ffn_gate_w"])).T))
    base["wu"] = tiled_lhsT(np.ascontiguousarray(
        u8_view(np.asarray(inputs["L0_ffn_up_w"])).T))
    base["wd"] = np.ascontiguousarray(
        u8_view(np.asarray(inputs["L0_ffn_down_w"])).T).reshape(64, 128, 2048)
    base["sg"] = np.ascontiguousarray(
        np.asarray(inputs["L0_ffn_gate_scale"], np.float32).reshape(64, 128).T)
    base["su"] = np.ascontiguousarray(
        np.asarray(inputs["L0_ffn_up_scale"], np.float32).reshape(64, 128).T)
    base["sd"] = np.ascontiguousarray(
        np.asarray(inputs["L0_ffn_down_scale"], np.float32).reshape(16, 128).T)
    wf = np.asarray(inputs["L1_ffn_weight"])
    base["wf"] = tiled_lhsT(np.ascontiguousarray(wf.T))
    base["bfb"] = np.ascontiguousarray(
        np.asarray(inputs["L1_ffn_bias"]).reshape(16, 128).T)
    base["ident"] = np.eye(128, dtype=np.float32)
    kk, qq = np.meshgrid(np.arange(128), np.arange(128), indexing="ij")
    tril = (qq >= kk).astype(np.float32)
    ckg, skg = rope_tables(np.arange(2048))
    tabk_g = np.ascontiguousarray(np.concatenate([ckg, skg], axis=1))

    in_maps = []
    for cid in range(NCORES):
        b, p = cid // 2, cid % 2
        perm = perm_for_core(p)
        own = perm[:1024]
        m = dict(base)
        m["x"] = np.ascontiguousarray(x_full[b].T[:, own])
        cq, sq_ = rope_tables(own)
        m["tabq0"] = np.ascontiguousarray(np.concatenate([cq, sq_], axis=1))
        m["tabk0"] = tabk_g
        if p == 0:
            m["maske"] = tril
            m["masko"] = np.zeros((128, 128), np.float32)
        else:
            m["maske"] = np.ones((128, 128), np.float32)
            m["masko"] = tril
        sidx = np.empty((128, 8), np.uint32)
        for tt in range(8):
            sidx[:, tt] = 128 * (p + 2 * tt) + np.arange(128)
        m["sidx"] = np.ascontiguousarray(sidx)
        in_maps.append(m)
    return in_maps, x_full


def gather_fused_output(results, x_full):
    out = np.empty_like(x_full)
    for cid in range(NCORES):
        b, p = cid // 2, cid % 2
        perm = perm_for_core(p)
        out[b, perm[:1024], :] = results[cid]["y"].T
    return out


def kernel(**inputs):
    key = "fused"
    if key not in _NC_CACHE:
        _NC_CACHE[key] = build_fused()
    in_maps, x_full = prep_fused_inputs(inputs)
    res = run_bass_kernel_spmd(_NC_CACHE[key], in_maps,
                               core_ids=list(range(NCORES)))
    return gather_fused_output(res.results, x_full)

def build_fused():
    marks = Marks()
    _build_fused(marks)
    marks.pass2 = True
    return _build_fused(marks)


def _build_fused(marks):
    nc = bass.Bass("TRN2", target_bir_lowering=False)
    nc.num_devices = 8

    xdr = nc.dram_tensor("x", [DM, TOKQ], F32, kind="ExternalInput").ap()
    ydr = nc.dram_tensor("y", [DM, TOKQ], F32, kind="ExternalOutput").ap()
    d = {}
    d["wq0"] = nc.dram_tensor("wq0", [16, 128, 2048], F32, kind="ExternalInput").ap()
    d["wk0"] = nc.dram_tensor("wk0", [16, 128, 512], F32, kind="ExternalInput").ap()
    d["wv0"] = nc.dram_tensor("wv0", [16, 128, 512], F32, kind="ExternalInput").ap()
    d["wo0"] = nc.dram_tensor("wo0", [16, 128, 2048], F32, kind="ExternalInput").ap()
    d["tabq0"] = nc.dram_tensor("tabq0", [128, 2048], F32, kind="ExternalInput").ap()
    d["tabk0"] = nc.dram_tensor("tabk0", [128, 4096], F32, kind="ExternalInput").ap()
    d["eq0"] = nc.dram_tensor("eq0", [128, 512], F32, kind="ExternalInput").ap()
    d["ek0"] = nc.dram_tensor("ek0", [128, 32], F32, kind="ExternalInput").ap()
    d["wq1"] = nc.dram_tensor("wq1", [16, 128, 2048], F32, kind="ExternalInput").ap()
    d["wk1"] = nc.dram_tensor("wk1", [16, 128, 512], F32, kind="ExternalInput").ap()
    d["wv1"] = nc.dram_tensor("wv1", [16, 128, 512], F32, kind="ExternalInput").ap()
    d["bq1"] = nc.dram_tensor("bq1", [128, 16], F32, kind="ExternalInput").ap()
    d["bk1"] = nc.dram_tensor("bk1", [128, 512], F32, kind="ExternalInput").ap()
    d["bv1"] = nc.dram_tensor("bv1", [128, 516], F32, kind="ExternalInput").ap()
    d["eq1"] = nc.dram_tensor("eq1", [128, 256], F32, kind="ExternalInput").ap()
    d["ek1"] = nc.dram_tensor("ek1", [128, 16], F32, kind="ExternalInput").ap()
    d["wg"] = nc.dram_tensor("wg", [64, 128, 2048], U8, kind="ExternalInput").ap()
    d["wu"] = nc.dram_tensor("wu", [64, 128, 2048], U8, kind="ExternalInput").ap()
    d["wd"] = nc.dram_tensor("wd", [64, 128, 2048], U8, kind="ExternalInput").ap()
    d["sg"] = nc.dram_tensor("sg", [128, 64], F32, kind="ExternalInput").ap()
    d["su"] = nc.dram_tensor("su", [128, 64], F32, kind="ExternalInput").ap()
    d["sd"] = nc.dram_tensor("sd", [128, 16], F32, kind="ExternalInput").ap()
    d["wf"] = nc.dram_tensor("wf", [16, 128, 2048], BF16, kind="ExternalInput").ap()
    d["bfb"] = nc.dram_tensor("bfb", [128, 16], BF16, kind="ExternalInput").ap()
    d["maske"] = nc.dram_tensor("maske", [128, 128], F32, kind="ExternalInput").ap()
    d["masko"] = nc.dram_tensor("masko", [128, 128], F32, kind="ExternalInput").ap()
    d["ident"] = nc.dram_tensor("ident", [128, 128], F32, kind="ExternalInput").ap()
    d["sidx"] = nc.dram_tensor("sidx", [128, 8], U32, kind="ExternalInput").ap()
    shr_k = [nc.dram_tensor(f"shr_k{l}", [2048, 512], BF16, kind="Internal",
                            addr_space="Shared").ap() for l in range(2)]
    shr_v = [nc.dram_tensor(f"shr_v{l}", [2048, 520], BF16, kind="Internal",
                            addr_space="Shared").ap() for l in range(2)]
    bar = nc.dram_tensor("bar", [1, 1], F32, kind="Internal").ap()
    cc_in = nc.dram_tensor("cc_in", [1, 1], F32, kind="Internal").ap()

    with ExitStack() as top:
        E = top.enter_context
        blk0 = E(nc.Block())

        x_f = [E(nc.sbuf_tensor(f"x{i}", [128, TOKQ], F32)) for i in range(16)]
        xh = [E(nc.sbuf_tensor(f"xh{i}", [128, TOKQ], BF16)) for i in range(16)]
        ones_c = E(nc.sbuf_tensor("ones_c", [128, 1], BF16))
        ones_r = E(nc.sbuf_tensor("ones_r", [1, 128], BF16))
        zrow = E(nc.sbuf_tensor("zrow", [1, 65], BF16))
        stat = E(nc.sbuf_tensor("stat", [1, 1537], F32))
        ab_bf = E(nc.sbuf_tensor("ab_bf", [1, 2048], BF16))
        abc = E(nc.sbuf_tensor("abc", [128, 1024], BF16))
        x2b = [E(nc.sbuf_tensor(f"x2b{i}", [128, 1024], BF16)) for i in range(2)]
        mk_e = E(nc.sbuf_tensor("mk_e", [128, 128], BF16))
        mk_o = E(nc.sbuf_tensor("mk_o", [128, 128], BF16))
        id_s = E(nc.sbuf_tensor("id_s", [128, 128], F32))
        bias_sb = E(nc.sbuf_tensor("bias_sb", [128, 1], F32))
        sidx_s = E(nc.sbuf_tensor("sidx_s", [128, 8], U32))
        zf = E(nc.sbuf_tensor("zf", [1, 1], F32))

        x_ld = E(nc.semaphore("x_ld"))
        xb_ld = E(nc.semaphore("xb_ld"))
        s_ld = E(nc.semaphore("s_ld"))
        m_ld = E(nc.semaphore("m_ld"))
        ca = [E(nc.semaphore(f"ca{l}")) for l in range(2)]
        cv = [E(nc.semaphore(f"cv{l}")) for l in range(2)]
        b_ld = E(nc.semaphore("b_ld"))
        f0c = E(nc.semaphore("f0c"))
        f1c = E(nc.semaphore("f1c"))
        wl = [E(nc.semaphore(f"wl{i}")) for i in range(4)]
        d_lda = E(nc.semaphore("d_lda"))
        sh_a = E(nc.semaphore("sh_a"))
        sh_b = E(nc.semaphore("sh_b"))
        rb_s = E(nc.semaphore("rb_s"))
        st_a = E(nc.semaphore("st_a"))
        st_b = E(nc.semaphore("st_b"))
        cc_s = E(nc.semaphore("cc_s"))
        cc_c = E(nc.semaphore("cc_c"))
        bz = E(nc.semaphore("bz"))
        t_ln = E(nc.semaphore("t_ln"))
        sems_attn = {}
        for pfx in ("a0.", "a1."):
            for nm in ("t_kv", "t_q", "t_nb", "t_s", "t_av", "t_bc", "t_p5"):
                sems_attn[pfx + nm] = E(nc.semaphore(pfx + nm))
        t_mm = E(nc.semaphore("t_mm"))
        t_dn = E(nc.semaphore("t_dn"))
        t_f1 = E(nc.semaphore("t_f1"))
        v_self = E(nc.semaphore("v_self"))
        a_self = E(nc.semaphore("a_self"))
        g_self = E(nc.semaphore("g_self"))
        SV = {"sem": v_self, "n": 0}
        SA = {"sem": a_self, "n": 0}
        SG = {"sem": g_self, "n": 0}
        _vw, _aw, _gw = SerialEng(SV), SerialEng(SA), SerialEng(SG)

        class blk:
            tensor = staticmethod(blk0.tensor)
            sync = staticmethod(blk0.sync)
            vector = staticmethod(lambda f: blk0.vector(lambda e: f(_vw.bind(e))))
            scalar = staticmethod(lambda f: blk0.scalar(lambda e: f(_aw.bind(e))))
            gpsimd = staticmethod(lambda f: blk0.gpsimd(lambda e: f(_gw.bind(e))))

        def MKV(n):
            marks.mark(SV, n)

        def MKA(n):
            marks.mark(SA, n)

        def MKG(n):
            marks.mark(SG, n)

        def WV(eng, n):
            val = marks.get(n)
            if val > 0:
                eng.wait_ge(v_self, val)

        def WA(eng, n):
            val = marks.get(n)
            if val > 0:
                eng.wait_ge(a_self, val)

        def WG(eng, n):
            val = marks.get(n)
            if val > 0:
                eng.wait_ge(g_self, val)

        tot = {"sh": [0, 0], "rb": 0, "wl": [0, 0, 0, 0], "d": 0,
               "cc_c": 0, "cc_s": 0, "tln": 0}
        mkc = [mk_e if c % 2 == 0 else mk_o for c in range(16)]

        def _(s):
            for kc in range(16):
                s.dma_start(out=x_f[kc][:],
                            in_=xdr[128 * kc:128 * (kc + 1), :]).then_inc(x_ld, 16)
            s.dma_start(out=id_s[:], in_=d["ident"]).then_inc(s_ld, 16)
            s.dma_start(out=sidx_s[:], in_=d["sidx"]).then_inc(s_ld, 16)
        blk.sync(_)

        def _(gq):
            for kc in range(16):
                gq.dma_start(out=xh[kc][:],
                             in_=xdr[128 * kc:128 * (kc + 1), :]).then_inc(xb_ld, 16)
            gq.dma_start(out=mk_e[:], in_=d["maske"]).then_inc(m_ld, 16)
            gq.dma_start(out=mk_o[:], in_=d["masko"]).then_inc(m_ld, 16)
        blk.gpsimd(_)

        def _(v):
            v.memset(ones_c[:], 1.0)
            v.memset(ones_r[:], 1.0)
            v.memset(zrow[:], 0.0)
            v.memset(zf[:], 0.0)
            MKV("ones")
        blk.vector(_)

        def _(gq):
            WV(gq, "ones")
            gq.dma_start(out=bar, in_=zf[:]).then_inc(bz, 16)
        blk.gpsimd(_)

        def emit_ln(pfx, gate_scalar):
            nsg = 2
            tb = tot["tln"]
            with ExitStack() as lnp:
                le = lnp.enter_context
                lnps = [le(nc.psum_tensor(f"{pfx}lnps{i}", [1, 512], F32))
                        for i in range(2)]
                lnpq = [le(nc.psum_tensor(f"{pfx}lnpq{i}", [1, 512], F32))
                        for i in range(2)]
                lnbc = [le(nc.psum_tensor(f"{pfx}lnbc{i}", [128, 512], F32))
                        for i in range(2)]

                def _(sc):
                    gate_scalar(sc)
                    for kc in range(16):
                        if kc >= 2:
                            sc.wait_ge(t_ln, tb + kc - 1)
                        sc.activation(out=x2b[kc % 2][:], in_=xh[kc][:],
                                      func=AF.Square)
                        MKA(f"{pfx}sq{kc}")
                blk.scalar(_)

                def _(t):
                    WV(t, "ones")
                    for kc in range(16):
                        WA(t, f"{pfx}sq{kc}")
                        for sg_ in range(nsg):
                            t.matmul(out=lnps[sg_][0:1, :], lhsT=ones_c[:, 0:1],
                                     rhs=xh[kc][:, 512 * sg_:512 * (sg_ + 1)],
                                     start=(kc == 0), stop=(kc == 15))
                        ins = None
                        for sg_ in range(nsg):
                            ins = t.matmul(out=lnpq[sg_][0:1, :],
                                           lhsT=ones_c[:, 0:1],
                                           rhs=x2b[kc % 2][:, 512 * sg_:512 * (sg_ + 1)],
                                           start=(kc == 0), stop=(kc == 15))
                        ins.then_inc(t_ln, 1)
                blk.tensor(_)

                eps_r = stat[:, 1536:1537]
                mu_r = stat[:, 0:512]
                var_r = stat[:, 512:1024]
                sc_r = stat[:, 1024:1536]

                def _(v):
                    if pfx == "a0.":
                        v.memset(eps_r, EPS_LN)
                    v.wait_ge(t_ln, tb + 16)
                    for sg_ in range(nsg):
                        v.tensor_scalar_mul(out=mu_r, in0=lnps[sg_][0:1, :],
                                            scalar1=1.0 / DM)
                        v.tensor_scalar_mul(out=var_r, in0=lnpq[sg_][0:1, :],
                                            scalar1=1.0 / DM)
                        v.tensor_tensor(out=sc_r, in0=mu_r, in1=mu_r,
                                        op=ALU.mult)
                        v.tensor_tensor(out=var_r, in0=var_r, in1=sc_r,
                                        op=ALU.subtract)
                        MKV(f"{pfx}prep{sg_}")
                        WA(v, f"{pfx}rstd{sg_}")
                        v.tensor_tensor(out=sc_r, in0=mu_r, in1=var_r,
                                        op=ALU.mult)
                        v.tensor_scalar_mul(out=sc_r, in0=sc_r, scalar1=-1.0)
                        v.tensor_copy(out=ab_bf[:, 512 * sg_:512 * (sg_ + 1)],
                                      in_=var_r)
                        v.tensor_copy(
                            out=ab_bf[:, 1024 + 512 * sg_:1024 + 512 * (sg_ + 1)],
                            in_=sc_r)
                        MKV(f"{pfx}ab{sg_}")
                blk.vector(_)

                def _(sc):
                    for sg_ in range(nsg):
                        WV(sc, f"{pfx}prep{sg_}")
                        sc.activation(out=var_r, in_=var_r, func=AF.Ln,
                                      bias=eps_r)
                        sc.activation(out=var_r, in_=var_r, func=AF.Exp,
                                      scale=-0.5)
                        MKA(f"{pfx}rstd{sg_}")
                blk.scalar(_)

                def _(t):
                    for half in range(2):
                        if half == 1:
                            WV(t, f"{pfx}xh_mul")
                        for sg_ in range(nsg):
                            idx = half * nsg + sg_
                            WV(t, f"{pfx}ab{sg_}" if half == 0
                               else f"{pfx}ab{nsg - 1}")
                            if idx >= 2:
                                WA(t, f"{pfx}bce{idx - 2}")
                            t.matmul(out=lnbc[sg_ % 2][:, :],
                                     lhsT=ones_r[0:1, :],
                                     rhs=ab_bf[:, 512 * idx:512 * (idx + 1)],
                                     start=True, stop=True).then_inc(t_ln, 1)
                blk.tensor(_)

                def _(sc):
                    for half in range(2):
                        for sg_ in range(nsg):
                            idx = half * nsg + sg_
                            sc.wait_ge(t_ln, tb + 16 + idx + 1)
                            if half == 1 and sg_ == 0:
                                WV(sc, f"{pfx}xh_mul")
                            sc.copy(out=abc[:, 512 * sg_:512 * (sg_ + 1)],
                                    in_=lnbc[sg_ % 2][:, :])
                            MKA(f"{pfx}bce{idx}")
                blk.scalar(_)

                def _(v):
                    for half, op in ((0, ALU.mult), (1, ALU.add)):
                        WA(v, f"{pfx}bce{half * nsg + nsg - 1}")
                        for kc in range(16):
                            v.tensor_tensor(out=xh[kc][:], in0=xh[kc][:],
                                            in1=abc[:, 0:1024], op=op)
                        MKV(f"{pfx}xh_mul" if half == 0 else f"{pfx}xh_done")
                blk.vector(_)
            tot["tln"] += 20

        def emit_attn(layer, pfx, gate_g):
            if layer == 0:
                NH, NKV, HD = 32, 8, 64
                rope, bias_qkv, outproj = True, False, True
                scale = 0.125
                wqd_, wkd_, wvd_, wod_ = d["wq0"], d["wk0"], d["wv0"], d["wo0"]
            else:
                NH, NKV, HD = 16, 4, 128
                rope, bias_qkv, outproj = False, True, False
                scale = 1.0 / float(np.sqrt(128.0))
                wqd_, wkd_, wvd_ = d["wq1"], d["wk1"], d["wv1"]
                wod_ = None
            VG = HD + 1
            shk, shv = shr_k[layer], shr_v[layer]
            stor_heads = l0_head_order() if layer == 0 else list(range(NH))
            t_kv = sems_attn[pfx + "t_kv"]
            t_q = sems_attn[pfx + "t_q"]
            t_nb = sems_attn[pfx + "t_nb"]
            t_s = sems_attn[pfx + "t_s"]
            t_av = sems_attn[pfx + "t_av"]
            t_bc = sems_attn[pfx + "t_bc"]
            t_p5 = sems_attn[pfx + "t_p5"]

            with ExitStack() as ph:
                e = ph.enter_context
                qsb = [e(nc.sbuf_tensor(f"{pfx}q{i}", [128, TOKQ], BF16))
                       for i in range(16)]
                vsb = [e(nc.sbuf_tensor(f"{pfx}v{i}", [128, NKV * VG], BF16))
                       for i in range(16)]
                rp = [e(nc.sbuf_tensor(f"{pfx}rp{i}", [128, 2048], BF16))
                      for i in range(2)]
                wbuf = [e(nc.sbuf_tensor(f"{pfx}wb{i}", [128, DM], BF16))
                        for i in range(2)]
                if rope:
                    tabq = e(nc.sbuf_tensor(f"{pfx}tq", [128, 2048], BF16))
                if bias_qkv:
                    bq_s = e(nc.sbuf_tensor(f"{pfx}bq", [128, 16], F32))
                eq_s = e(nc.sbuf_tensor(f"{pfx}eq", [128, 16 * NH], BF16))
                ek_s = e(nc.sbuf_tensor(f"{pfx}ek", [128, 4 * NKV], BF16))

                def _(gq):
                    gate_g(gq)
                    gq.dma_start(out=eq_s[:], in_=d[f"eq{layer}"]).then_inc(ca[layer], 16)
                    gq.dma_start(out=ek_s[:], in_=d[f"ek{layer}"]).then_inc(ca[layer], 16)
                    if rope:
                        gq.dma_start(out=tabq[:], in_=d["tabq0"]).then_inc(ca[layer], 16)
                blk.gpsimd(_)
                CT_A = 16 * (3 if rope else 2)

                if bias_qkv:
                    def _(s):
                        WG(s, f"{pfx}xcp15")
                        s.dma_start(out=bq_s[:], in_=d["bq1"]).then_inc(b_ld, 16)
                    blk.sync(_)

                pw = ph.enter_context(ExitStack())
                we = pw.enter_context
                wkv = [we(nc.sbuf_tensor(f"{pfx}wkv{i}", [128, 512], BF16))
                       for i in range(16)]
                stg_k = [we(nc.sbuf_tensor(f"{pfx}sk{i}", [128, 512], BF16))
                         for i in range(2)]
                stg_v = [we(nc.sbuf_tensor(f"{pfx}sv{i}", [128, 520], BF16))
                         for i in range(2)]
                if bias_qkv:
                    bk_r = we(nc.sbuf_tensor(f"{pfx}bkr", [128, 512], BF16))
                    bv_r = we(nc.sbuf_tensor(f"{pfx}bvr", [128, 516], BF16))
                ps_kv = [we(nc.psum_tensor(f"{pfx}pkv{i}", [128, 512], F32))
                         for i in range(2)]

                def _(gq):
                    if bias_qkv:
                        gq.dma_start(out=bk_r[:], in_=d["bk1"]).then_inc(ca[layer], 16)
                        gq.dma_start(out=bv_r[:], in_=d["bv1"]).then_inc(ca[layer], 16)
                    for kc in range(16):
                        gq.dma_start(out=wkv[kc][:],
                                     in_=wkd_[kc]).then_inc(ca[layer], 16)
                blk.gpsimd(_)
                CT_K = CT_A + 256 + (32 if bias_qkv else 0)

                def _(t):
                    WV(t, f"{pfx}xh_done")
                    t.wait_ge(ca[layer], CT_K)
                    for tt in range(8):
                        if tt >= 2:
                            if bias_qkv:
                                WV(t, f"{pfx}bk{tt - 2}")
                            else:
                                WA(t, f"{pfx}evk{tt - 2}")
                        ins = None
                        for kc in range(16):
                            ins = t.matmul(
                                out=ps_kv[tt % 2][:, :],
                                lhsT=xh[kc][:, 128 * tt:128 * (tt + 1)],
                                rhs=wkv[kc][:, :],
                                start=(kc == 0), stop=(kc == 15))
                        ins.then_inc(t_kv, 1)
                blk.tensor(_)

                SH0 = [tot["sh"][0], tot["sh"][1]]

                def _(sc):
                    for tt in range(8):
                        sc.wait_ge(t_kv, tt + 1)
                        if tt >= 2:
                            sc.wait_ge(sh_a if tt % 2 == 0 else sh_b,
                                       SH0[tt % 2] + 16 * (tt // 2))
                        sc.copy(out=stg_k[tt % 2][:], in_=ps_kv[tt % 2][:, :])
                        MKA(f"{pfx}evk{tt}")
                blk.scalar(_)

                if bias_qkv:
                    def _(v):
                        v.wait_ge(ca[layer], CT_K)
                        for tt in range(8):
                            WA(v, f"{pfx}evk{tt}")
                            v.tensor_tensor(out=stg_k[tt % 2][:],
                                            in0=stg_k[tt % 2][:],
                                            in1=bk_r[:], op=ALU.add)
                            MKV(f"{pfx}bk{tt}")
                    blk.vector(_)

                def _(gq):
                    gq.wait_ge(s_ld, 32)
                    for tt in range(8):
                        if bias_qkv:
                            WV(gq, f"{pfx}bk{tt}")
                        else:
                            WA(gq, f"{pfx}evk{tt}")
                        gq.indirect_dma_start(
                            out=shk,
                            out_offset=bass.IndirectOffsetOnAxis(
                                ap=sidx_s[:, tt:tt + 1], axis=0),
                            in_=stg_k[tt % 2][:],
                            in_offset=None).then_inc(
                                sh_a if tt % 2 == 0 else sh_b, 16)
                blk.gpsimd(_)
                tot["sh"][0] += 16 * 4
                tot["sh"][1] += 16 * 4
                SHK = [tot["sh"][0], tot["sh"][1]]

                def _(gq):
                    gq.wait_ge(t_kv, 8)
                    for kc in range(16):
                        gq.dma_start(out=wkv[kc][:],
                                     in_=wvd_[kc]).then_inc(cv[layer], 16)
                blk.gpsimd(_)
                CT_V = 256

                def _(v):
                    WV(v, "ones")
                    v.memset(stg_v[0][:], 1.0)
                    v.memset(stg_v[1][:], 1.0)
                    MKV(f"{pfx}vst1")
                blk.vector(_)

                def _(t):
                    t.wait_ge(cv[layer], CT_V)
                    for tt in range(8):
                        if tt >= 2:
                            if bias_qkv:
                                WV(t, f"{pfx}bv{tt - 2}")
                            else:
                                WA(t, f"{pfx}evv{tt - 2}")
                        ins = None
                        for kc in range(16):
                            ins = t.matmul(
                                out=ps_kv[tt % 2][:, :],
                                lhsT=xh[kc][:, 128 * tt:128 * (tt + 1)],
                                rhs=wkv[kc][:, :],
                                start=(kc == 0), stop=(kc == 15))
                        ins.then_inc(t_kv, 1)
                blk.tensor(_)

                def _(sc):
                    WV(sc, f"{pfx}vst1")
                    for tt in range(8):
                        sc.wait_ge(t_kv, 8 + tt + 1)
                        if tt >= 2:
                            sc.wait_ge(sh_a if tt % 2 == 0 else sh_b,
                                       SHK[tt % 2] + 16 * (tt // 2))
                        src = ps_kv[tt % 2][:, :].rearrange(
                            "p (g dd) -> p g dd", g=NKV)
                        if HD == 64:
                            dst = stg_v[tt % 2][:, 0:NKV * VG].rearrange(
                                "p (g w) -> p g w", g=NKV)[:, :, 0:64]
                            sc.copy(out=dst, in_=src)
                        else:
                            src2 = ps_kv[tt % 2][:, :].rearrange(
                                "p (g s2 dd) -> p g s2 dd", g=NKV, s2=2)
                            gv = stg_v[tt % 2][:, 0:NKV * VG].rearrange(
                                "p (g w) -> p g w", g=NKV)
                            sc.copy(out=gv[:, :, 0:64], in_=src2[:, :, 0, :])
                            sc.copy(out=gv[:, :, 65:129], in_=src2[:, :, 1, :])
                        MKA(f"{pfx}evv{tt}")
                blk.scalar(_)

                if bias_qkv:
                    def _(v):
                        for tt in range(8):
                            WA(v, f"{pfx}evv{tt}")
                            v.tensor_tensor(out=stg_v[tt % 2][:, 0:NKV * VG],
                                            in0=stg_v[tt % 2][:, 0:NKV * VG],
                                            in1=bv_r[:], op=ALU.add)
                            MKV(f"{pfx}bv{tt}")
                    blk.vector(_)

                def _(gq):
                    for tt in range(8):
                        if bias_qkv:
                            WV(gq, f"{pfx}bv{tt}")
                        else:
                            WA(gq, f"{pfx}evv{tt}")
                        gq.indirect_dma_start(
                            out=shv,
                            out_offset=bass.IndirectOffsetOnAxis(
                                ap=sidx_s[:, tt:tt + 1], axis=0),
                            in_=stg_v[tt % 2][:],
                            in_offset=None).then_inc(
                                sh_a if tt % 2 == 0 else sh_b, 16)
                    gq.wait_ge(sh_a, SHK[0] + 16 * 4)
                    gq.wait_ge(sh_b, SHK[1] + 16 * 4)
                    gq.wait_ge(bz, 16)
                    gq.collective_compute(
                        "AllReduce", ALU.add,
                        replica_groups=[[0, 1], [2, 3], [4, 5], [6, 7]],
                        ins=[bar], outs=[bar]).then_inc(cc_c, 1)
                blk.gpsimd(_)
                tot["sh"][0] += 16 * 4
                tot["sh"][1] += 16 * 4
                tot["cc_c"] += 1
                AR1 = tot["cc_c"]

                pw.close()
                kbig = e(nc.sbuf_tensor(f"{pfx}kb", [128, 8192], BF16))
                if rope:
                    ptk = ph.enter_context(ExitStack())
                    tabk = ptk.enter_context(
                        nc.sbuf_tensor(f"{pfx}tk", [128, 4096], BF16))

                def kb(r0, r1, kt, c0, c1):
                    return kbig[r0:r1, 2048 * kt + c0:2048 * kt + c1]

                if rope:
                    def _(gq):
                        gq.dma_start(out=tabk[:],
                                     in_=d["tabk0"]).then_inc(ca[layer], 16)
                    blk.gpsimd(_)
                CT_K2 = CT_K + (16 if rope else 0)

                RB0 = tot["rb"]

                def _(s):
                    s.wait_ge(cc_c, AR1)
                    for j in range(16):
                        s.dma_start(out=vsb[j][:, 0:NKV * VG],
                                    in_=shv[128 * j:128 * (j + 1),
                                            0:NKV * VG]).then_inc(rb_s, 16)
                    s.wait_ge(rb_s, RB0 + 256)
                    s.wait_ge(x_ld, 256)
                    s.wait_ge(s_ld, 32)
                    dst = kbig[:].rearrange("p (m f) -> p m f", m=4)
                    s.dma_start(out=dst, in_=shk,
                                transpose=True).then_inc(rb_s, 16)
                    s.wait_ge(rb_s, RB0 + 272)
                blk.sync(_)
                tot["rb"] += 272
                RB_T = tot["rb"]

                with ExitStack() as pq:
                    qe = pq.enter_context
                    ps_p = [qe(nc.psum_tensor(f"{pfx}pp{i}", [128, TOKQ], F32))
                            for i in range(2)]
                    WLQ = [tot["wl"][0], tot["wl"][1]]

                    def _(gq):
                        for i in range(16):
                            if i >= 2:
                                gq.wait_ge(t_q, i - 1)
                            gq.dma_start(out=wbuf[i % 2][:],
                                         in_=wqd_[i]).then_inc(wl[i % 2], 16)
                    blk.gpsimd(_)
                    tot["wl"][0] += 16 * 8
                    tot["wl"][1] += 16 * 8

                    def _(t):
                        for mt in range(16):
                            t.wait_ge(wl[mt % 2],
                                      WLQ[mt % 2] + 16 * (mt // 2 + 1))
                            if mt >= 2:
                                if rope:
                                    WV(t, f"{pfx}ropeq{mt - 2}")
                                else:
                                    WA(t, f"{pfx}evq{mt - 2}")
                            ins = None
                            for kc in range(16):
                                for sgm in range(2):
                                    ins = t.matmul(
                                        out=ps_p[mt % 2][:, 512 * sgm:512 * (sgm + 1)],
                                        lhsT=wbuf[mt % 2][:, 128 * kc:128 * (kc + 1)],
                                        rhs=xh[kc][:, 512 * sgm:512 * (sgm + 1)],
                                        start=(kc == 0), stop=(kc == 15))
                            ins.then_inc(t_q, 1)
                    blk.tensor(_)

                    def _(sc):
                        if bias_qkv:
                            sc.wait_ge(b_ld, 16)
                        for mt in range(16):
                            sc.wait_ge(t_q, mt + 1)
                            if rope:
                                if mt >= 2:
                                    WV(sc, f"{pfx}ropeq{mt - 2}")
                                p = ps_p[mt % 2]
                                r1 = rp[mt % 2]
                                sc.copy(out=qsb[mt][:], in_=p[:, :])
                                sc.copy(out=r1[0:32, 0:TOKQ], in_=p[32:64, :])
                                sc.copy(out=r1[32:64, 0:TOKQ], in_=p[0:32, :])
                                sc.copy(out=r1[64:96, 0:TOKQ], in_=p[96:128, :])
                                sc.copy(out=r1[96:128, 0:TOKQ], in_=p[64:96, :])
                            else:
                                sc.activation(out=qsb[mt][:],
                                              in_=ps_p[mt % 2][:, :],
                                              func=AF.Identity,
                                              bias=bq_s[:, mt:mt + 1])
                            MKA(f"{pfx}evq{mt}")
                    blk.scalar(_)

                    if rope:
                        def _(v):
                            v.wait_ge(ca[layer], CT_K)
                            for mt in range(16):
                                WA(v, f"{pfx}evq{mt}")
                                r1 = rp[mt % 2]
                                v.tensor_tensor(out=qsb[mt][:], in0=qsb[mt][:],
                                                in1=tabq[:, 0:1024],
                                                op=ALU.mult)
                                v.tensor_tensor(out=r1[:, 0:TOKQ],
                                                in0=r1[:, 0:TOKQ],
                                                in1=tabq[:, 1024:2048],
                                                op=ALU.mult)
                                v.tensor_tensor(out=qsb[mt][:], in0=qsb[mt][:],
                                                in1=r1[:, 0:TOKQ], op=ALU.add)
                                MKV(f"{pfx}ropeq{mt}")
                        blk.vector(_)

                if rope:
                    def _(sc):
                        sc.wait_ge(rb_s, RB_T)
                        for j in range(4):
                            if j >= 2:
                                WV(sc, f"{pfx}ropek{j - 2}")
                            else:
                                WV(sc, f"{pfx}ropeq{14 + j}")
                            r1 = rp[j % 2]
                            sc.copy(out=r1[0:32, :], in_=kb(32, 64, j, 0, 2048))
                            sc.copy(out=r1[32:64, :], in_=kb(0, 32, j, 0, 2048))
                            sc.copy(out=r1[64:96, :], in_=kb(96, 128, j, 0, 2048))
                            sc.copy(out=r1[96:128, :], in_=kb(64, 96, j, 0, 2048))
                            MKA(f"{pfx}swk{j}")
                    blk.scalar(_)

                    def _(v):
                        v.wait_ge(ca[layer], CT_K2)
                        for j in range(4):
                            WA(v, f"{pfx}swk{j}")
                            r1 = rp[j % 2]
                            kj = kb(0, 128, j, 0, 2048)
                            v.tensor_tensor(out=kj, in0=kj,
                                            in1=tabk[:, 0:2048], op=ALU.mult)
                            v.tensor_tensor(out=r1[:, :], in0=r1[:, :],
                                            in1=tabk[:, 2048:4096],
                                            op=ALU.mult)
                            v.tensor_tensor(out=kj, in0=kj, in1=r1[:, :],
                                            op=ALU.add)
                            MKV(f"{pfx}ropek{j}")
                    blk.vector(_)
                    ptk.close()

                with ExitStack() as pp:
                    ee = pp.enter_context
                    ps_n = ee(nc.psum_tensor(f"{pfx}psn", [NH, TOKQ], F32))
                    ps_nk = ee(nc.psum_tensor(f"{pfx}psk", [NKV, 2048], F32))
                    ps_tr = ee(nc.psum_tensor(f"{pfx}ptr", [1, 128], F32))
                    ps_bb = ee(nc.psum_tensor(f"{pfx}pbb", [128, 1], F32))
                    nb = ee(nc.sbuf_tensor(f"{pfx}nb", [NH, 2], F32))
                    nbk = ee(nc.sbuf_tensor(f"{pfx}nbk", [NKV, 2], F32))
                    nbs = ee(nc.sbuf_tensor(f"{pfx}nbs", [1, 8], F32))
                    nb_bf = ee(nc.sbuf_tensor(f"{pfx}nbbf", [1, 1], BF16))

                    def _(sc):
                        if rope:
                            WV(sc, f"{pfx}ropek3")
                        else:
                            sc.wait_ge(rb_s, RB_T)
                        for mt in range(16):
                            if mt >= 2:
                                sc.wait_ge(t_nb, mt - 1)
                            sc.activation(out=rp[mt % 2][:, 0:TOKQ],
                                          in_=qsb[mt][:], func=AF.Square)
                            MKA(f"{pfx}nsq{mt}")
                        for j in range(4):
                            sc.wait_ge(t_nb, 15 + j)
                            sc.activation(out=rp[j % 2][:, :],
                                          in_=kb(0, 128, j, 0, 2048),
                                          func=AF.Square)
                            MKA(f"{pfx}nsk{j}")
                    blk.scalar(_)

                    def _(t):
                        for mt in range(16):
                            WA(t, f"{pfx}nsq{mt}")
                            ins = None
                            for sgm in range(2):
                                ins = t.matmul(
                                    out=ps_n[:, 512 * sgm:512 * (sgm + 1)],
                                    lhsT=eq_s[:, NH * mt:NH * (mt + 1)],
                                    rhs=rp[mt % 2][:, 512 * sgm:512 * (sgm + 1)],
                                    start=(mt == 0), stop=(mt == 15))
                            ins.then_inc(t_nb, 1)
                        for j in range(4):
                            WA(t, f"{pfx}nsk{j}")
                            ins = None
                            for sgm in range(4):
                                ins = t.matmul(
                                    out=ps_nk[:, 512 * sgm:512 * (sgm + 1)],
                                    lhsT=ek_s[:, NKV * j:NKV * (j + 1)],
                                    rhs=rp[j % 2][:, 512 * sgm:512 * (sgm + 1)],
                                    start=(j == 0), stop=(j == 3))
                            ins.then_inc(t_nb, 1)
                    blk.tensor(_)

                    def _(v):
                        v.wait_ge(t_nb, 20)
                        v.tensor_reduce(out=nb[:, 0:1], in_=ps_n[:, :],
                                        axis=AX.X, op=ALU.max)
                        v.tensor_reduce(out=nbk[:, 0:1], in_=ps_nk[:, :],
                                        axis=AX.X, op=ALU.max)
                        MKV(f"{pfx}nred1")
                    blk.vector(_)

                    def _(t):
                        WV(t, f"{pfx}nred1")
                        t.wait_ge(s_ld, 32)
                        t.transpose(out=ps_tr[0:1, 0:NH], in_=nb[:, 0:1],
                                    identity=id_s[0:NH, 0:NH])
                        t.transpose(out=ps_tr[0:1, 64:64 + NKV], in_=nbk[:, 0:1],
                                    identity=id_s[0:NKV, 0:NKV]).then_inc(t_nb, 1)
                    blk.tensor(_)

                    def _(v):
                        v.wait_ge(t_nb, 21)
                        v.tensor_reduce(out=nbs[:, 0:1], in_=ps_tr[0:1, 0:NH],
                                        axis=AX.X, op=ALU.max)
                        v.tensor_reduce(out=nbs[:, 1:2],
                                        in_=ps_tr[0:1, 64:64 + NKV],
                                        axis=AX.X, op=ALU.max)
                        v.tensor_tensor(out=nbs[:, 2:3], in0=nbs[:, 0:1],
                                        in1=nbs[:, 1:2], op=ALU.mult)
                        MKV(f"{pfx}nred2")
                    blk.vector(_)

                    def _(sc):
                        WV(sc, f"{pfx}nred2")
                        sc.activation(out=nbs[:, 3:4], in_=nbs[:, 2:3],
                                      func=AF.Ln)
                        sc.activation(out=nbs[:, 4:5], in_=nbs[:, 3:4],
                                      func=AF.Exp, scale=0.5)
                        MKA(f"{pfx}nsqrt")
                    blk.scalar(_)

                    def _(v):
                        WA(v, f"{pfx}nsqrt")
                        v.tensor_scalar_mul(out=nbs[:, 5:6], in0=nbs[:, 4:5],
                                            scalar1=-scale)
                        v.tensor_copy(out=nb_bf[:, :], in_=nbs[:, 5:6])
                        MKV(f"{pfx}nbf")
                    blk.vector(_)

                    def _(t):
                        WV(t, f"{pfx}nbf")
                        t.matmul(out=ps_bb[:, 0:1], lhsT=ones_r[0:1, :],
                                 rhs=nb_bf[0:1, 0:1], start=True,
                                 stop=True).then_inc(t_nb, 1)
                    blk.tensor(_)

                    def _(sc):
                        sc.wait_ge(t_nb, 22)
                        sc.copy(out=bias_sb[:, :], in_=ps_bb[:, :])
                        MKA(f"{pfx}bias")
                    blk.scalar(_)

                of = qsb
                with ExitStack() as pp:
                    ee = pp.enter_context
                    pt = x2b
                    rc_b = ee(nc.sbuf_tensor(f"{pfx}rcb", [1, TOKQ], BF16))
                    ob_a = [ee(nc.sbuf_tensor(f"{pfx}oa{i}", [65, TOKQ], BF16))
                            for i in range(2)]
                    if layer != 0:
                        ob_b = [ee(nc.sbuf_tensor(f"{pfx}ob{i}", [64, TOKQ],
                                                  BF16)) for i in range(2)]
                    bb_s = ee(nc.sbuf_tensor(f"{pfx}bbs", [HD, TOKQ], BF16))
                    ns_s = 2 if layer == 0 else 1
                    ps_s = [ee(nc.psum_tensor(f"{pfx}pss{i}", [128, TOKQ], F32))
                            for i in range(ns_s)]
                    if layer == 0:
                        ps_o = ee(nc.psum_tensor(f"{pfx}pso", [HD + 1, TOKQ], F32))
                    else:
                        ps_o1 = ee(nc.psum_tensor(f"{pfx}po1", [65, TOKQ], F32))
                        ps_o2 = ee(nc.psum_tensor(f"{pfx}po2", [64, TOKQ], F32))
                    ps_b = ee(nc.psum_tensor(f"{pfx}psb", [HD, TOKQ], F32))

                    def segs_for(s0):
                        a = 128 * s0
                        if a < 512:
                            return [(a, 512), (512, 1024)]
                        return [(a, 1024)]

                    def _(t):
                        WA(t, f"{pfx}bias")
                        for hs in range(NH):
                            h_orig = stor_heads[hs]
                            g = h_orig // (NH // NKV)
                            if HD == 64:
                                qt, qr = hs // 2, 64 * (hs % 2)
                            else:
                                qt, qr = hs, 0
                            kt, kr = (g * HD) // 128, (g * HD) % 128

                            def emit_S(c):
                                s0 = c // 2
                                idx = 16 * hs + c
                                if idx >= ns_s:
                                    WA(t, f"{pfx}exp{idx - ns_s}")
                                ins = None
                                for (a, b) in segs_for(s0):
                                    ins = t.matmul(
                                        out=ps_s[idx % ns_s][:, a:b],
                                        lhsT=kb(kr, kr + HD, kt,
                                                128 * c, 128 * (c + 1)),
                                        rhs=qsb[qt][qr:qr + HD, a:b],
                                        start=True, stop=True)
                                ins.then_inc(t_s, 1)

                            def emit_AV(c):
                                s0 = c // 2
                                idx = 16 * hs + c
                                WV(t, f"{pfx}msk{idx}")
                                if c == 0 and hs >= 1:
                                    WA(t, f"{pfx}oev{hs - 1}")
                                ins = None
                                for (a, b) in segs_for(s0):
                                    if layer == 0:
                                        ins = t.matmul(
                                            out=ps_o[:, a:b],
                                            lhsT=vsb[c][:, VG * g:VG * g + 65],
                                            rhs=pt[idx % 2][:, a:b],
                                            start=(c == 0), stop=False)
                                    else:
                                        t.matmul(
                                            out=ps_o1[:, a:b],
                                            lhsT=vsb[c][:, VG * g:VG * g + 65],
                                            rhs=pt[idx % 2][:, a:b],
                                            start=(c == 0), stop=False)
                                        ins = t.matmul(
                                            out=ps_o2[:, a:b],
                                            lhsT=vsb[c][:, VG * g + 65:VG * g + 129],
                                            rhs=pt[idx % 2][:, a:b],
                                            start=(c == 0), stop=False)
                                if c != 15:
                                    ins.then_inc(t_av, 1)

                            def emit_close():
                                ins = None
                                for sgm in range(2):
                                    cols = slice(512 * sgm, 512 * (sgm + 1))
                                    if layer == 0:
                                        ins = t.matmul(out=ps_o[:, cols],
                                                       lhsT=zrow[0:1, 0:65],
                                                       rhs=abc[0:1, 0:512],
                                                       start=False, stop=True)
                                    else:
                                        t.matmul(out=ps_o1[:, cols],
                                                 lhsT=zrow[0:1, 0:65],
                                                 rhs=abc[0:1, 0:512],
                                                 start=False, stop=True)
                                        ins = t.matmul(out=ps_o2[:, cols],
                                                       lhsT=zrow[0:1, 0:64],
                                                       rhs=abc[0:1, 0:512],
                                                       start=False, stop=True)
                                ins.then_inc(t_av, 1)

                            if ns_s == 2:
                                emit_S(0)
                                for c in range(1, 16):
                                    emit_S(c)
                                    emit_AV(c - 1)
                                emit_AV(15)
                            else:
                                for c in range(16):
                                    emit_S(c)
                                    emit_AV(c)
                            emit_close()
                            WV(t, f"{pfx}rc{hs}")
                            if hs >= 1:
                                WA(t, f"{pfx}bbe{hs - 1}")
                            ins = None
                            for sgm in range(2):
                                ins = t.matmul(
                                    out=ps_b[:, 512 * sgm:512 * (sgm + 1)],
                                    lhsT=ones_r[0:1, 0:HD],
                                    rhs=rc_b[0:1, 512 * sgm:512 * (sgm + 1)],
                                    start=True, stop=True)
                            ins.then_inc(t_bc, 1)
                    blk.tensor(_)

                    def _(sc):
                        for hs in range(NH):
                            for c in range(16):
                                s0 = c // 2
                                idx = 16 * hs + c
                                sc.wait_ge(t_s, idx + 1)
                                if idx >= 2:
                                    sc.wait_ge(t_av, idx - 1)
                                sc.activation(out=pt[idx % 2][:, 128 * s0:1024],
                                              in_=ps_s[idx % ns_s][:, 128 * s0:1024],
                                              func=AF.Exp, bias=bias_sb[:, 0:1],
                                              scale=scale)
                                MKA(f"{pfx}exp{idx}")
                            sc.wait_ge(t_av, 16 * (hs + 1))
                            if hs >= 2:
                                WV(sc, f"{pfx}epi{hs - 2}")
                            if layer == 0:
                                sc.copy(out=ob_a[hs % 2][:], in_=ps_o[0:65, :])
                            else:
                                sc.copy(out=ob_a[hs % 2][:], in_=ps_o1[0:65, :])
                                sc.copy(out=ob_b[hs % 2][:], in_=ps_o2[0:64, :])
                            MKA(f"{pfx}oev{hs}")
                            sc.wait_ge(t_bc, hs + 1)
                            if hs >= 1:
                                WV(sc, f"{pfx}epi{hs - 1}")
                            sc.copy(out=bb_s[:, :], in_=ps_b[:, :])
                            MKA(f"{pfx}bbe{hs}")
                    blk.scalar(_)

                    def _(v):
                        v.wait_ge(m_ld, 32)
                        for hs in range(NH):
                            for c in range(16):
                                s0 = c // 2
                                idx = 16 * hs + c
                                WA(v, f"{pfx}exp{idx}")
                                v.tensor_tensor(
                                    out=pt[idx % 2][:, 128 * s0:128 * (s0 + 1)],
                                    in0=pt[idx % 2][:, 128 * s0:128 * (s0 + 1)],
                                    in1=mkc[c][:, :], op=ALU.mult)
                                MKV(f"{pfx}msk{idx}")
                            WA(v, f"{pfx}oev{hs}")
                            with nc.allow_low_precision(reason="1/sum -> bf16"):
                                v.reciprocal(out=rc_b[:, :],
                                             in_=ob_a[hs % 2][64:65, :])
                            MKV(f"{pfx}rc{hs}")
                            WA(v, f"{pfx}bbe{hs}")
                            if layer == 0:
                                qt, qr = hs // 2, 64 * (hs % 2)
                                v.tensor_tensor(out=of[qt][qr:qr + HD, :],
                                                in0=ob_a[hs % 2][0:64, :],
                                                in1=bb_s[:, :], op=ALU.mult)
                            else:
                                v.tensor_tensor(out=of[hs][0:64, :],
                                                in0=ob_a[hs % 2][0:64, :],
                                                in1=bb_s[0:64, :], op=ALU.mult)
                                v.tensor_tensor(out=of[hs][64:128, :],
                                                in0=ob_b[hs % 2][0:64, :],
                                                in1=bb_s[64:128, :],
                                                op=ALU.mult)
                            MKV(f"{pfx}epi{hs}")
                    blk.vector(_)

                if outproj:
                    with ExitStack() as pp:
                        ee = pp.enter_context
                        ps_5 = [ee(nc.psum_tensor(f"{pfx}p5{i}", [128, TOKQ], F32))
                                for i in range(2)]
                        WL5 = [tot["wl"][2], tot["wl"][3]]

                        def _(gq):
                            WA(gq, f"{pfx}bias")
                            for mt in range(16):
                                if mt >= 2:
                                    gq.wait_ge(t_p5, mt - 1)
                                gq.dma_start(out=wbuf[mt % 2][:],
                                             in_=wod_[mt]).then_inc(
                                                 wl[2 + mt % 2], 16)
                        blk.gpsimd(_)
                        tot["wl"][2] += 16 * 8
                        tot["wl"][3] += 16 * 8

                        def _(t):
                            WV(t, f"{pfx}epi{NH - 1}")
                            for mt in range(16):
                                t.wait_ge(wl[2 + mt % 2],
                                          WL5[mt % 2] + 16 * (mt // 2 + 1))
                                if mt >= 2:
                                    WV(t, f"{pfx}res{mt - 2}")
                                ins = None
                                for kc in range(16):
                                    for sgm in range(2):
                                        ins = t.matmul(
                                            out=ps_5[mt % 2][:, 512 * sgm:512 * (sgm + 1)],
                                            lhsT=wbuf[mt % 2][:, 128 * kc:128 * (kc + 1)],
                                            rhs=of[kc][:, 512 * sgm:512 * (sgm + 1)],
                                            start=(kc == 0), stop=(kc == 15))
                                ins.then_inc(t_p5, 1)
                        blk.tensor(_)

                        def _(v):
                            v.wait_ge(x_ld, 256)
                            for mt in range(16):
                                v.wait_ge(t_p5, mt + 1)
                                v.tensor_tensor(out=x_f[mt][:], in0=x_f[mt][:],
                                                in1=ps_5[mt % 2][:, :],
                                                op=ALU.add)
                                MKV(f"{pfx}res{mt}")
                        blk.vector(_)
                else:
                    def _(v):
                        WV(v, f"{pfx}epi{NH - 1}")
                        for mt in range(16):
                            v.tensor_tensor(out=x_f[mt][:], in0=x_f[mt][:],
                                            in1=of[mt][:], op=ALU.add)
                            MKV(f"{pfx}res{mt}")
                    blk.vector(_)

        def emit_ffn0(pfx):
            with ExitStack() as ph:
                e = ph.enter_context
                wb = [e(nc.sbuf_tensor(f"{pfx}wb{i}", [128, 2048], BF16))
                      for i in range(4)]
                hb = [e(nc.sbuf_tensor(f"{pfx}hb{i}", [128, TOKQ], BF16))
                      for i in range(8)]
                db = [e(nc.sbuf_tensor(f"{pfx}db{i}", [128, 2048], BF16))
                      for i in range(8)]
                sg_s = e(nc.sbuf_tensor(f"{pfx}sgs", [128, 64], F32))
                su_s = e(nc.sbuf_tensor(f"{pfx}sus", [128, 64], F32))
                sd_s = e(nc.sbuf_tensor(f"{pfx}sds", [128, 16], F32))
                gb = e(nc.sbuf_tensor(f"{pfx}gb", [128, TOKQ], BF16))
                ub = e(nc.sbuf_tensor(f"{pfx}ub", [128, TOKQ], BF16))
                ps_g = e(nc.psum_tensor(f"{pfx}psg", [128, TOKQ], F32))
                ps_u = e(nc.psum_tensor(f"{pfx}psu", [128, TOKQ], F32))
                ps_d = [e(nc.psum_tensor(f"{pfx}psd{i}", [128, TOKQ], F32))
                        for i in range(2)]

                def _(s):
                    WV(s, "a0.res15")
                    s.dma_start(out=sg_s[:], in_=d["sg"]).then_inc(f0c, 16)
                    s.dma_start(out=su_s[:], in_=d["su"]).then_inc(f0c, 16)
                    s.dma_start(out=sd_s[:], in_=d["sd"]).then_inc(f0c, 16)
                blk.sync(_)

                def gu_src(i):
                    return (d["wu"] if i % 2 else d["wg"])[i // 2]

                WLF = [tot["wl"][j] for j in range(4)]
                D0 = tot["d"]

                def _(gq):
                    WV(gq, "a0.res15")
                    for i in range(4):
                        gq.dma_start(out=wb[i][:], in_=gu_src(i)).then_inc(
                            wl[i], 16)
                    for c in range(8):
                        gq.dma_start(out=db[c][:], in_=d["wd"][c]).then_inc(
                            d_lda, 16)
                    for r in range(8):
                        for m in range(8):
                            mt = 8 * r + m
                            for i in (2 * mt, 2 * mt + 1):
                                if i + 4 < 128:
                                    gq.wait_ge(t_mm, max(0, i + 1))
                                    gq.dma_start(out=wb[(i + 4) % 4][:],
                                                 in_=gu_src(i + 4)).then_inc(
                                                     wl[(i + 4) % 4], 16)
                        if r + 1 < 8:
                            gq.wait_ge(t_dn, 16 * (r + 1))
                            for c in range(8):
                                gq.dma_start(out=db[c][:],
                                             in_=d["wd"][8 * (r + 1) + c]
                                             ).then_inc(d_lda, 16)
                blk.gpsimd(_)
                for i in range(4):
                    tot["wl"][i] += 16 * 32
                tot["d"] += 16 * 64

                def _(t):
                    WV(t, f"{pfx}xh_done")
                    for r in range(8):
                        for m in range(8):
                            mt = 8 * r + m
                            for i in (2 * mt, 2 * mt + 1):
                                WV(t, f"{pfx}wsub{i}")
                                is_u = i % 2
                                if mt >= 1:
                                    WV(t, f"{pfx}usc{mt - 1}" if is_u
                                       else f"{pfx}gsc{mt - 1}")
                                ps = ps_u if is_u else ps_g
                                ins = None
                                for kc in range(16):
                                    for sgm in range(2):
                                        ins = t.matmul(
                                            out=ps[:, 512 * sgm:512 * (sgm + 1)],
                                            lhsT=wb[i % 4][:, 128 * kc:128 * (kc + 1)],
                                            rhs=xh[kc][:, 512 * sgm:512 * (sgm + 1)],
                                            start=(kc == 0), stop=(kc == 15))
                                ins.then_inc(t_mm, 1)
                        WV(t, f"{pfx}dsub{r}")
                        WV(t, f"{pfx}h{8 * r + 7}")
                        for ht in range(16):
                            if 16 * r + ht >= 2:
                                WV(t, f"{pfx}dev{16 * r + ht - 2}")
                            ins = None
                            for c in range(8):
                                for sgm in range(2):
                                    ins = t.matmul(
                                        out=ps_d[ht % 2][:, 512 * sgm:512 * (sgm + 1)],
                                        lhsT=db[c][:, 128 * ht:128 * (ht + 1)],
                                        rhs=hb[c][:, 512 * sgm:512 * (sgm + 1)],
                                        start=(c == 0), stop=(c == 7))
                            ins.then_inc(t_dn, 1)
                blk.tensor(_)

                def _(sc):
                    for mt in range(64):
                        WV(sc, f"{pfx}gsc{mt}")
                        sc.activation(out=gb[:], in_=gb[:], func=AF.Silu)
                        MKA(f"{pfx}silu{mt}")
                blk.scalar(_)

                def _(v):
                    v.wait_ge(f0c, 48)

                    def wsub(i):
                        v.wait_ge(wl[i % 4], WLF[i % 4] + 16 * (i // 4 + 1))
                        v.tensor_scalar_add(out=wb[i % 4][:], in0=wb[i % 4][:],
                                            scalar1=-128.0)
                        MKV(f"{pfx}wsub{i}")

                    for i in range(4):
                        wsub(i)
                    for r in range(8):
                        for m in range(8):
                            mt = 8 * r + m
                            for i in (2 * mt + 4, 2 * mt + 5):
                                if i < 128:
                                    wsub(i)
                            if mt >= 1:
                                WA(v, f"{pfx}silu{mt - 1}")
                            v.wait_ge(t_mm, 2 * mt + 1)
                            v.tensor_scalar_mul(out=gb[:], in0=ps_g[:, :],
                                                scalar1=sg_s[:, mt:mt + 1])
                            MKV(f"{pfx}gsc{mt}")
                            v.wait_ge(t_mm, 2 * mt + 2)
                            v.tensor_scalar_mul(out=ub[:], in0=ps_u[:, :],
                                                scalar1=su_s[:, mt:mt + 1])
                            MKV(f"{pfx}usc{mt}")
                            WA(v, f"{pfx}silu{mt}")
                            v.tensor_tensor(out=hb[m][:], in0=gb[:], in1=ub[:],
                                            op=ALU.mult)
                            MKV(f"{pfx}h{mt}")
                        v.wait_ge(d_lda, D0 + 16 * 8 * (r + 1))
                        for c in range(8):
                            v.tensor_scalar_add(out=db[c][:], in0=db[c][:],
                                                scalar1=-128.0)
                        MKV(f"{pfx}dsub{r}")
                        for ht in range(16):
                            v.wait_ge(t_dn, 16 * r + ht + 1)
                            v.scalar_tensor_tensor(out=x_f[ht][:],
                                                   in0=ps_d[ht % 2][:, :],
                                                   scalar=sd_s[:, ht:ht + 1],
                                                   in1=x_f[ht][:],
                                                   op0=ALU.mult, op1=ALU.add)
                            MKV(f"{pfx}dev{16 * r + ht}")
                blk.vector(_)

        def emit_ffn1(pfx):
            with ExitStack() as ph:
                e = ph.enter_context
                wq = [e(nc.sbuf_tensor(f"{pfx}wq{i}", [128, 2048], FP8))
                      for i in range(16)]
                xq = [e(nc.sbuf_tensor(f"{pfx}xq{i}", [128, TOKQ], FP8))
                      for i in range(16)]
                wstg = [e(nc.sbuf_tensor(f"{pfx}ws{i}", [128, 2048], BF16))
                        for i in range(2)]
                bf_s = e(nc.sbuf_tensor(f"{pfx}bfs", [128, 16], BF16))
                bf_f = e(nc.sbuf_tensor(f"{pfx}bff", [128, 16], F32))
                red = e(nc.sbuf_tensor(f"{pfx}red", [128, 4], F32))
                sca = e(nc.sbuf_tensor(f"{pfx}sca", [1, 8], F32))
                sc_bf = e(nc.sbuf_tensor(f"{pfx}scbf", [1, 4], BF16))
                sc3 = e(nc.sbuf_tensor(f"{pfx}sc3", [128, 4], F32))
                ybf = e(nc.sbuf_tensor(f"{pfx}ybf", [128, TOKQ], BF16))
                ps_m = [e(nc.psum_tensor(f"{pfx}psm{i}", [128, TOKQ], F32))
                        for i in range(2)]
                ps_t = e(nc.psum_tensor(f"{pfx}pst", [1, 128], F32))
                ps_t2 = e(nc.psum_tensor(f"{pfx}pst2", [1, 128], F32))
                ps_c = e(nc.psum_tensor(f"{pfx}psc", [128, 4], F32))

                def _(s):
                    WV(s, "a1.res15")
                    s.dma_start(out=bf_s[:], in_=d["bfb"]).then_inc(f1c, 16)
                blk.sync(_)

                WLF = [tot["wl"][0], tot["wl"][1]]

                def _(gq):
                    WV(gq, "a1.res15")
                    for i in range(16):
                        if i >= 2:
                            WV(gq, f"{pfx}wmax{i - 2}")
                        gq.dma_start(out=wstg[i % 2][:], in_=d["wf"][i]
                                     ).then_inc(wl[i % 2], 16)
                blk.gpsimd(_)
                tot["wl"][0] += 16 * 8
                tot["wl"][1] += 16 * 8

                def _(v):
                    for i in range(16):
                        v.wait_ge(wl[i % 2], WLF[i % 2] + 16 * (i // 2 + 1))
                        v.tensor_reduce(out=red[:, 0:1], in_=wstg[i % 2][:],
                                        axis=AX.X, op=ALU.max,
                                        apply_absolute_value=True)
                        if i == 0:
                            v.tensor_copy(out=red[:, 1:2], in_=red[:, 0:1])
                        else:
                            v.tensor_tensor(out=red[:, 1:2], in0=red[:, 1:2],
                                            in1=red[:, 0:1], op=ALU.max)
                        MKV(f"{pfx}wmax{i}")
                    WV(v, f"{pfx}xh_done")
                    for kc in range(16):
                        v.tensor_reduce(out=red[:, 2:3], in_=xh[kc][:],
                                        axis=AX.X, op=ALU.max,
                                        apply_absolute_value=True)
                        if kc == 0:
                            v.tensor_copy(out=red[:, 3:4], in_=red[:, 2:3])
                        else:
                            v.tensor_tensor(out=red[:, 3:4], in0=red[:, 3:4],
                                            in1=red[:, 2:3], op=ALU.max)
                    MKV(f"{pfx}xmax")
                blk.vector(_)

                def _(t):
                    WV(t, f"{pfx}xmax")
                    t.wait_ge(s_ld, 32)
                    t.transpose(out=ps_t[0:1, 0:128], in_=red[:, 3:4],
                                identity=id_s[:, :]).then_inc(t_f1, 1)
                    t.transpose(out=ps_t2[0:1, 0:128], in_=red[:, 1:2],
                                identity=id_s[:, :]).then_inc(t_f1, 1)
                blk.tensor(_)

                def _(v):
                    v.wait_ge(t_f1, 2)
                    v.tensor_reduce(out=sca[:, 0:1], in_=ps_t[0:1, 0:128],
                                    axis=AX.X, op=ALU.max)
                    v.tensor_scalar_max(out=sca[:, 0:1], in0=sca[:, 0:1],
                                        scalar1=1e-6)
                    v.tensor_reduce(out=sca[:, 2:3], in_=ps_t2[0:1, 0:128],
                                    axis=AX.X, op=ALU.max)
                    v.tensor_scalar_max(out=sca[:, 2:3], in0=sca[:, 2:3],
                                        scalar1=1e-6)
                    MKV(f"{pfx}xsloc")
                blk.vector(_)

                CS0 = tot["cc_s"]
                CC0 = tot["cc_c"]

                def _(gq):
                    WV(gq, f"{pfx}xsloc")
                    gq.dma_start(out=cc_in, in_=sca[:, 0:1]).then_inc(cc_s, 16)
                    gq.wait_ge(cc_s, CS0 + 16)
                    gq.collective_compute(
                        "AllReduce", ALU.max,
                        replica_groups=[[0, 1, 2, 3, 4, 5, 6, 7]],
                        ins=[cc_in], outs=[cc_in]).then_inc(cc_c, 1)
                    gq.wait_ge(cc_c, CC0 + 1)
                    gq.dma_start(out=sca[:, 1:2], in_=cc_in).then_inc(cc_s, 16)
                    for i in range(16):
                        WV(gq, f"{pfx}wq{i - 2}" if i >= 2
                           else f"{pfx}wmax{14 + i}")
                        gq.dma_start(out=wstg[i % 2][:], in_=d["wf"][i]
                                     ).then_inc(wl[i % 2], 16)
                blk.gpsimd(_)
                tot["cc_s"] += 32
                tot["cc_c"] += 1
                CSX = tot["cc_s"]
                CCX = tot["cc_c"]
                WLB = [WLF[0] + 16 * 8, WLF[1] + 16 * 8]
                tot["wl"][0] += 16 * 8
                tot["wl"][1] += 16 * 8

                def _(v):
                    v.wait_ge(cc_s, CSX)
                    v.wait_ge(cc_c, CCX)
                    v.reciprocal(out=sca[:, 3:4], in_=sca[:, 1:2])
                    v.reciprocal(out=sca[:, 4:5], in_=sca[:, 2:3])
                    v.tensor_tensor(out=sca[:, 5:6], in0=sca[:, 1:2],
                                    in1=sca[:, 2:3], op=ALU.mult)
                    v.tensor_copy(out=sc_bf[:, 0:3], in_=sca[:, 3:6])
                    MKV(f"{pfx}scbf")
                blk.vector(_)

                def _(t):
                    WV(t, f"{pfx}scbf")
                    t.matmul(out=ps_c[:, 0:3], lhsT=ones_r[0:1, :],
                             rhs=sc_bf[0:1, 0:3], start=True,
                             stop=True).then_inc(t_f1, 1)
                blk.tensor(_)

                def _(sc):
                    sc.wait_ge(t_f1, 3)
                    sc.copy(out=sc3[:, 0:3], in_=ps_c[:, 0:3])
                    MKA(f"{pfx}sc3")
                    sc.wait_ge(f1c, 16)
                    sc.copy(out=bf_f[:, :], in_=bf_s[:, :])
                    MKA(f"{pfx}bff")
                blk.scalar(_)

                def _(v):
                    WA(v, f"{pfx}sc3")
                    for i in range(16):
                        v.wait_ge(wl[i % 2], WLB[i % 2] + 16 * (i // 2 + 1))
                        v.tensor_scalar_mul(out=wq[i][:], in0=wstg[i % 2][:],
                                            scalar1=sc3[:, 1:2])
                        MKV(f"{pfx}wq{i}")
                    for kc in range(16):
                        v.tensor_scalar_mul(out=xq[kc][:], in0=xh[kc][:],
                                            scalar1=sc3[:, 0:1])
                        MKV(f"{pfx}xq{kc}")
                blk.vector(_)

                def _(t):
                    WV(t, f"{pfx}xq15")
                    for mt in range(16):
                        WV(t, f"{pfx}wq{mt}")
                        if mt >= 2:
                            WV(t, f"{pfx}yev{mt - 2}")
                        ins = None
                        for kc in range(16):
                            for sgm in range(2):
                                ins = t.matmul(
                                    out=ps_m[mt % 2][:, 512 * sgm:512 * (sgm + 1)],
                                    lhsT=wq[mt][:, 128 * kc:128 * (kc + 1)],
                                    rhs=xq[kc][:, 512 * sgm:512 * (sgm + 1)],
                                    start=(kc == 0), stop=(kc == 15))
                        ins.then_inc(t_f1, 1)
                blk.tensor(_)

                def _(v):
                    WA(v, f"{pfx}bff")
                    for mt in range(16):
                        v.wait_ge(t_f1, 3 + mt + 1)
                        v.tensor_scalar_mul(out=ybf[:], in0=ps_m[mt % 2][:, :],
                                            scalar1=sc3[:, 2:3])
                        v.tensor_scalar_add(out=ybf[:], in0=ybf[:],
                                            scalar1=bf_f[:, mt:mt + 1])
                        v.tensor_tensor(out=x_f[mt][:], in0=x_f[mt][:],
                                        in1=ybf[:], op=ALU.add)
                        MKV(f"{pfx}yev{mt}")
                blk.vector(_)

        emit_ln("a0.", lambda sc: sc.wait_ge(xb_ld, 256))
        emit_attn(0, "a0.", lambda gq: None)

        def _(gq):
            for kc in range(16):
                WV(gq, f"a0.res{kc}")
                gq.tensor_copy(out=xh[kc][:], in_=x_f[kc][:])
                MKG(f"f0.xcp{kc}")
        blk.gpsimd(_)

        emit_ln("f0.", lambda sc: WG(sc, "f0.xcp15"))
        emit_ffn0("f0.")

        def _(gq):
            for kc in range(16):
                WV(gq, f"f0.dev{16 * 7 + kc}")
                gq.tensor_copy(out=xh[kc][:], in_=x_f[kc][:])
                MKG(f"a1.xcp{kc}")
        blk.gpsimd(_)

        emit_ln("a1.", lambda sc: WG(sc, "a1.xcp15"))
        emit_attn(1, "a1.", lambda gq: WG(gq, "a1.xcp15"))

        def _(gq):
            for kc in range(16):
                WV(gq, f"a1.res{kc}")
                gq.tensor_copy(out=xh[kc][:], in_=x_f[kc][:])
                MKG(f"f1.xcp{kc}")
        blk.gpsimd(_)

        emit_ln("f1.", lambda sc: WG(sc, "f1.xcp15"))
        emit_ffn1("f1.")

        def _(s):
            for mt in range(16):
                WV(s, f"f1.yev{mt}")
                s.dma_start(out=ydr[128 * mt:128 * (mt + 1), :],
                            in_=x_f[mt][:]).then_inc(
                                st_a if mt % 2 == 0 else st_b, 16)
            s.wait_ge(st_a, 16 * 8)
            s.wait_ge(st_b, 16 * 8)
        blk.sync(_)

    return nc



def prep_fused_inputs(inputs):
    x_full = np.ascontiguousarray(
        np.asarray(inputs["L0_attn_hidden_states"], np.float32))
    ro = l0_row_order()
    base = {}
    base["wq0"] = tiled_lhsT(np.ascontiguousarray(
        np.asarray(inputs["L0_attn_Wq"], np.float32)[ro].T))
    base["wk0"] = np.ascontiguousarray(
        np.asarray(inputs["L0_attn_Wk"], np.float32).T).reshape(16, 128, 512)
    base["wv0"] = np.ascontiguousarray(
        np.asarray(inputs["L0_attn_Wv"], np.float32).T).reshape(16, 128, 512)
    base["wo0"] = tiled_lhsT(np.ascontiguousarray(
        np.asarray(inputs["L0_attn_Wo"], np.float32)[:, ro].T))
    wq2 = np.asarray(inputs["L1_attn_Wq"],
                     np.float32).transpose(1, 0, 2).reshape(2048, 2048)
    wk2 = np.asarray(inputs["L1_attn_Wk"],
                     np.float32).transpose(1, 0, 2).reshape(2048, 512)
    wv2 = np.asarray(inputs["L1_attn_Wv"],
                     np.float32).transpose(1, 0, 2).reshape(2048, 512)
    base["wq1"] = tiled_lhsT(np.ascontiguousarray(wq2))
    base["wk1"] = np.ascontiguousarray(wk2).reshape(16, 128, 512)
    base["wv1"] = np.ascontiguousarray(wv2).reshape(16, 128, 512)
    bq = np.asarray(inputs["L1_attn_bq"], np.float32).reshape(2048)
    bk = np.asarray(inputs["L1_attn_bk"], np.float32).reshape(512)
    bv = np.asarray(inputs["L1_attn_bv"], np.float32).reshape(512)
    base["bq1"] = np.ascontiguousarray(bq.reshape(16, 128).T)
    base["bk1"] = np.ascontiguousarray(
        np.broadcast_to(bk.reshape(1, 512), (128, 512)))
    bvg = np.zeros((4, 129), np.float32)
    bvr = bv.reshape(4, 128)
    bvg[:, 0:64] = bvr[:, 0:64]
    bvg[:, 65:129] = bvr[:, 64:128]
    base["bv1"] = np.ascontiguousarray(
        np.broadcast_to(bvg.reshape(1, 516), (128, 516)))
    base["eq0"] = head_indicator_tiles(2048, 32)
    base["ek0"] = head_indicator_tiles(512, 8)
    base["eq1"] = head_indicator_tiles(2048, 16)
    base["ek1"] = head_indicator_tiles(512, 4)
    base["wg"] = tiled_lhsT(np.ascontiguousarray(
        u8_view(np.asarray(inputs["L0_ffn_gate_w"])).T))
    base["wu"] = tiled_lhsT(np.ascontiguousarray(
        u8_view(np.asarray(inputs["L0_ffn_up_w"])).T))
    base["wd"] = np.ascontiguousarray(
        u8_view(np.asarray(inputs["L0_ffn_down_w"])).T).reshape(64, 128, 2048)
    base["sg"] = np.ascontiguousarray(
        np.asarray(inputs["L0_ffn_gate_scale"], np.float32).reshape(64, 128).T)
    base["su"] = np.ascontiguousarray(
        np.asarray(inputs["L0_ffn_up_scale"], np.float32).reshape(64, 128).T)
    base["sd"] = np.ascontiguousarray(
        np.asarray(inputs["L0_ffn_down_scale"], np.float32).reshape(16, 128).T)
    wf = np.asarray(inputs["L1_ffn_weight"])
    base["wf"] = tiled_lhsT(np.ascontiguousarray(wf.T))
    base["bfb"] = np.ascontiguousarray(
        np.asarray(inputs["L1_ffn_bias"]).reshape(16, 128).T)
    base["ident"] = np.eye(128, dtype=np.float32)
    kk, qq = np.meshgrid(np.arange(128), np.arange(128), indexing="ij")
    tril = (qq >= kk).astype(np.float32)
    ckg, skg = rope_tables(np.arange(2048))
    tabk_g = np.ascontiguousarray(np.concatenate([ckg, skg], axis=1))

    in_maps = []
    for cid in range(NCORES):
        b, p = cid // 2, cid % 2
        perm = perm_for_core(p)
        own = perm[:1024]
        m = dict(base)
        m["x"] = np.ascontiguousarray(x_full[b].T[:, own])
        cq, sq_ = rope_tables(own)
        m["tabq0"] = np.ascontiguousarray(np.concatenate([cq, sq_], axis=1))
        m["tabk0"] = tabk_g
        if p == 0:
            m["maske"] = tril
            m["masko"] = np.zeros((128, 128), np.float32)
        else:
            m["maske"] = np.ones((128, 128), np.float32)
            m["masko"] = tril
        sidx = np.empty((128, 8), np.uint32)
        for tt in range(8):
            sidx[:, tt] = 128 * (p + 2 * tt) + np.arange(128)
        m["sidx"] = np.ascontiguousarray(sidx)
        in_maps.append(m)
    return in_maps, x_full


def gather_fused_output(results, x_full):
    out = np.empty_like(x_full)
    for cid in range(NCORES):
        b, p = cid // 2, cid % 2
        perm = perm_for_core(p)
        out[b, perm[:1024], :] = results[cid]["y"].T
    return out


def kernel(**inputs):
    key = "fused"
    if key not in _NC_CACHE:
        _NC_CACHE[key] = build_fused()
    in_maps, x_full = prep_fused_inputs(inputs)
    res = run_bass_kernel_spmd(_NC_CACHE[key], in_maps,
                               core_ids=list(range(NCORES)))
    return gather_fused_output(res.results, x_full)

def build_fused():
    marks = Marks()
    _build_fused(marks)
    marks.pass2 = True
    return _build_fused(marks)


def _build_fused(marks):
    nc = bass.Bass("TRN2", target_bir_lowering=False)
    nc.num_devices = 8

    xdr = nc.dram_tensor("x", [DM, TOKQ], F32, kind="ExternalInput").ap()
    ydr = nc.dram_tensor("y", [DM, TOKQ], F32, kind="ExternalOutput").ap()
    d = {}
    d["wq0"] = nc.dram_tensor("wq0", [16, 128, 2048], F32, kind="ExternalInput").ap()
    d["wk0"] = nc.dram_tensor("wk0", [16, 128, 512], F32, kind="ExternalInput").ap()
    d["wv0"] = nc.dram_tensor("wv0", [16, 128, 512], F32, kind="ExternalInput").ap()
    d["wo0"] = nc.dram_tensor("wo0", [16, 128, 2048], F32, kind="ExternalInput").ap()
    d["tabq0"] = nc.dram_tensor("tabq0", [128, 2048], F32, kind="ExternalInput").ap()
    d["tabk0"] = nc.dram_tensor("tabk0", [128, 4096], F32, kind="ExternalInput").ap()
    d["eq0"] = nc.dram_tensor("eq0", [128, 512], F32, kind="ExternalInput").ap()
    d["ek0"] = nc.dram_tensor("ek0", [128, 32], F32, kind="ExternalInput").ap()
    d["wq1"] = nc.dram_tensor("wq1", [16, 128, 2048], F32, kind="ExternalInput").ap()
    d["wk1"] = nc.dram_tensor("wk1", [16, 128, 512], F32, kind="ExternalInput").ap()
    d["wv1"] = nc.dram_tensor("wv1", [16, 128, 512], F32, kind="ExternalInput").ap()
    d["bq1"] = nc.dram_tensor("bq1", [128, 16], F32, kind="ExternalInput").ap()
    d["bk1"] = nc.dram_tensor("bk1", [128, 512], F32, kind="ExternalInput").ap()
    d["bv1"] = nc.dram_tensor("bv1", [128, 516], F32, kind="ExternalInput").ap()
    d["eq1"] = nc.dram_tensor("eq1", [128, 256], F32, kind="ExternalInput").ap()
    d["ek1"] = nc.dram_tensor("ek1", [128, 16], F32, kind="ExternalInput").ap()
    d["wg"] = nc.dram_tensor("wg", [64, 128, 2048], U8, kind="ExternalInput").ap()
    d["wu"] = nc.dram_tensor("wu", [64, 128, 2048], U8, kind="ExternalInput").ap()
    d["wd"] = nc.dram_tensor("wd", [64, 128, 2048], U8, kind="ExternalInput").ap()
    d["sg"] = nc.dram_tensor("sg", [128, 64], F32, kind="ExternalInput").ap()
    d["su"] = nc.dram_tensor("su", [128, 64], F32, kind="ExternalInput").ap()
    d["sd"] = nc.dram_tensor("sd", [128, 16], F32, kind="ExternalInput").ap()
    d["wf"] = nc.dram_tensor("wf", [16, 128, 2048], BF16, kind="ExternalInput").ap()
    d["bfb"] = nc.dram_tensor("bfb", [128, 16], BF16, kind="ExternalInput").ap()
    d["maske"] = nc.dram_tensor("maske", [128, 128], F32, kind="ExternalInput").ap()
    d["masko"] = nc.dram_tensor("masko", [128, 128], F32, kind="ExternalInput").ap()
    d["ident"] = nc.dram_tensor("ident", [128, 128], F32, kind="ExternalInput").ap()
    d["sidx"] = nc.dram_tensor("sidx", [128, 8], U32, kind="ExternalInput").ap()
    shr_k = [nc.dram_tensor(f"shr_k{l}", [2048, 512], BF16, kind="Internal",
                            addr_space="Shared").ap() for l in range(2)]
    shr_v = [nc.dram_tensor(f"shr_v{l}", [2048, 520], BF16, kind="Internal",
                            addr_space="Shared").ap() for l in range(2)]
    bar = nc.dram_tensor("bar", [1, 1], F32, kind="Internal").ap()
    cc_in = nc.dram_tensor("cc_in", [1, 1], F32, kind="Internal").ap()

    with ExitStack() as top:
        E = top.enter_context
        blk0 = E(nc.Block())

        x_f = [E(nc.sbuf_tensor(f"x{i}", [128, TOKQ], F32)) for i in range(16)]
        xh = [E(nc.sbuf_tensor(f"xh{i}", [128, TOKQ], BF16)) for i in range(16)]
        ones_c = E(nc.sbuf_tensor("ones_c", [128, 1], BF16))
        ones_r = E(nc.sbuf_tensor("ones_r", [1, 128], BF16))
        zrow = E(nc.sbuf_tensor("zrow", [1, 65], BF16))
        stat = E(nc.sbuf_tensor("stat", [1, 1537], F32))
        ab_bf = E(nc.sbuf_tensor("ab_bf", [1, 2048], BF16))
        abc = E(nc.sbuf_tensor("abc", [128, 1024], BF16))
        x2b = [E(nc.sbuf_tensor(f"x2b{i}", [128, 1024], BF16)) for i in range(2)]
        mk_e = E(nc.sbuf_tensor("mk_e", [128, 128], BF16))
        mk_o = E(nc.sbuf_tensor("mk_o", [128, 128], BF16))
        id_s = E(nc.sbuf_tensor("id_s", [128, 128], F32))
        bias_sb = E(nc.sbuf_tensor("bias_sb", [128, 1], F32))
        sidx_s = E(nc.sbuf_tensor("sidx_s", [128, 8], U32))
        zf = E(nc.sbuf_tensor("zf", [1, 1], F32))

        x_ld = E(nc.semaphore("x_ld"))
        xb_ld = E(nc.semaphore("xb_ld"))
        s_ld = E(nc.semaphore("s_ld"))
        m_ld = E(nc.semaphore("m_ld"))
        ca = [E(nc.semaphore(f"ca{l}")) for l in range(2)]
        cv = [E(nc.semaphore(f"cv{l}")) for l in range(2)]
        b_ld = E(nc.semaphore("b_ld"))
        f0c = E(nc.semaphore("f0c"))
        f1c = E(nc.semaphore("f1c"))
        wl = [E(nc.semaphore(f"wl{i}")) for i in range(4)]
        d_lda = E(nc.semaphore("d_lda"))
        sh_a = E(nc.semaphore("sh_a"))
        sh_b = E(nc.semaphore("sh_b"))
        rb_s = E(nc.semaphore("rb_s"))
        st_a = E(nc.semaphore("st_a"))
        st_b = E(nc.semaphore("st_b"))
        cc_s = E(nc.semaphore("cc_s"))
        cc_c = E(nc.semaphore("cc_c"))
        bz = E(nc.semaphore("bz"))
        t_ln = E(nc.semaphore("t_ln"))
        sems_attn = {}
        for pfx in ("a0.", "a1."):
            for nm in ("t_kv", "t_q", "t_nb", "t_s", "t_av", "t_bc", "t_p5"):
                sems_attn[pfx + nm] = E(nc.semaphore(pfx + nm))
        t_mm = E(nc.semaphore("t_mm"))
        t_dn = E(nc.semaphore("t_dn"))
        t_f1 = E(nc.semaphore("t_f1"))
        v_self = E(nc.semaphore("v_self"))
        a_self = E(nc.semaphore("a_self"))
        g_self = E(nc.semaphore("g_self"))
        SV = {"sem": v_self, "n": 0}
        SA = {"sem": a_self, "n": 0}
        SG = {"sem": g_self, "n": 0}
        _vw, _aw, _gw = SerialEng(SV), SerialEng(SA), SerialEng(SG)

        class blk:
            tensor = staticmethod(blk0.tensor)
            sync = staticmethod(blk0.sync)
            vector = staticmethod(lambda f: blk0.vector(lambda e: f(_vw.bind(e))))
            scalar = staticmethod(lambda f: blk0.scalar(lambda e: f(_aw.bind(e))))
            gpsimd = staticmethod(lambda f: blk0.gpsimd(lambda e: f(_gw.bind(e))))

        def MKV(n):
            marks.mark(SV, n)

        def MKA(n):
            marks.mark(SA, n)

        def MKG(n):
            marks.mark(SG, n)

        def WV(eng, n):
            val = marks.get(n)
            if val > 0:
                eng.wait_ge(v_self, val)

        def WA(eng, n):
            val = marks.get(n)
            if val > 0:
                eng.wait_ge(a_self, val)

        def WG(eng, n):
            val = marks.get(n)
            if val > 0:
                eng.wait_ge(g_self, val)

        tot = {"sh": [0, 0], "rb": 0, "wl": [0, 0, 0, 0], "d": 0,
               "cc_c": 0, "cc_s": 0, "tln": 0}
        mkc = [mk_e if c % 2 == 0 else mk_o for c in range(16)]

        def _(s):
            for kc in range(16):
                s.dma_start(out=x_f[kc][:],
                            in_=xdr[128 * kc:128 * (kc + 1), :]).then_inc(x_ld, 16)
            s.dma_start(out=id_s[:], in_=d["ident"]).then_inc(s_ld, 16)
            s.dma_start(out=sidx_s[:], in_=d["sidx"]).then_inc(s_ld, 16)
        blk.sync(_)

        def _(gq):
            for kc in range(16):
                gq.dma_start(out=xh[kc][:],
                             in_=xdr[128 * kc:128 * (kc + 1), :]).then_inc(xb_ld, 16)
            gq.dma_start(out=mk_e[:], in_=d["maske"]).then_inc(m_ld, 16)
            gq.dma_start(out=mk_o[:], in_=d["masko"]).then_inc(m_ld, 16)
        blk.gpsimd(_)

        def _(v):
            v.memset(ones_c[:], 1.0)
            v.memset(ones_r[:], 1.0)
            v.memset(zrow[:], 0.0)
            v.memset(zf[:], 0.0)
            MKV("ones")
        blk.vector(_)

        def _(gq):
            WV(gq, "ones")
            gq.dma_start(out=bar, in_=zf[:]).then_inc(bz, 16)
        blk.gpsimd(_)

        def emit_ln(pfx, gate_scalar):
            nsg = 2
            tb = tot["tln"]
            with ExitStack() as lnp:
                le = lnp.enter_context
                lnps = [le(nc.psum_tensor(f"{pfx}lnps{i}", [1, 512], F32))
                        for i in range(2)]
                lnpq = [le(nc.psum_tensor(f"{pfx}lnpq{i}", [1, 512], F32))
                        for i in range(2)]
                lnbc = [le(nc.psum_tensor(f"{pfx}lnbc{i}", [128, 512], F32))
                        for i in range(2)]

                def _(sc):
                    gate_scalar(sc)
                    for kc in range(16):
                        if kc >= 2:
                            sc.wait_ge(t_ln, tb + kc - 1)
                        sc.activation(out=x2b[kc % 2][:], in_=xh[kc][:],
                                      func=AF.Square)
                        MKA(f"{pfx}sq{kc}")
                blk.scalar(_)

                def _(t):
                    WV(t, "ones")
                    for kc in range(16):
                        WA(t, f"{pfx}sq{kc}")
                        for sg_ in range(nsg):
                            t.matmul(out=lnps[sg_][0:1, :], lhsT=ones_c[:, 0:1],
                                     rhs=xh[kc][:, 512 * sg_:512 * (sg_ + 1)],
                                     start=(kc == 0), stop=(kc == 15))
                        ins = None
                        for sg_ in range(nsg):
                            ins = t.matmul(out=lnpq[sg_][0:1, :],
                                           lhsT=ones_c[:, 0:1],
                                           rhs=x2b[kc % 2][:, 512 * sg_:512 * (sg_ + 1)],
                                           start=(kc == 0), stop=(kc == 15))
                        ins.then_inc(t_ln, 1)
                blk.tensor(_)

                eps_r = stat[:, 1536:1537]
                mu_r = stat[:, 0:512]
                var_r = stat[:, 512:1024]
                sc_r = stat[:, 1024:1536]

                def _(v):
                    if pfx == "a0.":
                        v.memset(eps_r, EPS_LN)
                    v.wait_ge(t_ln, tb + 16)
                    for sg_ in range(nsg):
                        v.tensor_scalar_mul(out=mu_r, in0=lnps[sg_][0:1, :],
                                            scalar1=1.0 / DM)
                        v.tensor_scalar_mul(out=var_r, in0=lnpq[sg_][0:1, :],
                                            scalar1=1.0 / DM)
                        v.tensor_tensor(out=sc_r, in0=mu_r, in1=mu_r,
                                        op=ALU.mult)
                        v.tensor_tensor(out=var_r, in0=var_r, in1=sc_r,
                                        op=ALU.subtract)
                        MKV(f"{pfx}prep{sg_}")
                        WA(v, f"{pfx}rstd{sg_}")
                        v.tensor_tensor(out=sc_r, in0=mu_r, in1=var_r,
                                        op=ALU.mult)
                        v.tensor_scalar_mul(out=sc_r, in0=sc_r, scalar1=-1.0)
                        v.tensor_copy(out=ab_bf[:, 512 * sg_:512 * (sg_ + 1)],
                                      in_=var_r)
                        v.tensor_copy(
                            out=ab_bf[:, 1024 + 512 * sg_:1024 + 512 * (sg_ + 1)],
                            in_=sc_r)
                        MKV(f"{pfx}ab{sg_}")
                blk.vector(_)

                def _(sc):
                    for sg_ in range(nsg):
                        WV(sc, f"{pfx}prep{sg_}")
                        sc.activation(out=var_r, in_=var_r, func=AF.Ln,
                                      bias=eps_r)
                        sc.activation(out=var_r, in_=var_r, func=AF.Exp,
                                      scale=-0.5)
                        MKA(f"{pfx}rstd{sg_}")
                blk.scalar(_)

                def _(t):
                    for half in range(2):
                        if half == 1:
                            WV(t, f"{pfx}xh_mul")
                        for sg_ in range(nsg):
                            idx = half * nsg + sg_
                            WV(t, f"{pfx}ab{sg_}" if half == 0
                               else f"{pfx}ab{nsg - 1}")
                            if idx >= 2:
                                WA(t, f"{pfx}bce{idx - 2}")
                            t.matmul(out=lnbc[sg_ % 2][:, :],
                                     lhsT=ones_r[0:1, :],
                                     rhs=ab_bf[:, 512 * idx:512 * (idx + 1)],
                                     start=True, stop=True).then_inc(t_ln, 1)
                blk.tensor(_)

                def _(sc):
                    for half in range(2):
                        for sg_ in range(nsg):
                            idx = half * nsg + sg_
                            sc.wait_ge(t_ln, tb + 16 + idx + 1)
                            if half == 1 and sg_ == 0:
                                WV(sc, f"{pfx}xh_mul")
                            sc.copy(out=abc[:, 512 * sg_:512 * (sg_ + 1)],
                                    in_=lnbc[sg_ % 2][:, :])
                            MKA(f"{pfx}bce{idx}")
                blk.scalar(_)

                def _(v):
                    for half, op in ((0, ALU.mult), (1, ALU.add)):
                        WA(v, f"{pfx}bce{half * nsg + nsg - 1}")
                        for kc in range(16):
                            v.tensor_tensor(out=xh[kc][:], in0=xh[kc][:],
                                            in1=abc[:, 0:1024], op=op)
                        MKV(f"{pfx}xh_mul" if half == 0 else f"{pfx}xh_done")
                blk.vector(_)
            tot["tln"] += 20

        def emit_attn(layer, pfx, gate_g):
            if layer == 0:
                NH, NKV, HD = 32, 8, 64
                rope, bias_qkv, outproj = True, False, True
                scale = 0.125
                wqd_, wkd_, wvd_, wod_ = d["wq0"], d["wk0"], d["wv0"], d["wo0"]
            else:
                NH, NKV, HD = 16, 4, 128
                rope, bias_qkv, outproj = False, True, False
                scale = 1.0 / float(np.sqrt(128.0))
                wqd_, wkd_, wvd_ = d["wq1"], d["wk1"], d["wv1"]
                wod_ = None
            VG = HD + 1
            shk, shv = shr_k[layer], shr_v[layer]
            stor_heads = l0_head_order() if layer == 0 else list(range(NH))
            t_kv = sems_attn[pfx + "t_kv"]
            t_q = sems_attn[pfx + "t_q"]
            t_nb = sems_attn[pfx + "t_nb"]
            t_s = sems_attn[pfx + "t_s"]
            t_av = sems_attn[pfx + "t_av"]
            t_bc = sems_attn[pfx + "t_bc"]
            t_p5 = sems_attn[pfx + "t_p5"]

            with ExitStack() as ph:
                e = ph.enter_context
                qsb = [e(nc.sbuf_tensor(f"{pfx}q{i}", [128, TOKQ], BF16))
                       for i in range(16)]
                vsb = [e(nc.sbuf_tensor(f"{pfx}v{i}", [128, NKV * VG], BF16))
                       for i in range(16)]
                rp = [e(nc.sbuf_tensor(f"{pfx}rp{i}", [128, 2048], BF16))
                      for i in range(2)]
                wbuf = [e(nc.sbuf_tensor(f"{pfx}wb{i}", [128, DM], BF16))
                        for i in range(2)]
                if rope:
                    tabq = e(nc.sbuf_tensor(f"{pfx}tq", [128, 2048], BF16))
                if bias_qkv:
                    bq_s = e(nc.sbuf_tensor(f"{pfx}bq", [128, 16], F32))
                eq_s = e(nc.sbuf_tensor(f"{pfx}eq", [128, 16 * NH], BF16))
                ek_s = e(nc.sbuf_tensor(f"{pfx}ek", [128, 4 * NKV], BF16))

                def _(gq):
                    gate_g(gq)
                    gq.dma_start(out=eq_s[:], in_=d[f"eq{layer}"]).then_inc(ca[layer], 16)
                    gq.dma_start(out=ek_s[:], in_=d[f"ek{layer}"]).then_inc(ca[layer], 16)
                    if rope:
                        gq.dma_start(out=tabq[:], in_=d["tabq0"]).then_inc(ca[layer], 16)
                blk.gpsimd(_)
                CT_A = 16 * (3 if rope else 2)

                if bias_qkv:
                    def _(s):
                        WG(s, f"{pfx}xcp15")
                        s.dma_start(out=bq_s[:], in_=d["bq1"]).then_inc(b_ld, 16)
                    blk.sync(_)

                pw = ph.enter_context(ExitStack())
                we = pw.enter_context
                wkv = [we(nc.sbuf_tensor(f"{pfx}wkv{i}", [128, 512], BF16))
                       for i in range(16)]
                stg_k = [we(nc.sbuf_tensor(f"{pfx}sk{i}", [128, 512], BF16))
                         for i in range(2)]
                stg_v = [we(nc.sbuf_tensor(f"{pfx}sv{i}", [128, 520], BF16))
                         for i in range(2)]
                if bias_qkv:
                    bk_r = we(nc.sbuf_tensor(f"{pfx}bkr", [128, 512], BF16))
                    bv_r = we(nc.sbuf_tensor(f"{pfx}bvr", [128, 516], BF16))
                ps_kv = [we(nc.psum_tensor(f"{pfx}pkv{i}", [128, 512], F32))
                         for i in range(2)]

                def _(gq):
                    if bias_qkv:
                        gq.dma_start(out=bk_r[:], in_=d["bk1"]).then_inc(ca[layer], 16)
                        gq.dma_start(out=bv_r[:], in_=d["bv1"]).then_inc(ca[layer], 16)
                    for kc in range(16):
                        gq.dma_start(out=wkv[kc][:],
                                     in_=wkd_[kc]).then_inc(ca[layer], 16)
                blk.gpsimd(_)
                CT_K = CT_A + 256 + (32 if bias_qkv else 0)

                def _(t):
                    WV(t, f"{pfx}xh_done")
                    t.wait_ge(ca[layer], CT_K)
                    for tt in range(8):
                        if tt >= 2:
                            if bias_qkv:
                                WV(t, f"{pfx}bk{tt - 2}")
                            else:
                                WA(t, f"{pfx}evk{tt - 2}")
                        ins = None
                        for kc in range(16):
                            ins = t.matmul(
                                out=ps_kv[tt % 2][:, :],
                                lhsT=xh[kc][:, 128 * tt:128 * (tt + 1)],
                                rhs=wkv[kc][:, :],
                                start=(kc == 0), stop=(kc == 15))
                        ins.then_inc(t_kv, 1)
                blk.tensor(_)

                SH0 = [tot["sh"][0], tot["sh"][1]]

                def _(sc):
                    for tt in range(8):
                        sc.wait_ge(t_kv, tt + 1)
                        if tt >= 2:
                            sc.wait_ge(sh_a if tt % 2 == 0 else sh_b,
                                       SH0[tt % 2] + 16 * (tt // 2))
                        sc.copy(out=stg_k[tt % 2][:], in_=ps_kv[tt % 2][:, :])
                        MKA(f"{pfx}evk{tt}")
                blk.scalar(_)

                if bias_qkv:
                    def _(v):
                        v.wait_ge(ca[layer], CT_K)
                        for tt in range(8):
                            WA(v, f"{pfx}evk{tt}")
                            v.tensor_tensor(out=stg_k[tt % 2][:],
                                            in0=stg_k[tt % 2][:],
                                            in1=bk_r[:], op=ALU.add)
                            MKV(f"{pfx}bk{tt}")
                    blk.vector(_)

                def _(gq):
                    gq.wait_ge(s_ld, 32)
                    for tt in range(8):
                        if bias_qkv:
                            WV(gq, f"{pfx}bk{tt}")
                        else:
                            WA(gq, f"{pfx}evk{tt}")
                        gq.indirect_dma_start(
                            out=shk,
                            out_offset=bass.IndirectOffsetOnAxis(
                                ap=sidx_s[:, tt:tt + 1], axis=0),
                            in_=stg_k[tt % 2][:],
                            in_offset=None).then_inc(
                                sh_a if tt % 2 == 0 else sh_b, 16)
                blk.gpsimd(_)
                tot["sh"][0] += 16 * 4
                tot["sh"][1] += 16 * 4
                SHK = [tot["sh"][0], tot["sh"][1]]

                def _(gq):
                    gq.wait_ge(t_kv, 8)
                    for kc in range(16):
                        gq.dma_start(out=wkv[kc][:],
                                     in_=wvd_[kc]).then_inc(cv[layer], 16)
                blk.gpsimd(_)
                CT_V = 256

                def _(v):
                    WV(v, "ones")
                    v.memset(stg_v[0][:], 1.0)
                    v.memset(stg_v[1][:], 1.0)
                    MKV(f"{pfx}vst1")
                blk.vector(_)

                def _(t):
                    t.wait_ge(cv[layer], CT_V)
                    for tt in range(8):
                        if tt >= 2:
                            if bias_qkv:
                                WV(t, f"{pfx}bv{tt - 2}")
                            else:
                                WA(t, f"{pfx}evv{tt - 2}")
                        ins = None
                        for kc in range(16):
                            ins = t.matmul(
                                out=ps_kv[tt % 2][:, :],
                                lhsT=xh[kc][:, 128 * tt:128 * (tt + 1)],
                                rhs=wkv[kc][:, :],
                                start=(kc == 0), stop=(kc == 15))
                        ins.then_inc(t_kv, 1)
                blk.tensor(_)

                def _(sc):
                    WV(sc, f"{pfx}vst1")
                    for tt in range(8):
                        sc.wait_ge(t_kv, 8 + tt + 1)
                        if tt >= 2:
                            sc.wait_ge(sh_a if tt % 2 == 0 else sh_b,
                                       SHK[tt % 2] + 16 * (tt // 2))
                        src = ps_kv[tt % 2][:, :].rearrange(
                            "p (g dd) -> p g dd", g=NKV)
                        if HD == 64:
                            dst = stg_v[tt % 2][:, 0:NKV * VG].rearrange(
                                "p (g w) -> p g w", g=NKV)[:, :, 0:64]
                            sc.copy(out=dst, in_=src)
                        else:
                            src2 = ps_kv[tt % 2][:, :].rearrange(
                                "p (g s2 dd) -> p g s2 dd", g=NKV, s2=2)
                            gv = stg_v[tt % 2][:, 0:NKV * VG].rearrange(
                                "p (g w) -> p g w", g=NKV)
                            sc.copy(out=gv[:, :, 0:64], in_=src2[:, :, 0, :])
                            sc.copy(out=gv[:, :, 65:129], in_=src2[:, :, 1, :])
                        MKA(f"{pfx}evv{tt}")
                blk.scalar(_)

                if bias_qkv:
                    def _(v):
                        for tt in range(8):
                            WA(v, f"{pfx}evv{tt}")
                            v.tensor_tensor(out=stg_v[tt % 2][:, 0:NKV * VG],
                                            in0=stg_v[tt % 2][:, 0:NKV * VG],
                                            in1=bv_r[:], op=ALU.add)
                            MKV(f"{pfx}bv{tt}")
                    blk.vector(_)

                def _(gq):
                    for tt in range(8):
                        if bias_qkv:
                            WV(gq, f"{pfx}bv{tt}")
                        else:
                            WA(gq, f"{pfx}evv{tt}")
                        gq.indirect_dma_start(
                            out=shv,
                            out_offset=bass.IndirectOffsetOnAxis(
                                ap=sidx_s[:, tt:tt + 1], axis=0),
                            in_=stg_v[tt % 2][:],
                            in_offset=None).then_inc(
                                sh_a if tt % 2 == 0 else sh_b, 16)
                    gq.wait_ge(sh_a, SHK[0] + 16 * 4)
                    gq.wait_ge(sh_b, SHK[1] + 16 * 4)
                    gq.wait_ge(bz, 16)
                    gq.collective_compute(
                        "AllReduce", ALU.add,
                        replica_groups=[[0, 1], [2, 3], [4, 5], [6, 7]],
                        ins=[bar], outs=[bar]).then_inc(cc_c, 1)
                blk.gpsimd(_)
                tot["sh"][0] += 16 * 4
                tot["sh"][1] += 16 * 4
                tot["cc_c"] += 1
                AR1 = tot["cc_c"]

                pw.close()
                kbig = e(nc.sbuf_tensor(f"{pfx}kb", [128, 8192], BF16))
                if rope:
                    ptk = ph.enter_context(ExitStack())
                    tabk = ptk.enter_context(
                        nc.sbuf_tensor(f"{pfx}tk", [128, 4096], BF16))

                def kb(r0, r1, kt, c0, c1):
                    return kbig[r0:r1, 2048 * kt + c0:2048 * kt + c1]

                if rope:
                    def _(gq):
                        gq.dma_start(out=tabk[:],
                                     in_=d["tabk0"]).then_inc(ca[layer], 16)
                    blk.gpsimd(_)
                CT_K2 = CT_K + (16 if rope else 0)

                RB0 = tot["rb"]

                def _(s):
                    s.wait_ge(cc_c, AR1)
                    for j in range(16):
                        s.dma_start(out=vsb[j][:, 0:NKV * VG],
                                    in_=shv[128 * j:128 * (j + 1),
                                            0:NKV * VG]).then_inc(rb_s, 16)
                    s.wait_ge(rb_s, RB0 + 256)
                    s.wait_ge(x_ld, 256)
                    s.wait_ge(s_ld, 32)
                    dst = kbig[:].rearrange("p (m f) -> p m f", m=4)
                    s.dma_start(out=dst, in_=shk,
                                transpose=True).then_inc(rb_s, 16)
                    s.wait_ge(rb_s, RB0 + 272)
                blk.sync(_)
                tot["rb"] += 272
                RB_T = tot["rb"]

                with ExitStack() as pq:
                    qe = pq.enter_context
                    ps_p = [qe(nc.psum_tensor(f"{pfx}pp{i}", [128, TOKQ], F32))
                            for i in range(2)]
                    WLQ = [tot["wl"][0], tot["wl"][1]]

                    def _(gq):
                        for i in range(16):
                            if i >= 2:
                                gq.wait_ge(t_q, i - 1)
                            gq.dma_start(out=wbuf[i % 2][:],
                                         in_=wqd_[i]).then_inc(wl[i % 2], 16)
                    blk.gpsimd(_)
                    tot["wl"][0] += 16 * 8
                    tot["wl"][1] += 16 * 8

                    def _(t):
                        for mt in range(16):
                            t.wait_ge(wl[mt % 2],
                                      WLQ[mt % 2] + 16 * (mt // 2 + 1))
                            if mt >= 2:
                                if rope:
                                    WV(t, f"{pfx}ropeq{mt - 2}")
                                else:
                                    WA(t, f"{pfx}evq{mt - 2}")
                            ins = None
                            for kc in range(16):
                                for sgm in range(2):
                                    ins = t.matmul(
                                        out=ps_p[mt % 2][:, 512 * sgm:512 * (sgm + 1)],
                                        lhsT=wbuf[mt % 2][:, 128 * kc:128 * (kc + 1)],
                                        rhs=xh[kc][:, 512 * sgm:512 * (sgm + 1)],
                                        start=(kc == 0), stop=(kc == 15))
                            ins.then_inc(t_q, 1)
                    blk.tensor(_)

                    def _(sc):
                        if bias_qkv:
                            sc.wait_ge(b_ld, 16)
                        for mt in range(16):
                            sc.wait_ge(t_q, mt + 1)
                            if rope:
                                if mt >= 2:
                                    WV(sc, f"{pfx}ropeq{mt - 2}")
                                p = ps_p[mt % 2]
                                r1 = rp[mt % 2]
                                sc.copy(out=qsb[mt][:], in_=p[:, :])
                                sc.copy(out=r1[0:32, 0:TOKQ], in_=p[32:64, :])
                                sc.copy(out=r1[32:64, 0:TOKQ], in_=p[0:32, :])
                                sc.copy(out=r1[64:96, 0:TOKQ], in_=p[96:128, :])
                                sc.copy(out=r1[96:128, 0:TOKQ], in_=p[64:96, :])
                            else:
                                sc.activation(out=qsb[mt][:],
                                              in_=ps_p[mt % 2][:, :],
                                              func=AF.Identity,
                                              bias=bq_s[:, mt:mt + 1])
                            MKA(f"{pfx}evq{mt}")
                    blk.scalar(_)

                    if rope:
                        def _(v):
                            v.wait_ge(ca[layer], CT_K)
                            for mt in range(16):
                                WA(v, f"{pfx}evq{mt}")
                                r1 = rp[mt % 2]
                                v.tensor_tensor(out=qsb[mt][:], in0=qsb[mt][:],
                                                in1=tabq[:, 0:1024],
                                                op=ALU.mult)
                                v.tensor_tensor(out=r1[:, 0:TOKQ],
                                                in0=r1[:, 0:TOKQ],
                                                in1=tabq[:, 1024:2048],
                                                op=ALU.mult)
                                v.tensor_tensor(out=qsb[mt][:], in0=qsb[mt][:],
                                                in1=r1[:, 0:TOKQ], op=ALU.add)
                                MKV(f"{pfx}ropeq{mt}")
                        blk.vector(_)

                if rope:
                    def _(sc):
                        sc.wait_ge(rb_s, RB_T)
                        for j in range(4):
                            if j >= 2:
                                WV(sc, f"{pfx}ropek{j - 2}")
                            else:
                                WV(sc, f"{pfx}ropeq{14 + j}")
                            r1 = rp[j % 2]
                            sc.copy(out=r1[0:32, :], in_=kb(32, 64, j, 0, 2048))
                            sc.copy(out=r1[32:64, :], in_=kb(0, 32, j, 0, 2048))
                            sc.copy(out=r1[64:96, :], in_=kb(96, 128, j, 0, 2048))
                            sc.copy(out=r1[96:128, :], in_=kb(64, 96, j, 0, 2048))
                            MKA(f"{pfx}swk{j}")
                    blk.scalar(_)

                    def _(v):
                        v.wait_ge(ca[layer], CT_K2)
                        for j in range(4):
                            WA(v, f"{pfx}swk{j}")
                            r1 = rp[j % 2]
                            kj = kb(0, 128, j, 0, 2048)
                            v.tensor_tensor(out=kj, in0=kj,
                                            in1=tabk[:, 0:2048], op=ALU.mult)
                            v.tensor_tensor(out=r1[:, :], in0=r1[:, :],
                                            in1=tabk[:, 2048:4096],
                                            op=ALU.mult)
                            v.tensor_tensor(out=kj, in0=kj, in1=r1[:, :],
                                            op=ALU.add)
                            MKV(f"{pfx}ropek{j}")
                    blk.vector(_)
                    ptk.close()

                with ExitStack() as pp:
                    ee = pp.enter_context
                    ps_n = ee(nc.psum_tensor(f"{pfx}psn", [NH, TOKQ], F32))
                    ps_nk = ee(nc.psum_tensor(f"{pfx}psk", [NKV, 2048], F32))
                    ps_tr = ee(nc.psum_tensor(f"{pfx}ptr", [1, 128], F32))
                    ps_bb = ee(nc.psum_tensor(f"{pfx}pbb", [128, 1], F32))
                    nb = ee(nc.sbuf_tensor(f"{pfx}nb", [NH, 2], F32))
                    nbk = ee(nc.sbuf_tensor(f"{pfx}nbk", [NKV, 2], F32))
                    nbs = ee(nc.sbuf_tensor(f"{pfx}nbs", [1, 8], F32))
                    nb_bf = ee(nc.sbuf_tensor(f"{pfx}nbbf", [1, 1], BF16))

                    def _(sc):
                        if rope:
                            WV(sc, f"{pfx}ropek3")
                        else:
                            sc.wait_ge(rb_s, RB_T)
                        for mt in range(16):
                            if mt >= 2:
                                sc.wait_ge(t_nb, mt - 1)
                            sc.activation(out=rp[mt % 2][:, 0:TOKQ],
                                          in_=qsb[mt][:], func=AF.Square)
                            MKA(f"{pfx}nsq{mt}")
                        for j in range(4):
                            sc.wait_ge(t_nb, 15 + j)
                            sc.activation(out=rp[j % 2][:, :],
                                          in_=kb(0, 128, j, 0, 2048),
                                          func=AF.Square)
                            MKA(f"{pfx}nsk{j}")
                    blk.scalar(_)

                    def _(t):
                        for mt in range(16):
                            WA(t, f"{pfx}nsq{mt}")
                            ins = None
                            for sgm in range(2):
                                ins = t.matmul(
                                    out=ps_n[:, 512 * sgm:512 * (sgm + 1)],
                                    lhsT=eq_s[:, NH * mt:NH * (mt + 1)],
                                    rhs=rp[mt % 2][:, 512 * sgm:512 * (sgm + 1)],
                                    start=(mt == 0), stop=(mt == 15))
                            ins.then_inc(t_nb, 1)
                        for j in range(4):
                            WA(t, f"{pfx}nsk{j}")
                            ins = None
                            for sgm in range(4):
                                ins = t.matmul(
                                    out=ps_nk[:, 512 * sgm:512 * (sgm + 1)],
                                    lhsT=ek_s[:, NKV * j:NKV * (j + 1)],
                                    rhs=rp[j % 2][:, 512 * sgm:512 * (sgm + 1)],
                                    start=(j == 0), stop=(j == 3))
                            ins.then_inc(t_nb, 1)
                    blk.tensor(_)

                    def _(v):
                        v.wait_ge(t_nb, 20)
                        v.tensor_reduce(out=nb[:, 0:1], in_=ps_n[:, :],
                                        axis=AX.X, op=ALU.max)
                        v.tensor_reduce(out=nbk[:, 0:1], in_=ps_nk[:, :],
                                        axis=AX.X, op=ALU.max)
                        MKV(f"{pfx}nred1")
                    blk.vector(_)

                    def _(t):
                        WV(t, f"{pfx}nred1")
                        t.wait_ge(s_ld, 32)
                        t.transpose(out=ps_tr[0:1, 0:NH], in_=nb[:, 0:1],
                                    identity=id_s[0:NH, 0:NH])
                        t.transpose(out=ps_tr[0:1, 64:64 + NKV], in_=nbk[:, 0:1],
                                    identity=id_s[0:NKV, 0:NKV]).then_inc(t_nb, 1)
                    blk.tensor(_)

                    def _(v):
                        v.wait_ge(t_nb, 21)
                        v.tensor_reduce(out=nbs[:, 0:1], in_=ps_tr[0:1, 0:NH],
                                        axis=AX.X, op=ALU.max)
                        v.tensor_reduce(out=nbs[:, 1:2],
                                        in_=ps_tr[0:1, 64:64 + NKV],
                                        axis=AX.X, op=ALU.max)
                        v.tensor_tensor(out=nbs[:, 2:3], in0=nbs[:, 0:1],
                                        in1=nbs[:, 1:2], op=ALU.mult)
                        MKV(f"{pfx}nred2")
                    blk.vector(_)

                    def _(sc):
                        WV(sc, f"{pfx}nred2")
                        sc.activation(out=nbs[:, 3:4], in_=nbs[:, 2:3],
                                      func=AF.Ln)
                        sc.activation(out=nbs[:, 4:5], in_=nbs[:, 3:4],
                                      func=AF.Exp, scale=0.5)
                        MKA(f"{pfx}nsqrt")
                    blk.scalar(_)

                    def _(v):
                        WA(v, f"{pfx}nsqrt")
                        v.tensor_scalar_mul(out=nbs[:, 5:6], in0=nbs[:, 4:5],
                                            scalar1=-scale)
                        v.tensor_copy(out=nb_bf[:, :], in_=nbs[:, 5:6])
                        MKV(f"{pfx}nbf")
                    blk.vector(_)

                    def _(t):
                        WV(t, f"{pfx}nbf")
                        t.matmul(out=ps_bb[:, 0:1], lhsT=ones_r[0:1, :],
                                 rhs=nb_bf[0:1, 0:1], start=True,
                                 stop=True).then_inc(t_nb, 1)
                    blk.tensor(_)

                    def _(sc):
                        sc.wait_ge(t_nb, 22)
                        sc.copy(out=bias_sb[:, :], in_=ps_bb[:, :])
                        MKA(f"{pfx}bias")
                    blk.scalar(_)

                of = qsb
                with ExitStack() as pp:
                    ee = pp.enter_context
                    pt = x2b
                    rc_b = ee(nc.sbuf_tensor(f"{pfx}rcb", [1, TOKQ], BF16))
                    ob = [ee(nc.sbuf_tensor(f"{pfx}oa{i}", [128, TOKQ], BF16))
                          for i in range(2)]
                    srow = ee(nc.sbuf_tensor(f"{pfx}sr", [1, TOKQ], BF16))
                    ns_s = 2 if layer == 0 else 1
                    ps_s = [ee(nc.psum_tensor(f"{pfx}pss{i}", [128, TOKQ], F32))
                            for i in range(ns_s)]
                    if layer == 0:
                        ps_o = ee(nc.psum_tensor(f"{pfx}pso", [HD + 1, TOKQ], F32))
                    else:
                        ps_o1 = ee(nc.psum_tensor(f"{pfx}po1", [65, TOKQ], F32))
                        ps_o2 = ee(nc.psum_tensor(f"{pfx}po2", [64, TOKQ], F32))
                    ps_b = ee(nc.psum_tensor(f"{pfx}psb", [128, TOKQ], F32))

                    def segs_for(s0):
                        a = 128 * s0
                        if a < 512:
                            return [(a, 512), (512, 1024)]
                        return [(a, 1024)]

                    def _(t):
                        WA(t, f"{pfx}bias")
                        for hs in range(NH):
                            h_orig = stor_heads[hs]
                            g = h_orig // (NH // NKV)
                            if HD == 64:
                                qt, qr = hs // 2, 64 * (hs % 2)
                            else:
                                qt, qr = hs, 0
                            kt, kr = (g * HD) // 128, (g * HD) % 128

                            def emit_S(c):
                                s0 = c // 2
                                idx = 16 * hs + c
                                if idx >= ns_s:
                                    WA(t, f"{pfx}exp{idx - ns_s}")
                                ins = None
                                for (a, b) in segs_for(s0):
                                    ins = t.matmul(
                                        out=ps_s[idx % ns_s][:, a:b],
                                        lhsT=kb(kr, kr + HD, kt,
                                                128 * c, 128 * (c + 1)),
                                        rhs=qsb[qt][qr:qr + HD, a:b],
                                        start=True, stop=True)
                                ins.then_inc(t_s, 1)

                            def emit_AV(c):
                                s0 = c // 2
                                idx = 16 * hs + c
                                WV(t, f"{pfx}msk{idx}")
                                if c == 0 and hs >= 1:
                                    WA(t, f"{pfx}oev{hs - 1}")
                                ins = None
                                for (a, b) in segs_for(s0):
                                    if layer == 0:
                                        ins = t.matmul(
                                            out=ps_o[:, a:b],
                                            lhsT=vsb[c][:, VG * g:VG * g + 65],
                                            rhs=pt[idx % 2][:, a:b],
                                            start=(c == 0), stop=False)
                                    else:
                                        t.matmul(
                                            out=ps_o1[:, a:b],
                                            lhsT=vsb[c][:, VG * g:VG * g + 65],
                                            rhs=pt[idx % 2][:, a:b],
                                            start=(c == 0), stop=False)
                                        ins = t.matmul(
                                            out=ps_o2[:, a:b],
                                            lhsT=vsb[c][:, VG * g + 65:VG * g + 129],
                                            rhs=pt[idx % 2][:, a:b],
                                            start=(c == 0), stop=False)
                                if c != 15:
                                    ins.then_inc(t_av, 1)

                            def emit_close():
                                ins = None
                                for sgm in range(2):
                                    cols = slice(512 * sgm, 512 * (sgm + 1))
                                    if layer == 0:
                                        ins = t.matmul(out=ps_o[:, cols],
                                                       lhsT=zrow[0:1, 0:65],
                                                       rhs=abc[0:1, 0:512],
                                                       start=False, stop=True)
                                    else:
                                        t.matmul(out=ps_o1[:, cols],
                                                 lhsT=zrow[0:1, 0:65],
                                                 rhs=abc[0:1, 0:512],
                                                 start=False, stop=True)
                                        ins = t.matmul(out=ps_o2[:, cols],
                                                       lhsT=zrow[0:1, 0:64],
                                                       rhs=abc[0:1, 0:512],
                                                       start=False, stop=True)
                                ins.then_inc(t_av, 1)

                            if ns_s == 2:
                                emit_S(0)
                                for c in range(1, 16):
                                    emit_S(c)
                                    emit_AV(c - 1)
                                emit_AV(15)
                            else:
                                for c in range(16):
                                    emit_S(c)
                                    emit_AV(c)
                            emit_close()
                            WV(t, f"{pfx}rc{hs}")
                            if hs >= 1:
                                WV(t, f"{pfx}epi{hs - 1}")
                            rr = 64 * (hs % 2) if layer == 0 else 0
                            ins = None
                            for sgm in range(2):
                                ins = t.matmul(
                                    out=ps_b[rr:rr + HD,
                                             512 * sgm:512 * (sgm + 1)],
                                    lhsT=ones_r[0:1, 0:HD],
                                    rhs=rc_b[0:1, 512 * sgm:512 * (sgm + 1)],
                                    start=True, stop=True)
                            ins.then_inc(t_bc, 1)
                    blk.tensor(_)

                    def _(sc):
                        for hs in range(NH):
                            for c in range(16):
                                s0 = c // 2
                                idx = 16 * hs + c
                                sc.wait_ge(t_s, idx + 1)
                                if idx >= 2:
                                    sc.wait_ge(t_av, idx - 1)
                                sc.activation(out=pt[idx % 2][:, 128 * s0:1024],
                                              in_=ps_s[idx % ns_s][:, 128 * s0:1024],
                                              func=AF.Exp, bias=bias_sb[:, 0:1],
                                              scale=scale)
                                MKA(f"{pfx}exp{idx}")
                            sc.wait_ge(t_av, 16 * (hs + 1))
                            if hs >= 2:
                                WV(sc, f"{pfx}epi{hs - 2}")
                            if hs >= 1:
                                WV(sc, f"{pfx}rc{hs - 1}")
                            rr = 64 * (hs % 2) if layer == 0 else 0
                            if layer == 0:
                                sc.copy(out=ob[hs % 2][rr:rr + 64, :],
                                        in_=ps_o[0:64, :])
                                sc.copy(out=srow[:, :], in_=ps_o[64:65, :])
                            else:
                                sc.copy(out=ob[hs % 2][0:64, :],
                                        in_=ps_o1[0:64, :])
                                sc.copy(out=ob[hs % 2][64:128, :],
                                        in_=ps_o2[0:64, :])
                                sc.copy(out=srow[:, :], in_=ps_o1[64:65, :])
                            MKA(f"{pfx}oev{hs}")
                    blk.scalar(_)

                    def _(v):
                        v.wait_ge(m_ld, 32)
                        for hs in range(NH):
                            for c in range(16):
                                s0 = c // 2
                                idx = 16 * hs + c
                                WA(v, f"{pfx}exp{idx}")
                                v.tensor_tensor(
                                    out=pt[idx % 2][:, 128 * s0:128 * (s0 + 1)],
                                    in0=pt[idx % 2][:, 128 * s0:128 * (s0 + 1)],
                                    in1=mkc[c][:, :], op=ALU.mult)
                                MKV(f"{pfx}msk{idx}")
                            WA(v, f"{pfx}oev{hs}")
                            with nc.allow_low_precision(reason="1/sum -> bf16"):
                                v.reciprocal(out=rc_b[:, :], in_=srow[:, :])
                            MKV(f"{pfx}rc{hs}")
                            v.wait_ge(t_bc, hs + 1)
                            if layer == 0:
                                qt, qr = hs // 2, 64 * (hs % 2)
                                v.tensor_tensor(out=of[qt][qr:qr + HD, :],
                                                in0=ob[hs % 2][qr:qr + 64, :],
                                                in1=ps_b[qr:qr + 64, :],
                                                op=ALU.mult)
                            else:
                                v.tensor_tensor(out=of[hs][0:64, :],
                                                in0=ob[hs % 2][0:64, :],
                                                in1=ps_b[0:64, :], op=ALU.mult)
                                v.tensor_tensor(out=of[hs][64:128, :],
                                                in0=ob[hs % 2][64:128, :],
                                                in1=ps_b[64:128, :],
                                                op=ALU.mult)
                            MKV(f"{pfx}epi{hs}")
                    blk.vector(_)

                if outproj:
                    with ExitStack() as pp:
                        ee = pp.enter_context
                        ps_5 = [ee(nc.psum_tensor(f"{pfx}p5{i}", [128, TOKQ], F32))
                                for i in range(2)]
                        WL5 = [tot["wl"][2], tot["wl"][3]]

                        def _(gq):
                            WA(gq, f"{pfx}bias")
                            for mt in range(16):
                                if mt >= 2:
                                    gq.wait_ge(t_p5, mt - 1)
                                gq.dma_start(out=wbuf[mt % 2][:],
                                             in_=wod_[mt]).then_inc(
                                                 wl[2 + mt % 2], 16)
                        blk.gpsimd(_)
                        tot["wl"][2] += 16 * 8
                        tot["wl"][3] += 16 * 8

                        def _(t):
                            WV(t, f"{pfx}epi{NH - 1}")
                            for mt in range(16):
                                t.wait_ge(wl[2 + mt % 2],
                                          WL5[mt % 2] + 16 * (mt // 2 + 1))
                                if mt >= 2:
                                    WV(t, f"{pfx}res{mt - 2}")
                                ins = None
                                for kc in range(16):
                                    for sgm in range(2):
                                        ins = t.matmul(
                                            out=ps_5[mt % 2][:, 512 * sgm:512 * (sgm + 1)],
                                            lhsT=wbuf[mt % 2][:, 128 * kc:128 * (kc + 1)],
                                            rhs=of[kc][:, 512 * sgm:512 * (sgm + 1)],
                                            start=(kc == 0), stop=(kc == 15))
                                ins.then_inc(t_p5, 1)
                        blk.tensor(_)

                        def _(v):
                            v.wait_ge(x_ld, 256)
                            for mt in range(16):
                                v.wait_ge(t_p5, mt + 1)
                                v.tensor_tensor(out=x_f[mt][:], in0=x_f[mt][:],
                                                in1=ps_5[mt % 2][:, :],
                                                op=ALU.add)
                                MKV(f"{pfx}res{mt}")
                        blk.vector(_)
                else:
                    def _(v):
                        WV(v, f"{pfx}epi{NH - 1}")
                        for mt in range(16):
                            v.tensor_tensor(out=x_f[mt][:], in0=x_f[mt][:],
                                            in1=of[mt][:], op=ALU.add)
                            MKV(f"{pfx}res{mt}")
                    blk.vector(_)

        def emit_ffn0(pfx):
            with ExitStack() as ph:
                e = ph.enter_context
                wb = [e(nc.sbuf_tensor(f"{pfx}wb{i}", [128, 2048], BF16))
                      for i in range(4)]
                hb = [e(nc.sbuf_tensor(f"{pfx}hb{i}", [128, TOKQ], BF16))
                      for i in range(8)]
                db = [e(nc.sbuf_tensor(f"{pfx}db{i}", [128, 2048], BF16))
                      for i in range(8)]
                sg_s = e(nc.sbuf_tensor(f"{pfx}sgs", [128, 64], F32))
                su_s = e(nc.sbuf_tensor(f"{pfx}sus", [128, 64], F32))
                sd_s = e(nc.sbuf_tensor(f"{pfx}sds", [128, 16], F32))
                gb = e(nc.sbuf_tensor(f"{pfx}gb", [128, TOKQ], BF16))
                ub = e(nc.sbuf_tensor(f"{pfx}ub", [128, TOKQ], BF16))
                ps_g = e(nc.psum_tensor(f"{pfx}psg", [128, TOKQ], F32))
                ps_u = e(nc.psum_tensor(f"{pfx}psu", [128, TOKQ], F32))
                ps_d = [e(nc.psum_tensor(f"{pfx}psd{i}", [128, TOKQ], F32))
                        for i in range(2)]

                def _(s):
                    WV(s, "a0.res15")
                    s.dma_start(out=sg_s[:], in_=d["sg"]).then_inc(f0c, 16)
                    s.dma_start(out=su_s[:], in_=d["su"]).then_inc(f0c, 16)
                    s.dma_start(out=sd_s[:], in_=d["sd"]).then_inc(f0c, 16)
                blk.sync(_)

                def gu_src(i):
                    return (d["wu"] if i % 2 else d["wg"])[i // 2]

                WLF = [tot["wl"][j] for j in range(4)]
                D0 = tot["d"]

                def _(gq):
                    WV(gq, "a0.res15")
                    for i in range(4):
                        gq.dma_start(out=wb[i][:], in_=gu_src(i)).then_inc(
                            wl[i], 16)
                    for c in range(8):
                        gq.dma_start(out=db[c][:], in_=d["wd"][c]).then_inc(
                            d_lda, 16)
                    for r in range(8):
                        for m in range(8):
                            mt = 8 * r + m
                            for i in (2 * mt, 2 * mt + 1):
                                if i + 4 < 128:
                                    gq.wait_ge(t_mm, max(0, i + 1))
                                    gq.dma_start(out=wb[(i + 4) % 4][:],
                                                 in_=gu_src(i + 4)).then_inc(
                                                     wl[(i + 4) % 4], 16)
                        if r + 1 < 8:
                            gq.wait_ge(t_dn, 16 * (r + 1))
                            for c in range(8):
                                gq.dma_start(out=db[c][:],
                                             in_=d["wd"][8 * (r + 1) + c]
                                             ).then_inc(d_lda, 16)
                blk.gpsimd(_)
                for i in range(4):
                    tot["wl"][i] += 16 * 32
                tot["d"] += 16 * 64

                def _(t):
                    WV(t, f"{pfx}xh_done")
                    for r in range(8):
                        for m in range(8):
                            mt = 8 * r + m
                            for i in (2 * mt, 2 * mt + 1):
                                WV(t, f"{pfx}wsub{i}")
                                is_u = i % 2
                                if mt >= 1:
                                    WV(t, f"{pfx}usc{mt - 1}" if is_u
                                       else f"{pfx}gsc{mt - 1}")
                                ps = ps_u if is_u else ps_g
                                ins = None
                                for kc in range(16):
                                    for sgm in range(2):
                                        ins = t.matmul(
                                            out=ps[:, 512 * sgm:512 * (sgm + 1)],
                                            lhsT=wb[i % 4][:, 128 * kc:128 * (kc + 1)],
                                            rhs=xh[kc][:, 512 * sgm:512 * (sgm + 1)],
                                            start=(kc == 0), stop=(kc == 15))
                                ins.then_inc(t_mm, 1)
                        WV(t, f"{pfx}dsub{r}")
                        WV(t, f"{pfx}h{8 * r + 7}")
                        for ht in range(16):
                            if 16 * r + ht >= 2:
                                WV(t, f"{pfx}dev{16 * r + ht - 2}")
                            ins = None
                            for c in range(8):
                                for sgm in range(2):
                                    ins = t.matmul(
                                        out=ps_d[ht % 2][:, 512 * sgm:512 * (sgm + 1)],
                                        lhsT=db[c][:, 128 * ht:128 * (ht + 1)],
                                        rhs=hb[c][:, 512 * sgm:512 * (sgm + 1)],
                                        start=(c == 0), stop=(c == 7))
                            ins.then_inc(t_dn, 1)
                blk.tensor(_)

                def _(sc):
                    for mt in range(64):
                        WV(sc, f"{pfx}gsc{mt}")
                        sc.activation(out=gb[:], in_=gb[:], func=AF.Silu)
                        MKA(f"{pfx}silu{mt}")
                blk.scalar(_)

                def _(v):
                    v.wait_ge(f0c, 48)

                    def wsub(i):
                        v.wait_ge(wl[i % 4], WLF[i % 4] + 16 * (i // 4 + 1))
                        v.tensor_scalar_add(out=wb[i % 4][:], in0=wb[i % 4][:],
                                            scalar1=-128.0)
                        MKV(f"{pfx}wsub{i}")

                    for i in range(4):
                        wsub(i)
                    for r in range(8):
                        for m in range(8):
                            mt = 8 * r + m
                            for i in (2 * mt + 4, 2 * mt + 5):
                                if i < 128:
                                    wsub(i)
                            if mt >= 1:
                                WA(v, f"{pfx}silu{mt - 1}")
                            v.wait_ge(t_mm, 2 * mt + 1)
                            v.tensor_scalar_mul(out=gb[:], in0=ps_g[:, :],
                                                scalar1=sg_s[:, mt:mt + 1])
                            MKV(f"{pfx}gsc{mt}")
                            v.wait_ge(t_mm, 2 * mt + 2)
                            v.tensor_scalar_mul(out=ub[:], in0=ps_u[:, :],
                                                scalar1=su_s[:, mt:mt + 1])
                            MKV(f"{pfx}usc{mt}")
                            WA(v, f"{pfx}silu{mt}")
                            v.tensor_tensor(out=hb[m][:], in0=gb[:], in1=ub[:],
                                            op=ALU.mult)
                            MKV(f"{pfx}h{mt}")
                        v.wait_ge(d_lda, D0 + 16 * 8 * (r + 1))
                        for c in range(8):
                            v.tensor_scalar_add(out=db[c][:], in0=db[c][:],
                                                scalar1=-128.0)
                        MKV(f"{pfx}dsub{r}")
                        for ht in range(16):
                            v.wait_ge(t_dn, 16 * r + ht + 1)
                            v.scalar_tensor_tensor(out=x_f[ht][:],
                                                   in0=ps_d[ht % 2][:, :],
                                                   scalar=sd_s[:, ht:ht + 1],
                                                   in1=x_f[ht][:],
                                                   op0=ALU.mult, op1=ALU.add)
                            MKV(f"{pfx}dev{16 * r + ht}")
                blk.vector(_)

        def emit_ffn1(pfx):
            with ExitStack() as ph:
                e = ph.enter_context
                wq = [e(nc.sbuf_tensor(f"{pfx}wq{i}", [128, 2048], FP8))
                      for i in range(16)]
                xq = [e(nc.sbuf_tensor(f"{pfx}xq{i}", [128, TOKQ], FP8))
                      for i in range(16)]
                wstg = [e(nc.sbuf_tensor(f"{pfx}ws{i}", [128, 2048], BF16))
                        for i in range(2)]
                bf_s = e(nc.sbuf_tensor(f"{pfx}bfs", [128, 16], BF16))
                bf_f = e(nc.sbuf_tensor(f"{pfx}bff", [128, 16], F32))
                red = e(nc.sbuf_tensor(f"{pfx}red", [128, 4], F32))
                sca = e(nc.sbuf_tensor(f"{pfx}sca", [1, 8], F32))
                sc_bf = e(nc.sbuf_tensor(f"{pfx}scbf", [1, 4], BF16))
                sc3 = e(nc.sbuf_tensor(f"{pfx}sc3", [128, 4], F32))
                ybf = e(nc.sbuf_tensor(f"{pfx}ybf", [128, TOKQ], BF16))
                ps_m = [e(nc.psum_tensor(f"{pfx}psm{i}", [128, TOKQ], F32))
                        for i in range(2)]
                ps_t = e(nc.psum_tensor(f"{pfx}pst", [1, 128], F32))
                ps_t2 = e(nc.psum_tensor(f"{pfx}pst2", [1, 128], F32))
                ps_c = e(nc.psum_tensor(f"{pfx}psc", [128, 4], F32))

                def _(s):
                    WV(s, "a1.res15")
                    s.dma_start(out=bf_s[:], in_=d["bfb"]).then_inc(f1c, 16)
                blk.sync(_)

                WLF = [tot["wl"][0], tot["wl"][1]]

                def _(gq):
                    WV(gq, "a1.res15")
                    for i in range(16):
                        if i >= 2:
                            WV(gq, f"{pfx}wmax{i - 2}")
                        gq.dma_start(out=wstg[i % 2][:], in_=d["wf"][i]
                                     ).then_inc(wl[i % 2], 16)
                blk.gpsimd(_)
                tot["wl"][0] += 16 * 8
                tot["wl"][1] += 16 * 8

                def _(v):
                    for i in range(16):
                        v.wait_ge(wl[i % 2], WLF[i % 2] + 16 * (i // 2 + 1))
                        v.tensor_reduce(out=red[:, 0:1], in_=wstg[i % 2][:],
                                        axis=AX.X, op=ALU.max,
                                        apply_absolute_value=True)
                        if i == 0:
                            v.tensor_copy(out=red[:, 1:2], in_=red[:, 0:1])
                        else:
                            v.tensor_tensor(out=red[:, 1:2], in0=red[:, 1:2],
                                            in1=red[:, 0:1], op=ALU.max)
                        MKV(f"{pfx}wmax{i}")
                    WV(v, f"{pfx}xh_done")
                    for kc in range(16):
                        v.tensor_reduce(out=red[:, 2:3], in_=xh[kc][:],
                                        axis=AX.X, op=ALU.max,
                                        apply_absolute_value=True)
                        if kc == 0:
                            v.tensor_copy(out=red[:, 3:4], in_=red[:, 2:3])
                        else:
                            v.tensor_tensor(out=red[:, 3:4], in0=red[:, 3:4],
                                            in1=red[:, 2:3], op=ALU.max)
                    MKV(f"{pfx}xmax")
                blk.vector(_)

                def _(t):
                    WV(t, f"{pfx}xmax")
                    t.wait_ge(s_ld, 32)
                    t.transpose(out=ps_t[0:1, 0:128], in_=red[:, 3:4],
                                identity=id_s[:, :]).then_inc(t_f1, 1)
                    t.transpose(out=ps_t2[0:1, 0:128], in_=red[:, 1:2],
                                identity=id_s[:, :]).then_inc(t_f1, 1)
                blk.tensor(_)

                def _(v):
                    v.wait_ge(t_f1, 2)
                    v.tensor_reduce(out=sca[:, 0:1], in_=ps_t[0:1, 0:128],
                                    axis=AX.X, op=ALU.max)
                    v.tensor_scalar_max(out=sca[:, 0:1], in0=sca[:, 0:1],
                                        scalar1=1e-6)
                    v.tensor_reduce(out=sca[:, 2:3], in_=ps_t2[0:1, 0:128],
                                    axis=AX.X, op=ALU.max)
                    v.tensor_scalar_max(out=sca[:, 2:3], in0=sca[:, 2:3],
                                        scalar1=1e-6)
                    MKV(f"{pfx}xsloc")
                blk.vector(_)

                CS0 = tot["cc_s"]
                CC0 = tot["cc_c"]

                def _(gq):
                    WV(gq, f"{pfx}xsloc")
                    gq.dma_start(out=cc_in, in_=sca[:, 0:1]).then_inc(cc_s, 16)
                    gq.wait_ge(cc_s, CS0 + 16)
                    gq.collective_compute(
                        "AllReduce", ALU.max,
                        replica_groups=[[0, 1, 2, 3, 4, 5, 6, 7]],
                        ins=[cc_in], outs=[cc_in]).then_inc(cc_c, 1)
                    gq.wait_ge(cc_c, CC0 + 1)
                    gq.dma_start(out=sca[:, 1:2], in_=cc_in).then_inc(cc_s, 16)
                    for i in range(16):
                        WV(gq, f"{pfx}wq{i - 2}" if i >= 2
                           else f"{pfx}wmax{14 + i}")
                        gq.dma_start(out=wstg[i % 2][:], in_=d["wf"][i]
                                     ).then_inc(wl[i % 2], 16)
                blk.gpsimd(_)
                tot["cc_s"] += 32
                tot["cc_c"] += 1
                CSX = tot["cc_s"]
                CCX = tot["cc_c"]
                WLB = [WLF[0] + 16 * 8, WLF[1] + 16 * 8]
                tot["wl"][0] += 16 * 8
                tot["wl"][1] += 16 * 8

                def _(v):
                    v.wait_ge(cc_s, CSX)
                    v.wait_ge(cc_c, CCX)
                    v.reciprocal(out=sca[:, 3:4], in_=sca[:, 1:2])
                    v.reciprocal(out=sca[:, 4:5], in_=sca[:, 2:3])
                    v.tensor_tensor(out=sca[:, 5:6], in0=sca[:, 1:2],
                                    in1=sca[:, 2:3], op=ALU.mult)
                    v.tensor_copy(out=sc_bf[:, 0:3], in_=sca[:, 3:6])
                    MKV(f"{pfx}scbf")
                blk.vector(_)

                def _(t):
                    WV(t, f"{pfx}scbf")
                    t.matmul(out=ps_c[:, 0:3], lhsT=ones_r[0:1, :],
                             rhs=sc_bf[0:1, 0:3], start=True,
                             stop=True).then_inc(t_f1, 1)
                blk.tensor(_)

                def _(sc):
                    sc.wait_ge(t_f1, 3)
                    sc.copy(out=sc3[:, 0:3], in_=ps_c[:, 0:3])
                    MKA(f"{pfx}sc3")
                    sc.wait_ge(f1c, 16)
                    sc.copy(out=bf_f[:, :], in_=bf_s[:, :])
                    MKA(f"{pfx}bff")
                blk.scalar(_)

                def _(v):
                    WA(v, f"{pfx}sc3")
                    for i in range(16):
                        v.wait_ge(wl[i % 2], WLB[i % 2] + 16 * (i // 2 + 1))
                        v.tensor_scalar_mul(out=wq[i][:], in0=wstg[i % 2][:],
                                            scalar1=sc3[:, 1:2])
                        MKV(f"{pfx}wq{i}")
                    for kc in range(16):
                        v.tensor_scalar_mul(out=xq[kc][:], in0=xh[kc][:],
                                            scalar1=sc3[:, 0:1])
                        MKV(f"{pfx}xq{kc}")
                blk.vector(_)

                def _(t):
                    WV(t, f"{pfx}xq15")
                    for mt in range(16):
                        WV(t, f"{pfx}wq{mt}")
                        if mt >= 2:
                            WV(t, f"{pfx}yev{mt - 2}")
                        ins = None
                        for kc in range(16):
                            for sgm in range(2):
                                ins = t.matmul(
                                    out=ps_m[mt % 2][:, 512 * sgm:512 * (sgm + 1)],
                                    lhsT=wq[mt][:, 128 * kc:128 * (kc + 1)],
                                    rhs=xq[kc][:, 512 * sgm:512 * (sgm + 1)],
                                    start=(kc == 0), stop=(kc == 15))
                        ins.then_inc(t_f1, 1)
                blk.tensor(_)

                def _(v):
                    WA(v, f"{pfx}bff")
                    for mt in range(16):
                        v.wait_ge(t_f1, 3 + mt + 1)
                        v.tensor_scalar_mul(out=ybf[:], in0=ps_m[mt % 2][:, :],
                                            scalar1=sc3[:, 2:3])
                        v.tensor_scalar_add(out=ybf[:], in0=ybf[:],
                                            scalar1=bf_f[:, mt:mt + 1])
                        v.tensor_tensor(out=x_f[mt][:], in0=x_f[mt][:],
                                        in1=ybf[:], op=ALU.add)
                        MKV(f"{pfx}yev{mt}")
                blk.vector(_)

        emit_ln("a0.", lambda sc: sc.wait_ge(xb_ld, 256))
        emit_attn(0, "a0.", lambda gq: None)

        def _(gq):
            for kc in range(16):
                WV(gq, f"a0.res{kc}")
                gq.tensor_copy(out=xh[kc][:], in_=x_f[kc][:])
                MKG(f"f0.xcp{kc}")
        blk.gpsimd(_)

        emit_ln("f0.", lambda sc: WG(sc, "f0.xcp15"))
        emit_ffn0("f0.")

        def _(gq):
            for kc in range(16):
                WV(gq, f"f0.dev{16 * 7 + kc}")
                gq.tensor_copy(out=xh[kc][:], in_=x_f[kc][:])
                MKG(f"a1.xcp{kc}")
        blk.gpsimd(_)

        emit_ln("a1.", lambda sc: WG(sc, "a1.xcp15"))
        emit_attn(1, "a1.", lambda gq: WG(gq, "a1.xcp15"))

        def _(gq):
            for kc in range(16):
                WV(gq, f"a1.res{kc}")
                gq.tensor_copy(out=xh[kc][:], in_=x_f[kc][:])
                MKG(f"f1.xcp{kc}")
        blk.gpsimd(_)

        emit_ln("f1.", lambda sc: WG(sc, "f1.xcp15"))
        emit_ffn1("f1.")

        def _(s):
            for mt in range(16):
                WV(s, f"f1.yev{mt}")
                s.dma_start(out=ydr[128 * mt:128 * (mt + 1), :],
                            in_=x_f[mt][:]).then_inc(
                                st_a if mt % 2 == 0 else st_b, 16)
            s.wait_ge(st_a, 16 * 8)
            s.wait_ge(st_b, 16 * 8)
        blk.sync(_)

    return nc



def prep_fused_inputs(inputs):
    x_full = np.ascontiguousarray(
        np.asarray(inputs["L0_attn_hidden_states"], np.float32))
    ro = l0_row_order()
    base = {}
    base["wq0"] = tiled_lhsT(np.ascontiguousarray(
        np.asarray(inputs["L0_attn_Wq"], np.float32)[ro].T))
    base["wk0"] = np.ascontiguousarray(
        np.asarray(inputs["L0_attn_Wk"], np.float32).T).reshape(16, 128, 512)
    base["wv0"] = np.ascontiguousarray(
        np.asarray(inputs["L0_attn_Wv"], np.float32).T).reshape(16, 128, 512)
    base["wo0"] = tiled_lhsT(np.ascontiguousarray(
        np.asarray(inputs["L0_attn_Wo"], np.float32)[:, ro].T))
    wq2 = np.asarray(inputs["L1_attn_Wq"],
                     np.float32).transpose(1, 0, 2).reshape(2048, 2048)
    wk2 = np.asarray(inputs["L1_attn_Wk"],
                     np.float32).transpose(1, 0, 2).reshape(2048, 512)
    wv2 = np.asarray(inputs["L1_attn_Wv"],
                     np.float32).transpose(1, 0, 2).reshape(2048, 512)
    base["wq1"] = tiled_lhsT(np.ascontiguousarray(wq2))
    base["wk1"] = np.ascontiguousarray(wk2).reshape(16, 128, 512)
    base["wv1"] = np.ascontiguousarray(wv2).reshape(16, 128, 512)
    bq = np.asarray(inputs["L1_attn_bq"], np.float32).reshape(2048)
    bk = np.asarray(inputs["L1_attn_bk"], np.float32).reshape(512)
    bv = np.asarray(inputs["L1_attn_bv"], np.float32).reshape(512)
    base["bq1"] = np.ascontiguousarray(bq.reshape(16, 128).T)
    base["bk1"] = np.ascontiguousarray(
        np.broadcast_to(bk.reshape(1, 512), (128, 512)))
    bvg = np.zeros((4, 129), np.float32)
    bvr = bv.reshape(4, 128)
    bvg[:, 0:64] = bvr[:, 0:64]
    bvg[:, 65:129] = bvr[:, 64:128]
    base["bv1"] = np.ascontiguousarray(
        np.broadcast_to(bvg.reshape(1, 516), (128, 516)))
    base["eq0"] = head_indicator_tiles(2048, 32)
    base["ek0"] = head_indicator_tiles(512, 8)
    base["eq1"] = head_indicator_tiles(2048, 16)
    base["ek1"] = head_indicator_tiles(512, 4)
    base["wg"] = tiled_lhsT(np.ascontiguousarray(
        u8_view(np.asarray(inputs["L0_ffn_gate_w"])).T))
    base["wu"] = tiled_lhsT(np.ascontiguousarray(
        u8_view(np.asarray(inputs["L0_ffn_up_w"])).T))
    base["wd"] = np.ascontiguousarray(
        u8_view(np.asarray(inputs["L0_ffn_down_w"])).T).reshape(64, 128, 2048)
    base["sg"] = np.ascontiguousarray(
        np.asarray(inputs["L0_ffn_gate_scale"], np.float32).reshape(64, 128).T)
    base["su"] = np.ascontiguousarray(
        np.asarray(inputs["L0_ffn_up_scale"], np.float32).reshape(64, 128).T)
    base["sd"] = np.ascontiguousarray(
        np.asarray(inputs["L0_ffn_down_scale"], np.float32).reshape(16, 128).T)
    wf = np.asarray(inputs["L1_ffn_weight"])
    base["wf"] = tiled_lhsT(np.ascontiguousarray(wf.T))
    base["bfb"] = np.ascontiguousarray(
        np.asarray(inputs["L1_ffn_bias"]).reshape(16, 128).T)
    base["ident"] = np.eye(128, dtype=np.float32)
    kk, qq = np.meshgrid(np.arange(128), np.arange(128), indexing="ij")
    tril = (qq >= kk).astype(np.float32)
    ckg, skg = rope_tables(np.arange(2048))
    tabk_g = np.ascontiguousarray(np.concatenate([ckg, skg], axis=1))

    in_maps = []
    for cid in range(NCORES):
        b, p = cid // 2, cid % 2
        perm = perm_for_core(p)
        own = perm[:1024]
        m = dict(base)
        m["x"] = np.ascontiguousarray(x_full[b].T[:, own])
        cq, sq_ = rope_tables(own)
        m["tabq0"] = np.ascontiguousarray(np.concatenate([cq, sq_], axis=1))
        m["tabk0"] = tabk_g
        if p == 0:
            m["maske"] = tril
            m["masko"] = np.zeros((128, 128), np.float32)
        else:
            m["maske"] = np.ones((128, 128), np.float32)
            m["masko"] = tril
        sidx = np.empty((128, 8), np.uint32)
        for tt in range(8):
            sidx[:, tt] = 128 * (p + 2 * tt) + np.arange(128)
        m["sidx"] = np.ascontiguousarray(sidx)
        in_maps.append(m)
    return in_maps, x_full


def gather_fused_output(results, x_full):
    out = np.empty_like(x_full)
    for cid in range(NCORES):
        b, p = cid // 2, cid % 2
        perm = perm_for_core(p)
        out[b, perm[:1024], :] = results[cid]["y"].T
    return out


def kernel(**inputs):
    key = "fused"
    if key not in _NC_CACHE:
        _NC_CACHE[key] = build_fused()
    in_maps, x_full = prep_fused_inputs(inputs)
    res = run_bass_kernel_spmd(_NC_CACHE[key], in_maps,
                               core_ids=list(range(NCORES)))
    return gather_fused_output(res.results, x_full)
